# version 13
# speedup vs baseline: 1.3973x; 1.0224x over previous
import contextlib
import numpy as np
import ml_dtypes
import concourse.bass as bass
import concourse.mybir as mybir
from concourse.bass_utils import run_bass_kernel_spmd

F32 = mybir.dt.float32
BF16 = mybir.dt.bfloat16
U32 = mybir.dt.uint32
AF = mybir.ActivationFunctionType
ALU = mybir.AluOpType
AX = mybir.AxisListType

NCORES = 8
B_CORE = 2
T = 2048
D = 1024
PT = D // 128
NT = T // 128
NEG = -1.0e30
BF = ml_dtypes.bfloat16

ENGS = ("pe", "dve", "act", "pool", "sp")
DMAQS = ("sp", "act", "pool")
NSLOT = 16
_ATTR = {"pe": "tensor", "dve": "vector", "act": "scalar", "pool": "gpsimd", "sp": "sync"}


def ts(i, n=128):
    return slice(i * n, (i + 1) * n)


def sl(base, i, n=128):
    return slice(base + i * n, base + (i + 1) * n)


class Em:

    def __init__(self, nc, ctx):
        self.nc = nc
        self.sem = {e: ctx.enter_context(nc.semaphore(f"S{e}")) for e in ENGS}
        self.dsem = {}
        self.dcnt = {}
        self.dslot = {q: 0 for q in DMAQS}
        for q in DMAQS:
            for sl_ in range(NSLOT):
                self.dsem[f"d_{q}_{sl_}"] = ctx.enter_context(
                    nc.semaphore(f"Dm{q}{sl_}"))
                self.dcnt[f"d_{q}_{sl_}"] = 0
        self.cnt = {e: 0 for e in ENGS}
        self.q = {e: [] for e in ENGS}
        self.lastw = {}
        self.readers = {}
        self.hiw = {e: {} for e in ENGS}
        self.nops = 0

    def _sem_of(self, key):
        return self.sem[key] if key in self.sem else self.dsem[key]

    def _waits_for(self, eng, reads, writes):
        toks = {}

        def add(t):
            if t is None:
                return
            k, v = t
            if k == "pe" and eng == "pe":
                return
            if toks.get(k, -1) < v:
                toks[k] = v

        for r in reads:
            add(self.lastw.get(r))
        for w in writes:
            add(self.lastw.get(w))
            for t in self.readers.get(w, ()):
                add(t)
        out = []
        hw = self.hiw[eng]
        for k, v in toks.items():
            if hw.get(k, -1) >= v:
                continue
            hw[k] = v
            out.append((k, v))
        return out

    def _emit_waits(self, eng, waits):
        for k, v in waits:
            sem = self._sem_of(k)
            self.q[eng].append(lambda E, s=sem, vv=v: E.wait_ge(s, vv))

    def op(self, eng, fn, reads=(), writes=()):
        self._emit_waits(eng, self._waits_for(eng, reads, writes))
        self.cnt[eng] += 1
        tok = (eng, self.cnt[eng])
        sem = self.sem[eng]

        def doit(E, fn=fn, sem=sem):
            fn(E).then_inc(sem, 1)

        self.q[eng].append(doit)
        self.nops += 1
        for r in reads:
            self.readers.setdefault(r, []).append(tok)
        for w in writes:
            self.lastw[w] = tok
            self.readers[w] = []
        return tok

    def dma(self, qeng, fn, reads=(), writes=(), n=1):
        slot = self.dslot[qeng]
        self.dslot[qeng] = (slot + 1) % NSLOT
        key = f"d_{qeng}_{slot}"
        waits = self._waits_for(qeng, reads, writes)
        hw = self.hiw[qeng]
        if self.dcnt[key] > 0 and hw.get(key, -1) < self.dcnt[key]:
            waits = waits + [(key, self.dcnt[key])]
            hw[key] = self.dcnt[key]
        self._emit_waits(qeng, waits)
        self.dcnt[key] += 16 * n
        tok = (key, self.dcnt[key])
        dsem = self.dsem[key]

        def doit(E, fn=fn, dsem=dsem):
            insts = fn(E)
            if not isinstance(insts, (list, tuple)):
                insts = [insts]
            for inst in insts:
                inst.then_inc(dsem, 16)

        self.q[qeng].append(doit)
        self.nops += n
        for r in reads:
            self.readers.setdefault(r, []).append(tok)
        for w in writes:
            self.lastw[w] = tok
            self.readers[w] = []
        return tok

    def barrier(self):
        targets = {e: self.cnt[e] for e in ENGS}
        dtargets = dict(self.dcnt)
        for e in ENGS:
            hw = self.hiw[e]
            for o in ENGS:
                if o == e or targets[o] == 0 or hw.get(o, -1) >= targets[o]:
                    continue
                sem = self.sem[o]
                self.q[e].append(lambda E, s=sem, v=targets[o]: E.wait_ge(s, v))
                hw[o] = targets[o]
            for k2, v2 in dtargets.items():
                if v2 > 0 and hw.get(k2, -1) < v2:
                    sem = self.dsem[k2]
                    self.q[e].append(lambda E, s=sem, vv=v2: E.wait_ge(s, vv))
                    hw[k2] = v2
        self.lastw = {}
        self.readers = {}

    def finish(self):
        for qq in DMAQS:
            for sl_ in range(NSLOT):
                key = f"d_{qq}_{sl_}"
                v = self.dcnt[key]
                if v > 0:
                    sem = self.dsem[key]
                    self.q[qq].append(lambda E, s=sem, vv=v: E.wait_ge(s, vv))

    def finalize(self):
        with self.nc.Block() as block:
            for e in ENGS:
                deco = getattr(block, _ATTR[e])
                ops = self.q[e]

                def body(E, ops=ops):
                    for f in ops:
                        f(E)

                deco(body)


class Prog:
    def __init__(self, name):
        self.nc = bass.Bass("TRN2", target_bir_lowering=False, debug=False)
        self.ctx = contextlib.ExitStack()
        self._root_ctx = self.ctx
        self.s = Em(self.nc, self.ctx)
        self.name = name
        self.tag = ""
        self.inputs = {}
        self.pp_bytes = 0

    @contextlib.contextmanager
    def scope(self):
        outer = self.ctx
        self.ctx = contextlib.ExitStack()
        try:
            yield
        finally:
            self.ctx.close()
            self.ctx = outer

    def constv(self, val):
        if not hasattr(self, "_cvals"):
            self._cvals = {}
            old_ctx, self.ctx = self.ctx, getattr(self, "_root_ctx", self.ctx)
            self._cpool = self.sb("cpool", [128, 16], F32)
            self.ctx = old_ctx
        if val not in self._cvals:
            i = len(self._cvals)
            assert i < 16
            ap = self._cpool[:, i:i + 1]
            self.s.op("pool", lambda E, ap=ap, val=val: E.memset(ap, float(val)),
                      writes=["cpool"])
            self._cvals[val] = ap
        return self._cvals[val]

    def dram_in(self, name, arr=None, shape=None, dtype=F32):
        name = self.tag + name
        if arr is not None:
            npdt = {F32: np.float32, BF16: BF, U32: np.uint32}[dtype]
            arr = np.ascontiguousarray(np.asarray(arr).astype(npdt))
            shape = list(arr.shape)
            self.inputs[name] = arr
        h = self.nc.dram_tensor(name, list(shape), dtype, kind="ExternalInput")
        return h.ap()

    def dram_out(self, name, shape, dtype=F32):
        return self.nc.dram_tensor(self.tag + name, list(shape), dtype,
                                   kind="ExternalOutput").ap()

    def sb(self, name, shape, dtype):
        name = self.tag + name
        esz = {F32: 4, BF16: 2, U32: 4}[dtype]
        n = esz
        for d_ in shape[1:]:
            n *= d_
        self.pp_bytes += n
        return self.ctx.enter_context(self.nc.sbuf_tensor(name, list(shape), dtype))

    def psum(self, name, shape, dtype=F32):
        return self.ctx.enter_context(
            self.nc.psum_tensor(self.tag + name, list(shape), dtype))

    def build(self):
        if getattr(self, "_built", False):
            return
        self._built = True
        self.s.finish()
        self.s.finalize()

    def run(self, per_core_inputs, out_names):
        self.build()
        in_maps = []
        for ci in range(len(per_core_inputs)):
            m = dict(self.inputs)
            for k, v in per_core_inputs[ci].items():
                m[k] = np.ascontiguousarray(v)
            in_maps.append(m)
        res = run_bass_kernel_spmd(self.nc, in_maps, core_ids=list(range(len(in_maps))))
        return [{k: np.asarray(r[k]) for k in out_names} for r in res.results]



def consts_common():
    c = {}
    c["ident_bf"] = np.eye(128, dtype=BF)
    c["ident_f32"] = np.eye(128, dtype=np.float32)
    c["ones_col_bf"] = np.ones((128, 1), dtype=BF)
    m = np.zeros((128, 4 * 512), dtype=np.float32)
    for a in range(4):
        qq = np.arange(128)[:, None] + a * 128
        kk = np.arange(512)[None, :]
        m[:, a * 512:(a + 1) * 512] = np.where(kk <= qq, 0.0, NEG)
    c["cmask4"] = m
    c["cmask4_bf"] = m.astype(BF)
    return c


def load_consts(p, names):
    if not hasattr(p, "_cst"):
        p._cst = {}
    cn = consts_common()
    old_ctx, old_tag = p.ctx, p.tag
    p.ctx, p.tag = p._root_ctx, ""
    try:
        for k in names:
            if k in p._cst:
                continue
            v = cn[k]
            dt = F32 if v.dtype == np.float32 else BF16
            ap = p.dram_in("c_" + k, v, dtype=dt)
            t = p.sb("cs_" + k, list(v.shape), dt)
            p.s.dma("sp", lambda E, t=t, ap=ap: E.dma_start(t[:, :], ap[:, :]),
                    writes=[f"c:{k}"])
            p._cst[k] = t
    finally:
        p.ctx, p.tag = old_ctx, old_tag
    return p._cst



def alloc_lnt(p):
    d = {}
    d["xt"] = p.sb("lnt_xt", [128, D], F32)
    d["st"] = p.sb("lnt_st", [128, 8], F32)
    d["xn_tm"] = p.sb("lnt_xn", [128, D], BF16)
    d["sq"] = d["xn_tm"]
    return d


def emit_lnt_tile(p, lnt, x_src_ap, psT, psTk, xn_fm_writer, extra_tm=None):
    xkey = "lnt_xt"
    """LayerNorm one token-major tile then transpose to feature-major.
    x_src_ap: HBM AP [128, D] for this tile. xn_fm_writer(j) -> (dst_ap, key)
    for the j-th transposed [128,128] block. extra_tm: optional (dst_ap, key)
    to also keep the token-major normalized tile (bf16)."""
    s = p.s
    d = lnt
    s.dma("sp", lambda E: E.dma_start(d["xt"][:, :], x_src_ap), writes=[xkey])
    s.op("act", lambda E: E.activation(d["sq"][:, :], d["xt"][:, :], AF.Square,
                                       accum_out=d["st"][:, 1:2]),
         reads=[xkey], writes=["lnt_xn", "lnt_ssq"])
    s.op("dve", lambda E: E.tensor_reduce(d["st"][:, 0:1], d["xt"][:, :], AX.X, ALU.add),
         reads=[xkey], writes=["lnt_sum"])
    s.op("dve", lambda E: E.tensor_scalar(d["st"][:, 2:4], d["st"][:, 0:2], 1.0 / D, None,
                                          ALU.mult),
         reads=["lnt_sum", "lnt_ssq"], writes=["lnt_mean_msq"])
    s.op("act", lambda E: E.activation(d["st"][:, 4:5], d["st"][:, 2:3], AF.Square),
         reads=["lnt_mean_msq"], writes=["lnt_m2"])
    s.op("dve", lambda E: E.tensor_tensor(d["st"][:, 5:6], d["st"][:, 3:4], d["st"][:, 4:5],
                                          ALU.subtract),
         reads=["lnt_mean_msq", "lnt_m2"], writes=["lnt_var"])
    eps = p.constv(1e-5)
    s.op("act", lambda E: E.activation(d["st"][:, 6:7], d["st"][:, 5:6], AF.Sqrt, bias=eps),
         reads=["lnt_var", "cpool"], writes=["lnt_std"])
    s.op("dve", lambda E: E.reciprocal(d["st"][:, 7:8], d["st"][:, 6:7]),
         reads=["lnt_std"], writes=["lnt_rstd"])
    s.op("dve", lambda E: E.tensor_scalar(d["xn_tm"][:, :], d["xt"][:, :],
                                          d["st"][:, 2:3], d["st"][:, 7:8],
                                          ALU.subtract, ALU.mult),
         reads=[xkey, "lnt_mean_msq", "lnt_rstd"], writes=["lnt_xn"])
    ident = p._cst["ident_bf"]

    def mmT(E):
        last = None
        for j in range(PT):
            last = E.transpose(psT[:, ts(j)], d["xn_tm"][:, ts(j)], ident[:, :])
        return last

    s.op("pe", mmT, reads=["lnt_xn", "c:ident_bf"], writes=[psTk])
    for j in range(PT):
        dst, key = xn_fm_writer(j)
        s.op("act", lambda E, dst=dst, j=j: E.activation(dst, psT[:, ts(j)], AF.Copy),
             reads=[psTk], writes=[key])
    if extra_tm is not None:
        dst, key = extra_tm
        s.op("dve", lambda E, dst=dst: E.tensor_copy(dst, d["xn_tm"][:, :]),
             reads=["lnt_xn"], writes=[key])


def emit_residual_tile(p, yfm_fn, x_src_ap, x_dst_ap, psT32, psT32k, rbufs,
                       out_key=None):
    s = p.s
    xb, yb = rbufs
    s.dma("sp", lambda E: E.dma_start(xb[:, :], x_src_ap), writes=["res_rx"])
    ident = p._cst["ident_f32"]
    srcs = [yfm_fn(j) for j in range(PT)]
    for w in range(2):
        def mmTw(E, w=w):
            last = None
            for j in range(4):
                last = E.transpose(psT32[:, ts(j)], srcs[w * 4 + j][0], ident[:, :])
            return last

        s.op("pe", mmTw, reads=[srcs[w * 4 + j][1] for j in range(4)] + ["c:ident_f32"],
             writes=[psT32k])
        for j in range(4):
            jj = w * 4 + j
            s.op("dve", lambda E, j=j, jj=jj: E.tensor_copy(yb[:, ts(jj)], psT32[:, ts(j)]),
                 reads=[psT32k], writes=[f"res_ry:{jj}"])
    s.op("dve", lambda E: E.tensor_tensor(xb[:, :], xb[:, :], yb[:, :], ALU.add),
         reads=["res_rx"] + [f"res_ry:{jj}" for jj in range(PT)],
         writes=["res_rx"])
    s.dma("sp", lambda E: E.dma_start(x_dst_ap, xb[:, :]), reads=["res_rx"],
          writes=([out_key] if out_key else []))


def load_weight_kt(p, name, hbm_ap, rows, cols, col0, ncols, sb, sbk, qeng="pool"):
    kt_n = rows // 128
    src = hbm_ap[:, col0:col0 + ncols].rearrange("(kt p) c -> p kt c", p=128)
    dst = bass.AP(sb, 0, [[sb.shape[1], 128], [ncols, kt_n], [1, ncols]])
    return p.s.dma(qeng, lambda E: E.dma_start(dst, src), writes=[sbk])


def emit_group_residual(p, b, g, yall, xin, xout, psT32, psT32k, rbufs,
                        ykey="yall", out_key=None):
    for j4 in range(4):
        tt = g * 4 + j4

        def yfm_fn(jj, j4=j4):
            return (yall[:, sl(jj * 512, j4)], f"{ykey}:{jj}")

        emit_residual_tile(p, yfm_fn, xin[b, ts(tt), :], xout[b, ts(tt), :],
                           psT32, psT32k, rbufs, out_key=out_key)



def build_attn_causal(layer, p=None, io=None, W=None):
    fused = p is not None
    if not fused:
        p = Prog("attnX")
    _sc = p.scope() if fused else contextlib.nullcontext()
    _sc.__enter__()
    s = p.s
    p._cst = load_consts(p, ["ident_bf", "ident_f32", "cmask4_bf"])

    xin = io[0] if fused else p.dram_in("xin", shape=[B_CORE, T, D], dtype=F32)
    xout = io[1] if fused else p.dram_out("xout", [B_CORE, T, D], F32)
    def gw(name, shape):
        return W[name] if W else p.dram_in(name, shape=shape, dtype=F32)
    if layer == 0:
        wqkv = gw("w_qkv", [D, 3 * D])
        wproj = gw("w_proj", [D, D])
        NH, HD, SCALE = 16, 64, 1.0 / 8.0
        wsrc = {0: (wqkv, 0), 1: (wqkv, D), 2: (wqkv, 2 * D), 3: (wproj, 0)}
    else:
        NH, HD, SCALE = 1, 1024, 1.0 / 32.0
        wsrc = {0: (gw("Wq", [D, D]), 0), 1: (gw("Wk", [D, D]), 0),
                2: (gw("Wv", [D, D]), 0), 3: (gw("Wo", [D, D]), 0)}

    lnt = alloc_lnt(p)
    xnf = p.sb("xnf", [128, PT * T], BF16)
    Kf = p.sb("Kf", [128, PT * T], BF16)
    Vt = p.sb("Vt", [128, NT * D], BF16)
    Qf = p.sb("Qf", [128, PT * 512], BF16)
    Ot = p.sb("Ot", [128, 4 * D], BF16)
    Of = p.sb("Of", [128, PT * 512], BF16)
    Ssb = p.sb("Ssb", [128, T], F32)
    Pbf = p.sb("Pbf", [128, T], BF16)
    PTsb = p.sb("PTsb", [128, 512], BF16)
    wb = [p.sb("wbA", [128, PT * D], BF16),
          p.sb("wbB", [128, PT * D], BF16)]
    yall = p.sb("yall", [128, PT * 512], F32)
    rx = p.sb("rx", [128, D], F32)
    ry = p.sb("ry", [128, D], F32)
    mx = p.sb("mx", [128, 8], F32)
    den = p.sb("den", [128, 8], F32)
    rcp = p.sb("rcp", [128, 1], F32)
    nmx = p.sb("nmx", [128, 1], F32)

    psS = [p.psum("psS0", [128, 512], F32), p.psum("psS1", [128, 512], F32)]
    psO = p.psum("psO", [128, 1024], F32)
    psT = p.psum("psT", [128, 1024], BF16)
    psP = [p.psum("psP0", [128, 512], F32), p.psum("psP1", [128, 512], F32)]
    psR = p.psum("psR", [128, 512], F32)

    def wload(which, slot):
        hbm, col0 = wsrc[which]
        return load_weight_kt(p, f"w{which}", hbm, D, D, col0, D, wb[slot], f"wb{slot}")

    for b in range(B_CORE):
        for tt in range(NT):
            def writer(j, tt=tt):
                return (xnf[:, sl(j * T, tt)], f"xnf:{j}")
            emit_lnt_tile(p, lnt, xin[b, ts(tt), :], psT, "psT", writer)
        wload(1, 0)
        for c in range(T // 512):
            for mt in range(PT):
                def mmK(E, c=c, mt=mt):
                    last = None
                    for kt in range(PT):
                        last = E.matmul(psP[mt % 2][:, 0:512],
                                        wb[0][:, sl(kt * D, mt)],
                                        xnf[:, kt * T + c * 512:kt * T + (c + 1) * 512],
                                        start=(kt == 0), stop=(kt == PT - 1))
                    return last
                s.op("pe", mmK, reads=["wb0"] + [f"xnf:{k}" for k in range(PT)],
                     writes=[f"psP{mt % 2}"])
                s.op("act", lambda E, c=c, mt=mt: E.activation(
                    Kf[:, mt * T + c * 512:mt * T + (c + 1) * 512],
                    psP[mt % 2][:, 0:512], AF.Copy),
                    reads=[f"psP{mt % 2}"], writes=[f"Kf:{mt}"])
        wload(2, 1)
        for tt in range(NT):
            for nh in range(2):
                def mmV(E, tt=tt, nh=nh):
                    last = None
                    for kt in range(PT):
                        last = E.matmul(psP[nh][:, 0:512],
                                        xnf[:, sl(kt * T, tt)],
                                        wb[1][:, kt * D + nh * 512:kt * D + (nh + 1) * 512],
                                        start=(kt == 0), stop=(kt == PT - 1))
                    return last
                s.op("pe", mmV, reads=["wb1"] + [f"xnf:{k}" for k in range(PT)],
                     writes=[f"psP{nh}"])
                s.op("act", lambda E, tt=tt, nh=nh: E.activation(
                    Vt[:, tt * D + nh * 512:tt * D + (nh + 1) * 512],
                    psP[nh][:, 0:512], AF.Copy),
                    reads=[f"psP{nh}"], writes=[f"Vt:{tt}"])
        wload(0, 0)
        wload(3, 1)
        for g in range(T // 512):
            for mt in range(PT):
                def mmQ(E, g=g, mt=mt):
                    last = None
                    for kt in range(PT):
                        last = E.matmul(psP[mt % 2][:, 0:512],
                                        wb[0][:, sl(kt * D, mt)],
                                        xnf[:, kt * T + g * 512:kt * T + (g + 1) * 512],
                                        start=(kt == 0), stop=(kt == PT - 1))
                    return last
                s.op("pe", mmQ, reads=["wb0"] + [f"xnf:{k}" for k in range(PT)],
                     writes=[f"psP{mt % 2}"])
                s.op("act", lambda E, mt=mt: E.activation(
                    Qf[:, ts(mt, 512)], psP[mt % 2][:, 0:512], AF.Copy),
                    reads=[f"psP{mt % 2}"], writes=["Qf"])
            for qi4 in range(4):
                qi = g * 4 + qi4
                nkc = qi // 4 + 1
                for h in range(NH):
                    fp0 = (h * HD) // 128
                    po = (h * HD) % 128
                    for c in range(nkc):
                        diag = (c == nkc - 1)

                        def mmS(E, qi4=qi4, c=c, fp0=fp0, po=po):
                            if HD == 64:
                                return E.matmul(
                                    psS[c % 2][:, 0:512],
                                    Qf[po:po + 64, sl(fp0 * 512, qi4)],
                                    Kf[po:po + 64, fp0 * T + c * 512:fp0 * T + (c + 1) * 512],
                                    start=True, stop=True)
                            last = None
                            for kt in range(PT):
                                last = E.matmul(
                                    psS[c % 2][:, 0:512],
                                    Qf[:, sl(kt * 512, qi4)],
                                    Kf[:, kt * T + c * 512:kt * T + (c + 1) * 512],
                                    start=(kt == 0), stop=(kt == PT - 1))
                            return last

                        kreads = [f"Kf:{fp0}"] if HD == 64 else [f"Kf:{k}" for k in range(PT)]
                        s.op("pe", mmS, reads=["Qf"] + kreads, writes=[f"psS{c % 2}"])
                        if diag:
                            s.op("dve", lambda E, c=c, qi4=qi4: E.scalar_tensor_tensor(
                                Ssb[:, c * 512:(c + 1) * 512], psS[c % 2][:, 0:512], SCALE,
                                p._cst["cmask4_bf"][:, qi4 * 512:(qi4 + 1) * 512],
                                ALU.mult, ALU.add),
                                reads=[f"psS{c % 2}", "c:cmask4_bf"], writes=[f"Ssb:{c}"])
                            s.op("dve", lambda E, c=c: E.tensor_reduce(
                                mx[:, c:c + 1], Ssb[:, c * 512:(c + 1) * 512], AX.X, ALU.max),
                                reads=[f"Ssb:{c}"], writes=[f"mx:{c}"])
                        else:
                            s.op("dve", lambda E, c=c: E.tensor_scalar(
                                Ssb[:, c * 512:(c + 1) * 512], psS[c % 2][:, 0:512],
                                SCALE, None, ALU.mult, ALU.max, accum_out=mx[:, c:c + 1]),
                                reads=[f"psS{c % 2}"], writes=[f"Ssb:{c}", f"mx:{c}"])
                    s.op("dve", lambda E, nkc=nkc: E.tensor_reduce(
                        nmx[:, 0:1], mx[:, 0:nkc], AX.X, ALU.max, negate=True),
                        reads=[f"mx:{c}" for c in range(nkc)], writes=["nmx"])
                    for c in range(nkc):
                        s.op("act", lambda E, c=c: E.activation(
                            Pbf[:, c * 512:(c + 1) * 512], Ssb[:, c * 512:(c + 1) * 512],
                            AF.Exp, bias=nmx[:, 0:1], accum_out=den[:, c:c + 1]),
                            reads=[f"Ssb:{c}", "nmx"], writes=[f"Pbf:{c}", f"den:{c}"])
                    s.op("dve", lambda E, nkc=nkc: E.tensor_reduce(
                        den[:, 6:7], den[:, 0:nkc], AX.X, ALU.add),
                        reads=[f"den:{c}" for c in range(nkc)], writes=["densum"])
                    s.op("dve", lambda E: E.reciprocal(rcp[:, 0:1], den[:, 6:7]),
                         reads=["densum"], writes=["rcp"])
                    for c in range(nkc):
                        def mmT(E, c=c):
                            last = None
                            for j in range(4):
                                last = E.transpose(psT[:, ts(j)], Pbf[:, sl(c * 512, j)],
                                                   p._cst["ident_bf"][:, :])
                            return last

                        s.op("pe", mmT, reads=[f"Pbf:{c}", "c:ident_bf"], writes=["psT"])
                        s.op("act", lambda E: E.activation(PTsb[:, 0:512], psT[:, 0:512],
                                                           AF.Copy),
                             reads=["psT"], writes=["PTsb"])

                        def mmPV(E, c=c, h=h, nkc=nkc):
                            last = None
                            for j in range(4):
                                tt = c * 4 + j
                                st = (c == 0 and j == 0)
                                sp = (c == nkc - 1 and j == 3)
                                if HD == 64:
                                    last = E.matmul(psO[:, h * 64:(h + 1) * 64],
                                                    PTsb[:, ts(j)],
                                                    Vt[:, tt * D + h * 64:tt * D + (h + 1) * 64],
                                                    start=st, stop=sp)
                                else:
                                    for nh in range(2):
                                        last = E.matmul(
                                            psO[:, nh * 512:(nh + 1) * 512],
                                            PTsb[:, ts(j)],
                                            Vt[:, tt * D + nh * 512:tt * D + (nh + 1) * 512],
                                            start=st, stop=sp)
                            return last

                        s.op("pe", mmPV,
                             reads=["PTsb"] + [f"Vt:{c * 4 + j}" for j in range(4)],
                             writes=["psO"])
                    if HD == 64:
                        s.op("dve", lambda E, qi4=qi4, h=h: E.tensor_scalar(
                            Ot[:, qi4 * D + h * 64:qi4 * D + (h + 1) * 64],
                            psO[:, h * 64:(h + 1) * 64], rcp[:, 0:1], None, ALU.mult),
                            reads=["psO", "rcp"], writes=[f"Ot:{qi4}"])
                    else:
                        def nrm(E, qi4=qi4):
                            E.tensor_scalar(Ot[:, qi4 * D:qi4 * D + 512], psO[:, 0:512],
                                            rcp[:, 0:1], None, ALU.mult)
                            return E.tensor_scalar(Ot[:, qi4 * D + 512:(qi4 + 1) * D],
                                                   psO[:, 512:1024], rcp[:, 0:1], None,
                                                   ALU.mult)
                        s.op("dve", nrm, reads=["psO", "rcp"], writes=[f"Ot:{qi4}"])
            for qi4 in range(4):
                def mmTO(E, qi4=qi4):
                    last = None
                    for j in range(PT):
                        last = E.transpose(psT[:, ts(j)], Ot[:, sl(qi4 * D, j)],
                                           p._cst["ident_bf"][:, :])
                    return last

                s.op("pe", mmTO, reads=[f"Ot:{qi4}", "c:ident_bf"], writes=["psT"])
                for j in range(PT):
                    s.op("act", lambda E, qi4=qi4, j=j: E.activation(
                        Of[:, sl(j * 512, qi4)], psT[:, ts(j)], AF.Copy),
                        reads=["psT"], writes=[f"Of:{qi4}:{j}"])
            for mt in range(PT):
                def mmproj(E, mt=mt):
                    last = None
                    for kt in range(PT):
                        last = E.matmul(psR[:, 0:512],
                                        wb[1][:, sl(kt * D, mt)],
                                        Of[:, kt * 512:(kt + 1) * 512],
                                        start=(kt == 0), stop=(kt == PT - 1))
                    return last

                s.op("pe", mmproj,
                     reads=["wb1"] + [f"Of:{q4}:{kt}" for q4 in range(4) for kt in range(PT)],
                     writes=["psR"])
                s.op("dve", lambda E, mt=mt: E.tensor_copy(yall[:, ts(mt, 512)],
                                                           psR[:, 0:512]),
                     reads=["psR"], writes=[f"yall:{mt}"])
            emit_group_residual(p, b, g, yall, xin, xout, psS[0], "psS0", (rx, ry))
    if fused:
        p.s.barrier()
    _sc.__exit__(None, None, None)
    return p





def build_mmat(p=None, W=None, mout=None):
    fused = p is not None
    if not fused:
        p = Prog("mmat")
    _sc = p.scope() if fused else contextlib.nullcontext()
    _sc.__enter__()
    s = p.s
    n_ = np.arange(D)
    ang = 2.0 * np.pi / D * np.outer(n_, n_)
    Chbm = p.dram_in("Cmat", np.cos(ang).astype(np.float32), dtype=F32)
    Shbm = p.dram_in("Smat", np.sin(ang).astype(np.float32), dtype=F32)
    wfr = W["wfr"] if W else p.dram_in("wfr", shape=[D, D], dtype=F32)
    wfi = W["wfi"] if W else p.dram_in("wfi", shape=[D, D], dtype=F32)
    Mout = mout if mout is not None else p.dram_out("Mout", [D, D], F32)
    PTh = p.nc.dram_tensor(p.tag + "PTh", [D, D], F32, kind="Internal").ap()
    QNh = p.nc.dram_tensor(p.tag + "QNh", [D, D], F32, kind="Internal").ap()

    lbuf = p.sb("lbuf", [128, PT * 128], F32)
    lbuf2 = p.sb("lbuf2", [128, PT * 128], F32)
    rbuf = p.sb("rbuf", [128, PT * 512], F32)
    rbuf2 = p.sb("rbuf2", [128, PT * 512], F32)
    tmp = p.sb("tmp", [128, 512], F32)
    obuf = p.sb("obuf", [128, 512], F32)
    psA = p.psum("psA", [128, 512], F32)
    psB = p.psum("psB", [128, 512], F32)

    def ldl(buf, key, hbm, mt):
        src = hbm[:, ts(mt)].rearrange("(kt p) c -> p kt c", p=128)
        dst = bass.AP(buf, 0, [[buf.shape[1], 128], [128, PT], [1, 128]])
        p.s.dma("pool", lambda E: E.dma_start(dst, src), writes=[key])

    def ldr(buf, key, hbm, nc2):
        src = hbm[:, nc2 * 512:(nc2 + 1) * 512].rearrange("(kt p) c -> p kt c", p=128)
        dst = bass.AP(buf, 0, [[buf.shape[1], 128], [512, PT], [1, 512]])
        p.s.dma("pool", lambda E: E.dma_start(dst, src), writes=[key])

    def mmpass(lA, lB, rA, rB, out_h, mode, scale=None):
        for nc2 in range(2):
            ldr(rbuf, "rbuf", rA, nc2)
            ldr(rbuf2, "rbuf2", rB, nc2)
            for mt in range(PT):
                ldl(lbuf, "lbuf", lA, mt)
                ldl(lbuf2, "lbuf2", lB, mt)
                if mode == "acc":
                    def mm1b(E):
                        last = None
                        for kt in range(PT):
                            last = E.matmul(psA[:, 0:512], lbuf[:, ts(kt)],
                                            rbuf[:, ts(kt, 512)],
                                            start=(kt == 0), stop=False)
                        for kt in range(PT):
                            last = E.matmul(psA[:, 0:512], lbuf2[:, ts(kt)],
                                            rbuf2[:, ts(kt, 512)],
                                            start=False, stop=(kt == PT - 1))
                        return last
                    s.op("pe", mm1b, reads=["lbuf", "lbuf2", "rbuf", "rbuf2"],
                         writes=["psA"])
                    if scale is None:
                        s.op("dve", lambda E: E.tensor_copy(obuf[:, :], psA[:, 0:512]),
                             reads=["psA"], writes=["obuf"])
                    else:
                        s.op("dve", lambda E: E.tensor_scalar(
                            obuf[:, :], psA[:, 0:512], scale, None, ALU.mult),
                            reads=["psA"], writes=["obuf"])
                else:
                    def mm1(E):
                        last = None
                        for kt in range(PT):
                            last = E.matmul(psA[:, 0:512], lbuf[:, ts(kt)],
                                            rbuf[:, ts(kt, 512)],
                                            start=(kt == 0), stop=(kt == PT - 1))
                        return last
                    s.op("pe", mm1, reads=["lbuf", "rbuf"], writes=["psA"])
                    s.op("dve", lambda E: E.tensor_copy(tmp[:, :], psA[:, 0:512]),
                         reads=["psA"], writes=["tmp"])

                    def mm2(E):
                        last = None
                        for kt in range(PT):
                            last = E.matmul(psB[:, 0:512], lbuf2[:, ts(kt)],
                                            rbuf2[:, ts(kt, 512)],
                                            start=(kt == 0), stop=(kt == PT - 1))
                        return last
                    s.op("pe", mm2, reads=["lbuf2", "rbuf2"], writes=["psB"])
                    s.op("dve", lambda E: E.tensor_tensor(obuf[:, :], tmp[:, :],
                                                          psB[:, 0:512], ALU.subtract),
                         reads=["tmp", "psB"], writes=["obuf"])
                s.dma("sp", lambda E, mt=mt, nc2=nc2: E.dma_start(
                    out_h[ts(mt), nc2 * 512:(nc2 + 1) * 512], obuf[:, :]),
                    reads=["obuf"])

    mmpass(wfr, wfi, Chbm, Shbm, PTh, "acc")
    mmpass(wfr, wfi, Shbm, Chbm, QNh, "sub")
    mmpass(PTh, QNh, Chbm, Shbm, Mout, "acc", scale=1.0 / D)
    if fused:
        p.s.barrier()
    _sc.__exit__(None, None, None)
    return p


def build_ffn0(p=None, io=None, W=None):
    fused = p is not None
    if not fused:
        p = Prog("ffn0")
    _sc = p.scope() if fused else contextlib.nullcontext()
    _sc.__enter__()
    s = p.s
    p._cst = load_consts(p, ["ident_bf", "ident_f32"])
    xin = io[0] if fused else p.dram_in("xin", shape=[B_CORE, T, D], dtype=F32)
    xout = io[1] if fused else p.dram_out("xout", [B_CORE, T, D], F32)
    Min = W["Min"] if W else p.dram_in("Min", shape=[D, D], dtype=F32)
    bias = W["ffn_bias"] if W else p.dram_in("bias", shape=[1, D], dtype=F32)

    lnt = alloc_lnt(p)
    xnf = p.sb("xnf", [128, PT * T], BF16)
    Msb = p.sb("Msb", [128, PT * D], BF16)
    yall = p.sb("yall", [128, PT * 512], F32)
    rx = p.sb("rx", [128, D], F32)
    ry = p.sb("ry", [128, D], F32)
    brow = p.sb("brow", [8, 128], F32)
    bcol = p.sb("bcol", [128, 8], F32)
    psA = p.psum("psA", [128, 512], F32)
    psB = p.psum("psB", [128, 512], F32)
    psT = p.psum("psT", [128, 1024], BF16)
    psR = p.psum("psR", [128, 512], F32)

    load_weight_kt(p, "M", Min, D, D, 0, D, Msb, "Msb")
    s.dma("sp", lambda E: E.dma_start(
        brow[:, :], bias.rearrange("o (pt x) -> (o pt) x", x=128)), writes=["brow"])
    s.op("pe", lambda E: E.transpose(psB[0:128, 0:8], brow[0:8, 0:128],
                                     p._cst["ident_f32"][0:8, 0:8]),
         reads=["brow", "c:ident_f32"], writes=["psB"])
    s.op("dve", lambda E: E.tensor_copy(bcol[:, :], psB[0:128, 0:8]),
         reads=["psB"], writes=["bcol"])

    for b in range(B_CORE):
        for tt in range(NT):
            def writer(j, tt=tt):
                return (xnf[:, sl(j * T, tt)], f"xnf:{j}")
            emit_lnt_tile(p, lnt, xin[b, ts(tt), :], psT, "psT", writer)
        for g in range(T // 512):
            for mt in range(PT):
                def mmM(E, g=g, mt=mt):
                    last = None
                    for kt in range(PT):
                        last = E.matmul(psR[:, 0:512],
                                        Msb[:, sl(kt * D, mt)],
                                        xnf[:, kt * T + g * 512:kt * T + (g + 1) * 512],
                                        start=(kt == 0), stop=(kt == PT - 1))
                    return last

                s.op("pe", mmM, reads=["Msb"] + [f"xnf:{k}" for k in range(PT)],
                     writes=["psR"])
                s.op("dve", lambda E, mt=mt: E.tensor_scalar(
                    yall[:, ts(mt, 512)], psR[:, 0:512], bcol[:, mt:mt + 1], None, ALU.add),
                    reads=["psR", "bcol"], writes=[f"yall:{mt}"])
            emit_group_residual(p, b, g, yall, xin, xout, psA, "psA", (rx, ry))
    if fused:
        p.s.barrier()
    _sc.__exit__(None, None, None)
    return p





def build_hades(p=None, io=None, W=None):
    fused = p is not None
    if not fused:
        p = Prog("hades")
    _sc = p.scope() if fused else contextlib.nullcontext()
    _sc.__enter__()
    s = p.s
    p._cst = load_consts(p, ["ident_bf", "ident_f32", "ones_col_bf"])
    one = p.constv(1.0)

    xin = io[0] if fused else p.dram_in("xin", shape=[B_CORE, T, D], dtype=F32)
    xout = io[1] if fused else p.dram_out("xout", [B_CORE, T, D], F32)
    def gw(name, shape):
        return W[name] if W else p.dram_in(name, shape=shape, dtype=F32)
    WinT = gw("WinT", [D, 2336])
    WhT = gw("WhT", [1056, 40])
    WoT = gw("WoT", [D, D])
    convw = gw("convw", [1280, 4])
    convb = gw("convb", [1, 1280])
    dtb_h = gw("dtb", [1, 1])
    gam_h = gw("gam", [1, 16])
    alog_h = gw("alog", [16, 1])
    dd_h = gw("dd", [16, 1])
    nw_h = gw("nw", [1, D])
    trilnp = np.where(np.arange(128)[None, :] >= np.arange(128)[:, None],
                      0.0, NEG).astype(np.float32)
    tril_h = p.dram_in("tril", trilnp, dtype=F32)
    esel = np.zeros((16, 16 * 128), np.float32)
    for h in range(16):
        esel[h, h * 128:(h + 1) * 128] = 1.0
    esel_h = p.dram_in("esel", esel, dtype=F32)
    eexp = np.zeros((16, PT * 128), np.float32)
    for pt in range(PT):
        for m in range(128):
            eexp[2 * pt + m // 64, pt * 128 + m] = 1.0
    eexp_h = p.dram_in("eexp", eexp, dtype=F32)
    rd_h = p.dram_in("rdrow", (1.0 / np.arange(1, T + 1, dtype=np.float64)
                               ).astype(np.float32).reshape(1, T), dtype=F32)
    onesr_h = p.dram_in("onesr", np.ones((1, 128), np.float32), dtype=F32)

    lnt = alloc_lnt(p)
    xnf = p.sb("xnf", [128, PT * 512], BF16)
    xnf32 = p.sb("xnf32", [128, PT * 512], F32)
    cum = p.sb("cum", [128, PT * 512], F32)
    wbuf = p.sb("wbuf", [128, PT * 1280], BF16)
    woT = p.sb("woT", [128, PT * D], BF16)
    wdt32 = p.sb("wdt32", [128, PT * 32], F32)
    hp32 = p.sb("hp32", [128, 9 * 40], F32)
    zc = p.sb("zc", [128, PT * 512], F32)
    xbc = p.sb("xbc", [128, 10 * 512], F32)
    dta = p.sb("dta", [128, 4 * 32], F32)
    dtaf = p.sb("dtaf", [32, 4 * 128], F32)
    hb = p.sb("hb", [128, 4 * 40], F32)
    hw_ = p.sb("hw_", [128, 28], F32)
    ohb = p.sb("ohb", [128, 28], F32)
    csb = p.sb("csb", [128, 28], F32)
    fst = p.sb("fst", [128, 28], F32)
    gth = p.sb("gth", [128, 28], F32)
    dt16 = p.sb("dt16", [128, 4 * 16], F32)
    tmpA = p.sb("tmpA", [128, 16], F32)
    tmpB = p.sb("tmpB", [128, 16], F32)
    dtf = p.sb("dtf", [16, 512], F32)
    Dtc = p.sb("Dtc", [16, 128], F32)
    ADc = p.sb("ADc", [16, 128], F32)
    grow = p.sb("grow", [16, 128], F32)
    arg2 = p.sb("arg2", [16, 128], F32)
    dlf = p.sb("dlf", [16, 128], F32)
    ADt = p.sb("ADt", [128, 16], F32)
    dlt = p.sb("dlt", [128, 16], F32)
    w4 = p.sb("w4", [128, 10 * 4], F32)
    cbcol = p.sb("cbcol", [128, 10], F32)
    nwcol = p.sb("nwcol", [128, 8], F32)
    tail3 = p.sb("tail3", [128, 10 * 3], F32)
    xe = p.sb("xe", [128, 515], F32)
    cvt = p.sb("cvt", [128, 512], F32)
    rowb = p.sb("rowb", [10, 128], F32)
    Bb = p.sb("Bb", [128, 128], BF16)
    Cb = p.sb("Cb", [128, 128], BF16)
    Btb = p.sb("Btb", [128, 128], BF16)
    CBt = p.sb("CBt", [128, 128], BF16)
    gbf = p.sb("gbf", [128, 128], BF16)
    Mtb = p.sb("Mtb", [128, 128], BF16)
    Ctl = p.sb("Ctl", [128, 128], BF16)
    argsb = p.sb("argsb", [128, 128], F32)
    xtf = p.sb("xtf", [128, PT * 128], BF16)
    xtt = p.sb("xtt", [128, PT * 128], BF16)
    xds = p.sb("xds", [128, 64], BF16)
    Sst = p.sb("Sst", [128, 16 * 64], F32)
    Ssb = p.sb("Ssb", [128, 16 * 64], BF16)
    bsb = p.sb("bsb", [128, 16], F32)
    yc = cum
    szt = p.sb("szt", [128, 512], F32)
    sqb = p.sb("sqb", [128, 512], BF16)
    rrow = p.sb("rrow", [1, 2 * 512], F32)
    rmsbc = p.sb("rmsbc", [128, 512], F32)
    rdbc = p.sb("rdbc", [128, 512], F32)
    rdrow = p.sb("rdrow_s", [1, T], F32)
    gmbc = p.sb("gmbc", [128, 16], F32)
    dtbc = p.sb("dtbc", [128, 1], F32)
    Acol = p.sb("Acol", [16, 1], F32)
    Dxc = p.sb("Dxc", [128, 8], F32)
    carry = p.sb("carry", [128, 8], F32)
    trils = p.sb("trils", [128, 128], F32)
    esels = p.sb("esels", [16, 16 * 128], F32)
    eexps = p.sb("eexps", [16, PT * 128], F32)
    grow16 = p.sb("grow16", [1, 16], F32)
    d16 = p.sb("d16", [16, 1], F32)
    onesr = p.sb("onesr_s", [1, 128], F32)
    yall = xnf32
    rx = p.sb("rx", [128, D], F32)
    ry = p.sb("ry", [128, D], F32)

    psT = p.psum("psT", [128, 1024], BF16)
    psA = p.psum("psA", [128, 512], F32)
    psB = p.psum("psB", [128, 512], F32)
    psR = p.psum("psR", [128, 512], F32)
    psY0 = p.psum("psY0", [128, 512], F32)
    psY1 = p.psum("psY1", [128, 512], F32)
    psC = p.psum("psC", [128, 512], F32)

    s.dma("sp", lambda E: E.dma_start(trils[:, :], tril_h[:, :]), writes=["trils"])
    s.dma("sp", lambda E: E.dma_start(esels[:, :], esel_h[:, :]), writes=["esels"])
    s.dma("sp", lambda E: E.dma_start(eexps[:, :], eexp_h[:, :]), writes=["eexps"])
    s.dma("sp", lambda E: E.dma_start(rdrow[:, :], rd_h[:, :]), writes=["rdrow"])
    s.dma("sp", lambda E: E.dma_start(Acol[:, :], alog_h[:, :]), writes=["Acol"])
    s.dma("sp", lambda E: E.dma_start(d16[:, :], dd_h[:, :]), writes=["d16"])
    s.dma("sp", lambda E: E.dma_start(onesr[:, :], onesr_h[:, :]), writes=["onesr"])
    load_weight_kt(p, "woT", WoT, D, D, 0, D, woT, "woT")
    src_dt = WinT[:, 2304:2336].rearrange("(kt p) c -> p kt c", p=128)
    s.dma("sp", lambda E, src=src_dt: E.dma_start(
        bass.AP(wdt32, 0, [[wdt32.shape[1], 128], [32, PT], [1, 32]]), src),
        writes=["wdt32"])
    src_hp = WhT[0:1024, :].rearrange("(kt p) c -> p kt c", p=128)
    s.dma("sp", lambda E, src=src_hp: E.dma_start(
        bass.AP(hp32, 0, [[hp32.shape[1], 128], [40, 8], [1, 40]]), src),
        writes=["hp32"])
    s.dma("sp", lambda E: E.dma_start(hp32[0:32, 8 * 40:9 * 40], WhT[1024:1056, :]),
          writes=["hp32b"])
    s.dma("sp", lambda E: E.dma_start(
        bass.AP(w4, 0, [[w4.shape[1], 128], [4, 10], [1, 4]]),
        convw.rearrange("(ct p) s_ -> p ct s_", p=128)), writes=["w4"])
    s.dma("sp", lambda E: E.dma_start(
        rowb[:, :], convb.rearrange("o (ct x) -> (o ct) x", x=128)), writes=["rowb"])
    s.op("pe", lambda E: E.transpose(psC[0:128, 0:10], rowb[0:10, 0:128],
                                     p._cst["ident_f32"][0:10, 0:10]),
         reads=["rowb", "c:ident_f32"], writes=["psC"])
    s.op("dve", lambda E: E.tensor_copy(cbcol[:, :], psC[0:128, 0:10]),
         reads=["psC"], writes=["cbcol"])
    s.dma("sp", lambda E: E.dma_start(
        rowb[0:8, :], nw_h.rearrange("o (pt x) -> (o pt) x", x=128)), writes=["rowb"])
    s.op("pe", lambda E: E.transpose(psC[0:128, 16:24], rowb[0:8, 0:128],
                                     p._cst["ident_f32"][0:8, 0:8]),
         reads=["rowb", "c:ident_f32"], writes=["psC"])
    s.op("dve", lambda E: E.tensor_copy(nwcol[:, :], psC[0:128, 16:24]),
         reads=["psC"], writes=["nwcol"])
    s.op("act", lambda E: E.activation(Acol[:, :], Acol[:, :], AF.Exp),
         reads=["Acol"], writes=["Acol"])
    s.op("dve", lambda E: E.tensor_scalar(Acol[:, :], Acol[:, :], -1.0, None, ALU.mult),
         reads=["Acol"], writes=["Acol"])
    s.dma("sp", lambda E: E.dma_start(grow16[:, :], gam_h[:, :]), writes=["grow16"])
    s.dma("sp", lambda E: E.dma_start(rowb[0:1, 0:1], dtb_h[:, :]), writes=["rowb"])

    def bc16b(E):
        E.matmul(psC[:, 0:16], onesr[0:1, :], grow16[0:1, :], start=True, stop=True)
        return E.matmul(psC[:, 16:17], onesr[0:1, :], rowb[0:1, 0:1],
                        start=True, stop=True)
    s.op("pe", bc16b, reads=["grow16", "rowb", "onesr"], writes=["psC"])
    s.op("dve", lambda E: E.tensor_copy(gmbc[:, :], psC[:, 0:16]),
         reads=["psC"], writes=["gmbc"])
    s.op("dve", lambda E: E.tensor_copy(dtbc[:, :], psC[:, 16:17]),
         reads=["psC"], writes=["dtbc"])

    def dexp_mm(E):
        last = None
        for pt in range(PT):
            last = E.matmul(psC[:, 32 + pt:33 + pt], eexps[:, ts(pt)], d16[:, 0:1],
                            start=True, stop=True)
        return last
    s.op("pe", dexp_mm, reads=["eexps", "d16"], writes=["psC"])
    s.op("dve", lambda E: E.tensor_copy(Dxc[:, :], psC[:, 32:40]),
         reads=["psC"], writes=["Dxc"])

    NCH = T // 512
    for b in range(B_CORE):
        s.op("dve", lambda E: E.memset(carry[:, :], 0.0), writes=["carry"])
        s.op("dve", lambda E: E.memset(tail3[:, :], 0.0), writes=["tail3"])
        s.op("dve", lambda E: E.memset(Sst[:, :], 0.0), writes=["Sst"])
        s.op("dve", lambda E: E.memset(Ssb[:, :], 0.0), writes=["Ssb"])
        for c5 in range(NCH):
            t0 = c5 * 512
            for j4 in range(4):
                tt = c5 * 4 + j4

                def writer(j, j4=j4):
                    return (xnf[:, sl(j * 512, j4)], f"xnf:{j}")
                emit_lnt_tile(p, lnt, xin[b, ts(tt), :], psT, "psT", writer)
                s.op("pool", lambda E: E.tensor_scalar(
                    lnt["xt"][:, :], lnt["xt"][:, :], lnt["st"][:, 2:3],
                    lnt["st"][:, 7:8], ALU.subtract, ALU.mult),
                    reads=["lnt_xt", "lnt_mean_msq", "lnt_rstd"], writes=["lnt_xt"])
                for w2 in range(2):
                    def mmT32(E, w2=w2):
                        last = None
                        for j in range(4):
                            last = E.transpose(psA[:, ts(j)],
                                               lnt["xt"][:, sl(w2 * 512, j)],
                                               p._cst["ident_f32"][:, :])
                        return last
                    s.op("pe", mmT32, reads=["lnt_xt", "c:ident_f32"], writes=["psA"])
                    for j in range(4):
                        jj = w2 * 4 + j
                        s.op("dve", lambda E, jj=jj, j=j, j4=j4: E.tensor_copy(
                            xnf32[:, sl(jj * 512, j4)], psA[:, ts(j)]),
                            reads=["psA"], writes=[f"xnf32:{jj}"])
            load_weight_kt(p, "wz", WinT, D, 2336, 0, 1024, wbuf, "wbuf")
            for mt in range(PT):
                def mmz(E, mt=mt):
                    last = None
                    for kt in range(PT):
                        last = E.matmul(psR[:, 0:512], wbuf[:, sl(kt * 1024, mt)],
                                        xnf[:, ts(kt, 512)],
                                        start=(kt == 0), stop=(kt == PT - 1))
                    return last
                s.op("pe", mmz, reads=["wbuf"] + [f"xnf:{k}" for k in range(PT)],
                     writes=["psR"])
                s.op("dve", lambda E, mt=mt: E.tensor_copy(zc[:, ts(mt, 512)],
                                                           psR[:, 0:512]),
                     reads=["psR"], writes=[f"zc:{mt}"])
            load_weight_kt(p, "wx", WinT, D, 2336, 1024, 1280, wbuf, "wbuf")
            for mt in range(10):
                def mmx(E, mt=mt):
                    last = None
                    for kt in range(PT):
                        last = E.matmul(psR[:, 0:512], wbuf[:, sl(kt * 1280, mt)],
                                        xnf[:, ts(kt, 512)],
                                        start=(kt == 0), stop=(kt == PT - 1))
                    return last
                s.op("pe", mmx, reads=["wbuf"] + [f"xnf:{k}" for k in range(PT)],
                     writes=["psR"])
                s.op("dve", lambda E, mt=mt: E.tensor_copy(xbc[:, ts(mt, 512)],
                                                           psR[:, 0:512]),
                     reads=["psR"], writes=[f"xbc:{mt}"])
            for j4 in range(4):
                def mmdt(E, j4=j4):
                    last = None
                    for kt in range(PT):
                        last = E.matmul(psB[:, 0:32], xnf32[:, sl(kt * 512, j4)],
                                        wdt32[:, ts(kt, 32)],
                                        start=(kt == 0), stop=(kt == PT - 1))
                    return last
                s.op("pe", mmdt, reads=["wdt32"] + [f"xnf32:{k}" for k in range(PT)],
                     writes=["psB"])
                s.op("dve", lambda E, j4=j4: E.tensor_copy(dta[:, ts(j4, 32)],
                                                           psB[:, 0:32]),
                     reads=["psB"], writes=[f"dta:{j4}"])
                s.op("pe", lambda E, j4=j4: E.transpose(
                    psB[0:32, 128:256], dta[:, ts(j4, 32)],
                    p._cst["ident_f32"][:, :]),
                    reads=[f"dta:{j4}", "c:ident_f32"], writes=["psB"])
                s.op("dve", lambda E, j4=j4: E.tensor_copy(dtaf[0:32, ts(j4)],
                                                           psB[0:32, 128:256]),
                     reads=["psB"], writes=[f"dtaf:{j4}"])
            s.op("pe", lambda E, t0=t0: E.matmul(psA[:, 0:512], onesr[0:1, :],
                                                 rdrow[0:1, t0:t0 + 512],
                                                 start=True, stop=True),
                 reads=["rdrow", "onesr"], writes=["psA"])
            s.op("dve", lambda E: E.tensor_copy(rdbc[:, :], psA[:, 0:512]),
                 reads=["psA"], writes=["rdbc"])
            for pt in range(PT):
                s.op("dve", lambda E, pt=pt: E.tensor_tensor_scan(
                    cum[:, ts(pt, 512)], xnf32[:, ts(pt, 512)],
                    xnf32[:, ts(pt, 512)], carry[:, pt:pt + 1], ALU.add, ALU.bypass),
                    reads=[f"xnf32:{pt}", "carry"], writes=[f"cum:{pt}"])
                s.op("dve", lambda E, pt=pt: E.tensor_copy(
                    carry[:, pt:pt + 1], cum[:, pt * 512 + 511:pt * 512 + 512]),
                    reads=[f"cum:{pt}"], writes=["carry"])
                s.op("pool", lambda E, pt=pt: E.tensor_tensor(
                    cum[:, ts(pt, 512)], cum[:, ts(pt, 512)], rdbc[:, :], ALU.mult),
                    reads=[f"cum:{pt}", "rdbc"], writes=[f"cum:{pt}"])
                s.op("pool", lambda E, pt=pt: E.tensor_tensor(
                    cum[:, ts(pt, 512)], xnf32[:, ts(pt, 512)], cum[:, ts(pt, 512)],
                    ALU.subtract),
                    reads=[f"xnf32:{pt}", f"cum:{pt}"], writes=[f"cum:{pt}"])
            for j4 in range(4):
                def mmh(E, j4=j4):
                    for kt in range(PT):
                        E.matmul(psB[:, 0:40], cum[:, sl(kt * 512, j4)],
                                 hp32[:, kt * 40:(kt + 1) * 40],
                                 start=(kt == 0), stop=False)
                    return E.matmul(psB[:, 0:40], dtaf[0:32, ts(j4)],
                                    hp32[0:32, 8 * 40:9 * 40], start=False, stop=True)
                s.op("pe", mmh,
                     reads=[f"cum:{k}" for k in range(PT)] + [f"dtaf:{j4}", "hp32",
                                                              "hp32b"],
                     writes=["psB"])
                s.op("dve", lambda E, j4=j4: E.tensor_copy(hb[:, ts(j4, 40)],
                                                           psB[:, 0:40]),
                     reads=["psB"], writes=[f"hb:{j4}"])
            for j4 in range(4):
                s.op("dve", lambda E, j4=j4: E.tensor_copy(
                    hw_[:, :], hb[:, j4 * 40:j4 * 40 + 28]),
                    reads=[f"hb:{j4}"], writes=["hw_"])
                for k in range(12):
                    s.op("dve", lambda E: E.tensor_reduce(
                        tmpA[:, 0:1], hw_[:, :], AX.X, ALU.max),
                        reads=["hw_"], writes=["selmax"])
                    s.op("dve", lambda E: E.tensor_scalar(
                        ohb[:, :], hw_[:, :], tmpA[:, 0:1], None, ALU.is_equal),
                        reads=["hw_", "selmax"], writes=["ohb"])
                    s.op("dve", lambda E: E.tensor_tensor_scan(
                        csb[:, :], ohb[:, :], ohb[:, :], 0.0, ALU.add, ALU.bypass),
                        reads=["ohb"], writes=["csb"])
                    s.op("dve", lambda E: E.tensor_scalar(
                        csb[:, :], csb[:, :], 1.0, None, ALU.is_equal),
                        reads=["csb"], writes=["csb"])
                    s.op("dve", lambda E: E.tensor_tensor(
                        fst[:, :], ohb[:, :], csb[:, :], ALU.mult),
                        reads=["ohb", "csb"], writes=["fst"])
                    s.op("dve", lambda E, j4=j4: E.tensor_tensor(
                        gth[:, :], fst[:, :], dta[:, j4 * 32:j4 * 32 + 28], ALU.mult),
                        reads=["fst", f"dta:{j4}"], writes=["gth"])
                    s.op("dve", lambda E, j4=j4, k=k: E.tensor_reduce(
                        dt16[:, j4 * 16 + k:j4 * 16 + k + 1], gth[:, :], AX.X, ALU.add),
                        reads=["gth"], writes=[f"dt16:{j4}"])
                    s.op("dve", lambda E: E.scalar_tensor_tensor(
                        hw_[:, :], fst[:, :], NEG, hw_[:, :], ALU.mult, ALU.add),
                        reads=["fst", "hw_"], writes=["hw_"])
                s.op("dve", lambda E, j4=j4: E.tensor_copy(
                    dt16[:, j4 * 16 + 12:j4 * 16 + 16],
                    dta[:, j4 * 32 + 28:j4 * 32 + 32]),
                    reads=[f"dta:{j4}"], writes=[f"dt16:{j4}"])
                s.op("dve", lambda E, j4=j4: E.tensor_tensor(
                    tmpA[:, 0:12], gmbc[:, 0:12], hb[:, j4 * 40 + 28:j4 * 40 + 40],
                    ALU.mult),
                    reads=["gmbc", f"hb:{j4}"], writes=["tmpA"])
                s.op("dve", lambda E, j4=j4: E.tensor_tensor(
                    dt16[:, j4 * 16:j4 * 16 + 12], dt16[:, j4 * 16:j4 * 16 + 12],
                    tmpA[:, 0:12], ALU.add),
                    reads=["tmpA", f"dt16:{j4}"], writes=[f"dt16:{j4}"])
                s.op("dve", lambda E, j4=j4: E.tensor_scalar(
                    dt16[:, ts(j4, 16)], dt16[:, ts(j4, 16)], dtbc[:, 0:1], None,
                    ALU.add),
                    reads=["dtbc", f"dt16:{j4}"], writes=[f"dt16:{j4}"])
                s.op("act", lambda E, j4=j4: E.activation(
                    tmpA[:, :], dt16[:, ts(j4, 16)], AF.Abs),
                    reads=[f"dt16:{j4}"], writes=["tmpA"])
                s.op("act", lambda E: E.activation(
                    tmpA[:, :], tmpA[:, :], AF.Exp, scale=-1.0),
                    reads=["tmpA"], writes=["tmpA"])
                s.op("act", lambda E: E.activation(
                    tmpA[:, :], tmpA[:, :], AF.Ln, bias=one),
                    reads=["tmpA", "cpool"], writes=["tmpA"])
                s.op("act", lambda E, j4=j4: E.activation(
                    tmpB[:, :], dt16[:, ts(j4, 16)], AF.Relu),
                    reads=[f"dt16:{j4}"], writes=["tmpB"])
                s.op("dve", lambda E, j4=j4: E.tensor_tensor(
                    dt16[:, ts(j4, 16)], tmpA[:, :], tmpB[:, :], ALU.add),
                    reads=["tmpA", "tmpB"], writes=[f"dt16:{j4}"])
                s.op("pe", lambda E, j4=j4: E.transpose(
                    psB[0:16, 256:384], dt16[:, ts(j4, 16)],
                    p._cst["ident_f32"][:, :]),
                    reads=[f"dt16:{j4}", "c:ident_f32"], writes=["psB"])
                s.op("dve", lambda E, j4=j4: E.tensor_copy(
                    dtf[0:16, ts(j4)], psB[0:16, 256:384]),
                    reads=["psB"], writes=[f"dtf:{j4}"])
            for ct in range(10):
                s.op("dve", lambda E, ct=ct: E.tensor_copy(
                    xe[:, 0:3], tail3[:, ct * 3:(ct + 1) * 3]),
                    reads=["tail3"], writes=["xe"])
                s.op("dve", lambda E, ct=ct: E.tensor_copy(
                    xe[:, 3:515], xbc[:, ts(ct, 512)]),
                    reads=[f"xbc:{ct}"], writes=["xe"])
                s.op("dve", lambda E, ct=ct: E.tensor_copy(
                    tail3[:, ct * 3:(ct + 1) * 3], xe[:, 512:515]),
                    reads=["xe"], writes=["tail3"])
                s.op("dve", lambda E, ct=ct: E.tensor_scalar(
                    cvt[:, :], xe[:, 0:512], w4[:, ct * 4:ct * 4 + 1], None, ALU.mult),
                    reads=["xe", "w4"], writes=["cvt"])
                for ss in range(1, 4):
                    s.op("dve", lambda E, ct=ct, ss=ss: E.scalar_tensor_tensor(
                        cvt[:, :], xe[:, ss:ss + 512],
                        w4[:, ct * 4 + ss:ct * 4 + ss + 1],
                        cvt[:, :], ALU.mult, ALU.add),
                        reads=["xe", "w4", "cvt"], writes=["cvt"])
                s.op("act", lambda E, ct=ct: E.activation(
                    xbc[:, ts(ct, 512)], cvt[:, :], AF.Silu,
                    bias=cbcol[:, ct:ct + 1]),
                    reads=["cvt", "cbcol"], writes=[f"xbc:{ct}"])
            for cc in range(4):
                s.op("dve", lambda E, cc=cc: E.tensor_tensor_scan(
                    Dtc[0:16, :], dtf[0:16, ts(cc)], dtf[0:16, ts(cc)], 0.0,
                    ALU.add, ALU.bypass),
                    reads=[f"dtf:{cc}"], writes=["Dtc"])
                s.op("dve", lambda E: E.tensor_scalar(
                    ADc[0:16, :], Dtc[0:16, :], Acol[0:16, 0:1], None, ALU.mult),
                    reads=["Dtc", "Acol"], writes=["ADc"])
                s.op("act", lambda E: E.activation(grow[0:16, :], ADc[0:16, :], AF.Exp),
                     reads=["ADc"], writes=["grow"])
                s.op("dve", lambda E: E.tensor_scalar(
                    arg2[0:16, :], ADc[0:16, :], ADc[0:16, 127:128], -1.0,
                    ALU.subtract, ALU.mult),
                    reads=["ADc"], writes=["arg2"])
                s.op("act", lambda E: E.activation(dlf[0:16, :], arg2[0:16, :], AF.Exp),
                     reads=["arg2"], writes=["dlf"])

                def mmt16(E):
                    E.transpose(psB[0:128, 0:16], ADc[0:16, 0:128],
                                p._cst["ident_f32"][0:16, 0:16])
                    return E.transpose(psB[0:128, 16:32], dlf[0:16, 0:128],
                                       p._cst["ident_f32"][0:16, 0:16])
                s.op("pe", mmt16, reads=["ADc", "dlf", "c:ident_f32"], writes=["psB"])
                s.op("dve", lambda E: E.tensor_copy(ADt[:, :], psB[0:128, 0:16]),
                     reads=["psB"], writes=["ADt"])
                s.op("dve", lambda E: E.tensor_copy(dlt[:, :], psB[0:128, 16:32]),
                     reads=["psB"], writes=["dlt"])

                def mmbeta(E):
                    last = None
                    for h in range(16):
                        last = E.matmul(psC[:, 40 + h:41 + h], esels[:, ts(h)],
                                        grow[0:16, 127:128], start=True, stop=True)
                    return last
                s.op("pe", mmbeta, reads=["esels", "grow"], writes=["psC"])
                s.op("dve", lambda E: E.tensor_copy(bsb[:, :], psC[:, 40:56]),
                     reads=["psC"], writes=["bsb"])
                s.op("dve", lambda E, cc=cc: E.tensor_copy(
                    Bb[:, :], xbc[:, sl(8 * 512, cc)]),
                    reads=["xbc:8"], writes=["Bb"])
                s.op("dve", lambda E, cc=cc: E.tensor_copy(
                    Cb[:, :], xbc[:, sl(9 * 512, cc)]),
                    reads=["xbc:9"], writes=["Cb"])
                s.op("pe", lambda E: E.transpose(psT[:, 0:128], Bb[:, :],
                                                 p._cst["ident_bf"][:, :]),
                     reads=["Bb", "c:ident_bf"], writes=["psT"])
                s.op("dve", lambda E: E.tensor_copy(Btb[:, :], psT[:, 0:128]),
                     reads=["psT"], writes=["Btb"])
                s.op("pe", lambda E: E.matmul(psB[:, 0:128], Bb[:, :], Cb[:, :],
                                              start=True, stop=True),
                     reads=["Bb", "Cb"], writes=["psB"])
                s.op("act", lambda E: E.activation(CBt[:, :], psB[:, 0:128], AF.Copy),
                     reads=["psB"], writes=["CBt"])
                for pt in range(PT):
                    s.op("pe", lambda E, pt=pt, cc=cc: E.matmul(
                        psA[:, ts(pt % 4)], eexps[:, ts(pt)], dtf[0:16, ts(cc)],
                        start=True, stop=True),
                        reads=["eexps", f"dtf:{cc}"], writes=["psA"])
                    s.op("dve", lambda E, pt=pt, cc=cc: E.tensor_tensor(
                        xtf[:, ts(pt)], xbc[:, sl(pt * 512, cc)], psA[:, ts(pt % 4)],
                        ALU.mult),
                        reads=[f"xbc:{pt}", "psA"], writes=[f"xtf:{pt}"])

                def mmtx(E):
                    last = None
                    for pt in range(PT):
                        last = E.transpose(psT[:, ts(pt)], xtf[:, ts(pt)],
                                           p._cst["ident_bf"][:, :])
                    return last
                s.op("pe", mmtx, reads=[f"xtf:{pt}" for pt in range(PT)] +
                     ["c:ident_bf"], writes=["psT"])
                for pt in range(PT):
                    s.op("act", lambda E, pt=pt: E.activation(
                        xtt[:, ts(pt)], psT[:, ts(pt)], AF.Copy),
                        reads=["psT"], writes=[f"xtt:{pt}"])
                for h in range(16):
                    pt = h // 2
                    po = (h % 2) * 64
                    psY = psY0 if pt < 4 else psY1
                    psYk = "psY0" if pt < 4 else "psY1"
                    col = (pt % 4) * 128
                    s.op("pe", lambda E, h=h: E.matmul(
                        psA[:, 0:128], esels[:, ts(h)], ADc[0:16, :],
                        start=True, stop=True),
                        reads=["esels", "ADc"], writes=["psA"])
                    s.op("dve", lambda E, h=h: E.scalar_tensor_tensor(
                        argsb[:, :], psA[:, 0:128], ADt[:, h:h + 1], trils[:, :],
                        ALU.subtract, ALU.add),
                        reads=["psA", "ADt", "trils"], writes=["argsb"])
                    s.op("act", lambda E: E.activation(gbf[:, :], argsb[:, :], AF.Exp),
                         reads=["argsb"], writes=["gbf"])
                    s.op("dve", lambda E: E.tensor_tensor(Mtb[:, :], gbf[:, :],
                                                          CBt[:, :], ALU.mult),
                         reads=["gbf", "CBt"], writes=["Mtb"])
                    s.op("pe", lambda E, h=h: E.matmul(
                        psA[:, 128:256], esels[:, ts(h)], grow[0:16, :],
                        start=True, stop=True),
                        reads=["esels", "grow"], writes=["psA"])
                    s.op("dve", lambda E: E.tensor_tensor(
                        Ctl[:, :], Cb[:, :], psA[:, 128:256], ALU.mult),
                        reads=["Cb", "psA"], writes=["Ctl"])

                    def mmy(E, h=h, po=po, col=col, psY=psY):
                        E.matmul(psY[po:po + 64, col:col + 128],
                                 xtt[:, h * 64:(h + 1) * 64], Mtb[:, :],
                                 start=True, stop=False)
                        return E.matmul(psY[po:po + 64, col:col + 128],
                                        Ssb[:, h * 64:(h + 1) * 64], Ctl[:, :],
                                        start=False, stop=True)
                    s.op("pe", mmy, reads=[f"xtt:{h // 2}", "Mtb", "Ssb", "Ctl"],
                         writes=[psYk])
                    s.op("dve", lambda E, h=h: E.tensor_scalar(
                        xds[:, :], xtt[:, h * 64:(h + 1) * 64], dlt[:, h:h + 1],
                        None, ALU.mult),
                        reads=[f"xtt:{h // 2}", "dlt"], writes=["xds"])
                    s.op("pe", lambda E: E.matmul(psC[:, 448:512], Btb[:, :],
                                                  xds[:, :], start=True, stop=True),
                         reads=["Btb", "xds"], writes=["psC"])
                    s.op("dve", lambda E, h=h: E.scalar_tensor_tensor(
                        Sst[:, ts(h, 64)], Sst[:, ts(h, 64)], bsb[:, h:h + 1],
                        psC[:, 448:512], ALU.mult, ALU.add),
                        reads=["Sst", "bsb", "psC"], writes=["Sst"])
                for pt in range(PT):
                    psY = psY0 if pt < 4 else psY1
                    psYk = "psY0" if pt < 4 else "psY1"
                    col = (pt % 4) * 128
                    s.op("dve", lambda E, pt=pt, cc=cc, col=col, psY=psY:
                         E.scalar_tensor_tensor(
                             yc[:, sl(pt * 512, cc)], xbc[:, sl(pt * 512, cc)],
                             Dxc[:, pt:pt + 1], psY[:, col:col + 128],
                             ALU.mult, ALU.add),
                         reads=[f"xbc:{pt}", "Dxc", psYk], writes=[f"cum:{pt}"])
                s.op("dve", lambda E: E.tensor_copy(Ssb[:, :], Sst[:, :]),
                     reads=["Sst"], writes=["Ssb"])
            for pt in range(PT):
                s.op("act", lambda E, pt=pt: E.activation(
                    szt[:, :], zc[:, ts(pt, 512)], AF.Silu),
                    reads=[f"zc:{pt}"], writes=["szt"])
                s.op("dve", lambda E, pt=pt: E.tensor_tensor(
                    yc[:, ts(pt, 512)], yc[:, ts(pt, 512)], szt[:, :], ALU.mult),
                    reads=[f"cum:{pt}", "szt"], writes=[f"cum:{pt}"])
                s.op("act", lambda E, pt=pt: E.activation(
                    sqb[:, :], yc[:, ts(pt, 512)], AF.Square),
                    reads=[f"cum:{pt}"], writes=["sqb"])
                s.op("pe", lambda E, pt=pt: E.matmul(
                    psB[0:1, 0:512], p._cst["ones_col_bf"][:, 0:1], sqb[:, :],
                    start=(pt == 0), stop=(pt == PT - 1)),
                    reads=["sqb", "c:ones_col_bf"], writes=["psB"])
            s.op("dve", lambda E: E.tensor_scalar(
                rrow[0:1, 0:512], psB[0:1, 0:512], 1.0 / D, None, ALU.mult),
                reads=["psB"], writes=["rrow"])
            s.op("act", lambda E: E.activation(
                rrow[0:1, 512:1024], rrow[0:1, 0:512], AF.Sqrt,
                bias=p.constv(1e-5)[0:1, 0:1]),
                reads=["rrow", "cpool"], writes=["rrow2"])
            s.op("dve", lambda E: E.reciprocal(rrow[0:1, 0:512], rrow[0:1, 512:1024]),
                 reads=["rrow2"], writes=["rrow"])
            s.op("pe", lambda E: E.matmul(psA[:, 0:512], onesr[0:1, :],
                                          rrow[0:1, 0:512], start=True, stop=True),
                 reads=["rrow", "onesr"], writes=["psA"])
            s.op("dve", lambda E: E.tensor_copy(rmsbc[:, :], psA[:, 0:512]),
                 reads=["psA"], writes=["rmsbc"])
            for pt in range(PT):
                s.op("dve", lambda E, pt=pt: E.scalar_tensor_tensor(
                    xnf[:, ts(pt, 512)], yc[:, ts(pt, 512)], nwcol[:, pt:pt + 1],
                    rmsbc[:, :], ALU.mult, ALU.mult),
                    reads=[f"cum:{pt}", "nwcol", "rmsbc"], writes=[f"xnf:{pt}"])
            for mt in range(PT):
                def mmo(E, mt=mt):
                    last = None
                    for kt in range(PT):
                        last = E.matmul(psR[:, 0:512], woT[:, sl(kt * 1024, mt)],
                                        xnf[:, ts(kt, 512)],
                                        start=(kt == 0), stop=(kt == PT - 1))
                    return last
                s.op("pe", mmo, reads=["woT"] + [f"xnf:{k}" for k in range(PT)],
                     writes=["psR"])
                s.op("dve", lambda E, mt=mt: E.tensor_copy(yall[:, ts(mt, 512)],
                                                           psR[:, 0:512]),
                     reads=["psR"], writes=[f"xnf32:{mt}"])
            emit_group_residual(p, b, c5, yall, xin, xout, psA, "psA", (rx, ry),
                                ykey="xnf32")
    if fused:
        p.s.barrier()
    _sc.__exit__(None, None, None)
    return p





def build_moe(p=None, io=None, W=None):
    fused = p is not None
    if not fused:
        p = Prog("moe")
    _sc = p.scope() if fused else contextlib.nullcontext()
    _sc.__enter__()
    s = p.s
    p._cst = load_consts(p, ["ident_bf", "ident_f32"])
    xin = io[0] if fused else p.dram_in("xin", shape=[B_CORE, T, D], dtype=F32)
    xout = io[1] if fused else p.dram_out("xout", [B_CORE, T, D], F32)
    def gw(name, shape):
        return W[name] if W else p.dram_in(name, shape=shape, dtype=F32)
    Ws1 = gw("Ws1", [D, 4096])
    bs1 = gw("bs1", [1, 4096])
    Ws2 = gw("Ws2", [2048, D])
    bs2 = gw("bs2", [1, D])
    We1 = gw("We1", [12, D, 6144])
    be1 = gw("be1", [12, 6144])
    We2 = gw("We2", [12, 3072, D])
    be2 = gw("be2", [12, D])
    Wr = gw("Wr", [D, 12])
    br = gw("br", [1, 12])
    esel = np.zeros((16, 16 * 128), np.float32)
    for h in range(16):
        esel[h, h * 128:(h + 1) * 128] = 1.0
    esel_h = p.dram_in("esel", esel, dtype=F32)
    onesr_h = p.dram_in("onesr", np.ones((1, 128), np.float32), dtype=F32)

    lnt = alloc_lnt(p)
    xnf = p.sb("xnf", [128, PT * 512], BF16)
    xnf32 = p.sb("xnf32", [128, PT * 512], F32)
    actb = p.sb("actb", [128, 24 * 512], BF16)
    wA = p.sb("wA", [128, PT * 512], BF16)
    wG = p.sb("wG", [128, PT * 512], BF16)
    w2b = p.sb("w2b", [128, 24 * 128], BF16)
    wr32 = p.sb("wr32", [128, PT * 12], F32)
    logt = p.sb("logt", [128, 4 * 12], F32)
    lwk = p.sb("lwk", [128, 12], F32)
    oh1 = p.sb("oh1", [128, 12], F32)
    oh2 = p.sb("oh2", [128, 12], F32)
    cs2 = p.sb("cs2", [128, 12], F32)
    m12 = p.sb("m12", [128, 4], F32)
    wtm = p.sb("wtm", [128, 4 * 12], F32)
    wfm = p.sb("wfm", [12, 512], F32)
    webc = p.sb("webc", [128, 512], F32)
    glb = p.sb("glb", [128, 512], BF16)
    t1b = p.sb("t1b", [128, 512], F32)
    yacc = p.sb("yacc", [128, PT * 512], F32)
    bscol = p.sb("bscol", [128, 32 + 8 + 8], F32)
    be1c = p.sb("be1c", [128, 12 * 48], F32)
    be2c = p.sb("be2c", [128, 12 * 8], F32)
    brbc = p.sb("brbc", [128, 12], F32)
    rowb = p.sb("rowb", [48, 128], F32)
    esels = p.sb("esels", [16, 16 * 128], F32)
    onesr = p.sb("onesr_s", [1, 128], F32)
    rx = p.sb("rx", [128, D], F32)
    ry = p.sb("ry", [128, D], F32)

    psT = p.psum("psT", [128, 1024], BF16)
    psA = p.psum("psA", [128, 512], F32)
    psB = p.psum("psB", [128, 512], F32)
    psR = p.psum("psR", [128, 512], F32)
    psC = p.psum("psC", [128, 512], F32)

    s.dma("sp", lambda E: E.dma_start(esels[:, :], esel_h[:, :]), writes=["esels"])
    s.dma("sp", lambda E: E.dma_start(onesr[:, :], onesr_h[:, :]), writes=["onesr"])
    src_wr = Wr.rearrange("(kt p) c -> p kt c", p=128)
    s.dma("sp", lambda E, src=src_wr: E.dma_start(
        bass.AP(wr32, 0, [[wr32.shape[1], 128], [12, PT], [1, 12]]), src),
        writes=["wr32"])

    def coltab(hbm_row_ap, nrow, dst, dstk, col0):
        s.dma("sp", lambda E: E.dma_start(
            rowb[0:nrow, :], hbm_row_ap.rearrange("o (r x) -> (o r) x", x=128)),
            writes=["rowb"])
        s.op("pe", lambda E: E.transpose(psC[0:128, 0:nrow], rowb[0:nrow, 0:128],
                                         p._cst["ident_f32"][0:nrow, 0:nrow]),
             reads=["rowb", "c:ident_f32"], writes=["psC"])
        s.op("dve", lambda E: E.tensor_copy(dst[:, col0:col0 + nrow],
                                            psC[0:128, 0:nrow]),
             reads=["psC"], writes=[dstk])

    coltab(bs1, 32, bscol, "bscol", 0)
    coltab(bs2, 8, bscol, "bscol", 32)
    for e in range(12):
        coltab(be1[e:e + 1, :], 48, be1c, "be1c", e * 48)
        coltab(be2[e:e + 1, :], 8, be2c, "be2c", e * 8)
    s.dma("sp", lambda E: E.dma_start(rowb[0:1, 0:12], br[:, :]), writes=["rowb"])
    s.op("pe", lambda E: E.matmul(psC[:, 0:12], onesr[0:1, :], rowb[0:1, 0:12],
                                  start=True, stop=True),
         reads=["rowb", "onesr"], writes=["psC"])
    s.op("dve", lambda E: E.tensor_copy(brbc[:, :], psC[:, 0:12]),
         reads=["psC"], writes=["brbc"])

    def ldw_cols(hbm, col0, buf, bk, kt_n=PT, ncols=512):
        src = hbm[:, col0:col0 + ncols].rearrange("(kt p) c -> p kt c", p=128)
        dst = bass.AP(buf, 0, [[buf.shape[1], 128], [ncols, kt_n], [1, ncols]])
        s.dma("pool", lambda E, src=src, dst=dst: E.dma_start(dst, src), writes=[bk])

    def ldw_lhs(hbm, mt, buf, bk, kt_n):
        src = hbm[:, ts(mt)].rearrange("(kt p) c -> p kt c", p=128)
        dst = bass.AP(buf, 0, [[buf.shape[1], 128], [128, kt_n], [1, 128]])
        s.dma("pool", lambda E, src=src, dst=dst: E.dma_start(dst, src), writes=[bk])

    NCHU = B_CORE * T // 512
    for c in range(NCHU):
        b = c // (T // 512)
        g = c % (T // 512)
        for j4 in range(4):
            tt = g * 4 + j4

            def writer(j, j4=j4):
                return (xnf[:, sl(j * 512, j4)], f"xnf:{j}")
            emit_lnt_tile(p, lnt, xin[b, ts(tt), :], psT, "psT", writer)
            s.op("pool", lambda E: E.tensor_scalar(
                lnt["xt"][:, :], lnt["xt"][:, :], lnt["st"][:, 2:3],
                lnt["st"][:, 7:8], ALU.subtract, ALU.mult),
                reads=["lnt_xt", "lnt_mean_msq", "lnt_rstd"], writes=["lnt_xt"])
            for w2 in range(2):
                def mmT32(E, w2=w2):
                    last = None
                    for j in range(4):
                        last = E.transpose(psA[:, ts(j)],
                                           lnt["xt"][:, sl(w2 * 512, j)],
                                           p._cst["ident_f32"][:, :])
                    return last
                s.op("pe", mmT32, reads=["lnt_xt", "c:ident_f32"], writes=["psA"])
                for j in range(4):
                    jj = w2 * 4 + j
                    s.op("dve", lambda E, jj=jj, j=j, j4=j4: E.tensor_copy(
                        xnf32[:, sl(jj * 512, j4)], psA[:, ts(j)]),
                        reads=["psA"], writes=[f"xnf32:{jj}"])
        for j4 in range(4):
            def mmr(E, j4=j4):
                last = None
                for kt in range(PT):
                    last = E.matmul(psB[:, 0:12], xnf32[:, sl(kt * 512, j4)],
                                    wr32[:, ts(kt, 12)],
                                    start=(kt == 0), stop=(kt == PT - 1))
                return last
            s.op("pe", mmr, reads=["wr32"] + [f"xnf32:{k}" for k in range(PT)],
                 writes=["psB"])
            s.op("dve", lambda E, j4=j4: E.tensor_tensor(
                logt[:, ts(j4, 12)], psB[:, 0:12], brbc[:, :], ALU.add),
                reads=["psB", "brbc"], writes=[f"logt:{j4}"])
            s.op("dve", lambda E, j4=j4: E.tensor_copy(lwk[:, :], logt[:, ts(j4, 12)]),
                 reads=[f"logt:{j4}"], writes=["lwk"])
            s.op("dve", lambda E, j4=j4: E.tensor_reduce(
                m12[:, 0:1], lwk[:, :], AX.X, ALU.max),
                reads=["lwk"], writes=["m12"])
            s.op("dve", lambda E: E.tensor_scalar(
                oh1[:, :], lwk[:, :], m12[:, 0:1], None, ALU.is_equal),
                reads=["lwk", "m12"], writes=["oh1"])
            s.op("dve", lambda E: E.tensor_tensor_scan(
                cs2[:, :], oh1[:, :], oh1[:, :], 0.0, ALU.add, ALU.bypass),
                reads=["oh1"], writes=["cs2"])
            s.op("dve", lambda E: E.tensor_scalar(
                cs2[:, :], cs2[:, :], 1.0, None, ALU.is_equal),
                reads=["cs2"], writes=["cs2"])
            s.op("dve", lambda E: E.tensor_tensor(
                oh1[:, :], oh1[:, :], cs2[:, :], ALU.mult),
                reads=["oh1", "cs2"], writes=["oh1"])
            s.op("dve", lambda E: E.scalar_tensor_tensor(
                lwk[:, :], oh1[:, :], NEG, lwk[:, :], ALU.mult, ALU.add),
                reads=["oh1", "lwk"], writes=["lwk"])
            s.op("dve", lambda E: E.tensor_reduce(
                m12[:, 1:2], lwk[:, :], AX.X, ALU.max),
                reads=["lwk"], writes=["m12"])
            s.op("dve", lambda E: E.tensor_scalar(
                oh2[:, :], lwk[:, :], m12[:, 1:2], None, ALU.is_equal),
                reads=["lwk", "m12"], writes=["oh2"])
            s.op("dve", lambda E: E.tensor_tensor_scan(
                cs2[:, :], oh2[:, :], oh2[:, :], 0.0, ALU.add, ALU.bypass),
                reads=["oh2"], writes=["cs2"])
            s.op("dve", lambda E: E.tensor_scalar(
                cs2[:, :], cs2[:, :], 1.0, None, ALU.is_equal),
                reads=["cs2"], writes=["cs2"])
            s.op("dve", lambda E: E.tensor_tensor(
                oh2[:, :], oh2[:, :], cs2[:, :], ALU.mult),
                reads=["oh2", "cs2"], writes=["oh2"])
            s.op("dve", lambda E: E.tensor_tensor(
                m12[:, 2:3], m12[:, 0:1], m12[:, 1:2], ALU.subtract),
                reads=["m12"], writes=["m12"])
            s.op("act", lambda E: E.activation(
                m12[:, 0:1], m12[:, 2:3], AF.Sigmoid, scale=1.0 / 1.5),
                reads=["m12"], writes=["m12"])
            s.op("act", lambda E: E.activation(
                m12[:, 1:2], m12[:, 2:3], AF.Sigmoid, scale=-1.0 / 1.5),
                reads=["m12"], writes=["m12"])
            s.op("dve", lambda E, j4=j4: E.tensor_scalar(
                wtm[:, ts(j4, 12)], oh1[:, :], m12[:, 0:1], None, ALU.mult),
                reads=["oh1", "m12"], writes=[f"wtm:{j4}"])
            s.op("dve", lambda E, j4=j4: E.scalar_tensor_tensor(
                wtm[:, ts(j4, 12)], oh2[:, :], m12[:, 1:2], wtm[:, ts(j4, 12)],
                ALU.mult, ALU.add),
                reads=["oh2", "m12", f"wtm:{j4}"], writes=[f"wtm:{j4}"])
            s.op("pe", lambda E, j4=j4: E.transpose(
                psB[0:12, 128:256], wtm[:, ts(j4, 12)], p._cst["ident_f32"][:, :]),
                reads=[f"wtm:{j4}", "c:ident_f32"], writes=["psB"])
            s.op("dve", lambda E, j4=j4: E.tensor_copy(
                wfm[0:12, ts(j4)], psB[0:12, 128:256]),
                reads=["psB"], writes=["wfm"])
        for mt2 in range(4):
            ldw_cols(Ws1, mt2 * 512, wA, "wA")
            ldw_cols(Ws1, 2048 + mt2 * 512, wG, "wG")
            for m4 in range(4):
                mtg = mt2 * 4 + m4
                def mma(E, m4=m4):
                    last = None
                    for kt in range(PT):
                        last = E.matmul(psA[:, 0:512], wA[:, sl(kt * 512, m4)],
                                        xnf[:, ts(kt, 512)],
                                        start=(kt == 0), stop=(kt == PT - 1))
                    return last
                def mmg(E, m4=m4):
                    last = None
                    for kt in range(PT):
                        last = E.matmul(psB[:, 0:512], wG[:, sl(kt * 512, m4)],
                                        xnf[:, ts(kt, 512)],
                                        start=(kt == 0), stop=(kt == PT - 1))
                    return last
                s.op("pe", mma, reads=["wA"] + [f"xnf:{k}" for k in range(PT)],
                     writes=["psA"])
                s.op("pe", mmg, reads=["wG"] + [f"xnf:{k}" for k in range(PT)],
                     writes=["psB"])
                s.op("act", lambda E, mtg=mtg: E.activation(
                    glb[:, :], psB[:, 0:512], AF.Gelu,
                    bias=bscol[:, 16 + mtg:17 + mtg]),
                    reads=["psB", "bscol"], writes=["glb"])
                s.op("dve", lambda E, mtg=mtg: E.scalar_tensor_tensor(
                    actb[:, ts(mtg, 512)], psA[:, 0:512], bscol[:, mtg:mtg + 1],
                    glb[:, :], ALU.add, ALU.mult),
                    reads=["psA", "bscol", "glb"], writes=[f"actb:{mtg}"])
        for mt in range(PT):
            ldw_lhs(Ws2, mt, w2b, "w2b", 16)
            def mms2(E, mt=mt):
                last = None
                for kt in range(16):
                    last = E.matmul(psR[:, 0:512], w2b[:, ts(kt)],
                                    actb[:, ts(kt, 512)],
                                    start=(kt == 0), stop=(kt == 15))
                return last
            s.op("pe", mms2, reads=["w2b"] + [f"actb:{k}" for k in range(16)],
                 writes=["psR"])
            s.op("dve", lambda E, mt=mt: E.tensor_scalar(
                yacc[:, ts(mt, 512)], psR[:, 0:512], bscol[:, 32 + mt:33 + mt],
                None, ALU.add),
                reads=["psR", "bscol"], writes=[f"yacc:{mt}"])
        for e in range(12):
            s.op("pe", lambda E, e=e: E.matmul(
                psC[:, 0:512], esels[0:12, ts(e)], wfm[0:12, :],
                start=True, stop=True),
                reads=["esels", "wfm"], writes=["psC"])
            s.op("dve", lambda E: E.tensor_copy(webc[:, :], psC[:, 0:512]),
                 reads=["psC"], writes=["webc"])
            for j in range(6):
                ldw_cols(We1[e], j * 512, wA, "wA")
                ldw_cols(We1[e], 3072 + j * 512, wG, "wG")
                for m4 in range(4):
                    mtg = j * 4 + m4
                    def mma(E, m4=m4):
                        last = None
                        for kt in range(PT):
                            last = E.matmul(psA[:, 0:512], wA[:, sl(kt * 512, m4)],
                                            xnf[:, ts(kt, 512)],
                                            start=(kt == 0), stop=(kt == PT - 1))
                        return last
                    def mmg(E, m4=m4):
                        last = None
                        for kt in range(PT):
                            last = E.matmul(psB[:, 0:512], wG[:, sl(kt * 512, m4)],
                                            xnf[:, ts(kt, 512)],
                                            start=(kt == 0), stop=(kt == PT - 1))
                        return last
                    s.op("pe", mma, reads=["wA"] + [f"xnf:{k}" for k in range(PT)],
                         writes=["psA"])
                    s.op("pe", mmg, reads=["wG"] + [f"xnf:{k}" for k in range(PT)],
                         writes=["psB"])
                    s.op("act", lambda E, e=e, mtg=mtg: E.activation(
                        glb[:, :], psB[:, 0:512], AF.Gelu,
                        bias=be1c[:, e * 48 + 24 + mtg:e * 48 + 25 + mtg]),
                        reads=["psB", "be1c"], writes=["glb"])
                    s.op("dve", lambda E, e=e, mtg=mtg: E.scalar_tensor_tensor(
                        actb[:, ts(mtg, 512)], psA[:, 0:512],
                        be1c[:, e * 48 + mtg:e * 48 + mtg + 1],
                        glb[:, :], ALU.add, ALU.mult),
                        reads=["psA", "be1c", "glb"], writes=[f"actb:{mtg}"])
            for mt in range(PT):
                ldw_lhs(We2[e], mt, w2b, "w2b", 24)
                def mme2(E, mt=mt):
                    last = None
                    for kt in range(24):
                        last = E.matmul(psR[:, 0:512], w2b[:, ts(kt)],
                                        actb[:, ts(kt, 512)],
                                        start=(kt == 0), stop=(kt == 23))
                    return last
                s.op("pe", mme2, reads=["w2b"] + [f"actb:{k}" for k in range(24)],
                     writes=["psR"])
                s.op("dve", lambda E, e=e, mt=mt: E.scalar_tensor_tensor(
                    t1b[:, :], psR[:, 0:512], be2c[:, e * 8 + mt:e * 8 + mt + 1],
                    webc[:, :], ALU.add, ALU.mult),
                    reads=["psR", "be2c", "webc"], writes=["t1b"])
                s.op("dve", lambda E, mt=mt: E.tensor_tensor(
                    yacc[:, ts(mt, 512)], yacc[:, ts(mt, 512)], t1b[:, :], ALU.add),
                    reads=[f"yacc:{mt}", "t1b"], writes=[f"yacc:{mt}"])
        emit_group_residual(p, b, g, yacc, xin, xout, psA, "psA", (rx, ry),
                            ykey="yacc")
    if fused:
        p.s.barrier()
    _sc.__exit__(None, None, None)
    return p





def build_attn_b1(p=None, io=None, W=None):
    fused = p is not None
    if not fused:
        p = Prog("attnb1")
    _sc = p.scope() if fused else contextlib.nullcontext()
    _sc.__enter__()
    s = p.s
    p._cst = load_consts(p, ["ident_bf", "ident_f32", "cmask4_bf"])
    xin = io[0] if fused else p.dram_in("xin", shape=[B_CORE, T, D], dtype=F32)
    xout = io[1] if fused else p.dram_out("xout", [B_CORE, T, D], F32)
    def gw(name, shape):
        return W[name] if W else p.dram_in(name, shape=shape, dtype=F32)
    WqkvT = gw("WqkvT", [D, 3 * D])
    bqkv = gw("bqkv", [1, 3 * D])
    WqaT = gw("WqaT", [D, D])
    bqa = gw("bqa", [1, D])
    WkaT = gw("WkaT", [D, D])
    bka = gw("bka", [1, D])
    WdT = gw("WdT", [D, D])
    bd = gw("bd", [1, D])
    RD = 16
    inv = 1.0 / (10000.0 ** (np.arange(0, RD, 2, dtype=np.float64) / RD))
    tpos = np.arange(T, dtype=np.float64)
    fr = np.outer(inv, tpos)
    emb = np.concatenate([fr, fr], axis=0)
    cosT_h = p.dram_in("cosT", np.cos(emb).astype(np.float32), dtype=F32)
    sinT_h = p.dram_in("sinT", np.sin(emb).astype(np.float32), dtype=F32)
    Rm = np.zeros((128, 128), np.float32)
    for base in (0, 64):
        for i in range(8):
            Rm[base + i + 8, base + i] = -1.0
            Rm[base + i, base + i + 8] = 1.0
    R_h = p.dram_in("Rm", Rm, dtype=F32)
    i_ = np.arange(128)[:, None]
    k_ = np.arange(256)[None, :]
    wm = np.where((k_ >= 113 + i_) & (k_ <= 128 + i_), 0.0, NEG).astype(np.float32)
    wmask_h = p.dram_in("wmask", wm, dtype=F32)
    k0 = np.arange(128)[None, :]
    wm0 = np.where((k0 <= i_) & (k0 >= i_ - 15), 0.0, NEG).astype(np.float32)
    wmask0_h = p.dram_in("wmask0", wm0, dtype=F32)

    SCALE = 1.0 / 8.0
    lnt = alloc_lnt(p)
    xnfB = p.sb("xnfB", [128, PT * T], BF16)
    KfB = p.sb("KfB", [128, PT * T], BF16)
    VtB = p.sb("VtB", [128, NT * D], BF16)
    QaB = p.sb("QaB", [128, PT * T], BF16)
    wb = p.sb("wb", [128, PT * D], BF16)
    Rbf = p.sb("Rbf", [128, 128], BF16)
    vbbc = p.sb("vbbc", [128, D], BF16)
    bqkv_bf = p.sb("bqkv_bf", [1, D], BF16)
    btab = p.sb("btab", [128, 40], F32)
    rowb = p.sb("rowb", [8, 128], F32)
    wmsk = p.sb("wmsk", [128, 256], F32)
    wmsk0 = p.sb("wmsk0", [128, 128], F32)
    Ssb = p.sb("Ssb", [128, T], F32)
    rpq = Ssb[0:16, 0:512]
    rpr = Ssb[0:16, 512:1024]
    u1b = Ssb[0:16, 1024:1536]
    u2b = Ssb[0:16, 1536:2048]
    Pbf = p.sb("Pbf", [128, T], BF16)
    PTsb = p.sb("PTsb", [128, 512], BF16)
    Ot = p.sb("Ot", [128, 128], BF16)
    Of = p.sb("Of", [128, PT * 512], BF16)
    yall = p.sb("yall", [128, PT * 512], F32)
    cosS = yall[0:16, 0:T]
    sinS = yall[0:16, T:2 * T]
    rx = p.sb("rx", [128, D], F32)
    ry = p.sb("ry", [128, D], F32)
    mx = p.sb("mx", [128, 8], F32)
    den = p.sb("den", [128, 8], F32)
    rcp = p.sb("rcp", [128, 16], F32)
    nmx = p.sb("nmx", [128, 1], F32)
    onebf = p.sb("onebf", [1, 512], BF16)

    psS = [p.psum("psS0", [128, 512], F32), p.psum("psS1", [128, 512], F32)]
    psO = p.psum("psO", [128, 1024], F32)
    psT = p.psum("psT", [128, 1024], BF16)
    psP = [p.psum("psP0", [128, 512], F32), p.psum("psP1", [128, 512], F32)]
    psR = p.psum("psR", [128, 512], F32)

    s.dma("pool", lambda E: E.dma_start(Rbf[:, :], R_h[:, :]), writes=["Rbf"])
    s.dma("sp", lambda E: E.dma_start(wmsk[:, :], wmask_h[:, :]), writes=["wmsk"])
    s.dma("sp", lambda E: E.dma_start(wmsk0[:, :], wmask0_h[:, :]), writes=["wmsk0"])
    s.dma("pool", lambda E: E.dma_start(bqkv_bf[:, :], bqkv[:, 2 * D:3 * D]), writes=["bqkv_bf"])

    def coltab(hbm_row_ap, col0):
        s.dma("sp", lambda E, hbm_row_ap=hbm_row_ap: E.dma_start(
            rowb[0:8, :], hbm_row_ap.rearrange("o (r x) -> (o r) x", x=128)),
            writes=["rowb"])
        s.op("pe", lambda E: E.transpose(psR[0:128, 0:8], rowb[0:8, 0:128],
                                         p._cst["ident_f32"][0:8, 0:8]),
             reads=["rowb", "c:ident_f32"], writes=["psR"])
        s.op("dve", lambda E, col0=col0: E.tensor_copy(btab[:, col0:col0 + 8],
                                                       psR[0:128, 0:8]),
             reads=["psR"], writes=["btab"])

    coltab(bqkv[:, 0:D], 0)
    coltab(bqkv[:, D:2 * D], 8)
    coltab(bqa, 16)
    coltab(bka, 24)
    coltab(bd, 32)
    s.op("dve", lambda E: E.memset(onebf[:, :], 1.0), writes=["onebf"])

    def vb_mm(E):
        E.matmul(psO[:, 0:512], onebf[0:1, 0:128], bqkv_bf[0:1, 0:512],
                 start=True, stop=True)
        return E.matmul(psO[:, 512:1024], onebf[0:1, 0:128],
                        bqkv_bf[0:1, 512:1024], start=True, stop=True)
    s.op("pe", vb_mm, reads=["onebf", "bqkv_bf"], writes=["psO"])

    def vb_ev(E):
        E.tensor_copy(vbbc[:, 0:512], psO[:, 0:512])
        return E.tensor_copy(vbbc[:, 512:1024], psO[:, 512:1024])
    s.op("dve", vb_ev, reads=["psO"], writes=["vbbc"])

    def ldw_cols(hbm, col0, buf, bk, ncols=D):
        src = hbm[:, col0:col0 + ncols].rearrange("(kt p) c -> p kt c", p=128)
        dst = bass.AP(buf, 0, [[buf.shape[1], 128], [ncols, PT], [1, ncols]])
        s.dma("pool", lambda E, src=src, dst=dst: E.dma_start(dst, src), writes=[bk])

    def rope_chunk(buf, keyf, t0, c):
        for pt in range(PT):
            ap = buf[:, pt * T + c * 512:pt * T + (c + 1) * 512]
            key = keyf(pt)
            s.op("pe", lambda E, ap=ap: E.matmul(
                psS[1][:, 0:512], Rbf[:, :], ap, start=True, stop=True),
                reads=[key, "Rbf"], writes=["psS1"])
            for base in (0, 64):
                s.op("dve", lambda E, ap=ap, base=base: E.tensor_copy(
                    rpq, ap[base:base + 16, :]),
                    reads=[key], writes=["rpq", "Ssb:0"])
                s.op("act", lambda E, base=base: E.activation(
                    rpr, psS[1][base:base + 16, 0:512], AF.Copy),
                    reads=["psS1"], writes=["rpr", "Ssb:1"])
                s.op("dve", lambda E, t0=t0: E.tensor_tensor(
                    u1b, rpq, yall[0:16, t0:t0 + 512], ALU.mult),
                    reads=["rpq", "sh16"], writes=["u1b", "Ssb:2"])
                s.op("dve", lambda E, t0=t0: E.tensor_tensor(
                    u2b, rpr, yall[0:16, T + t0:T + t0 + 512], ALU.mult),
                    reads=["rpr", "sh16"], writes=["u2b", "Ssb:3"])
                s.op("dve", lambda E, ap=ap, base=base: E.tensor_tensor(
                    ap[base:base + 16, :], u1b, u2b, ALU.add),
                    reads=["u1b", "u2b"], writes=[key])

    NG = T // 512
    for b in range(B_CORE):
        s.dma("sp", lambda E: E.dma_start(cosS, cosT_h[:, :]),
              writes=["sh16"] + [f"yall:{m}" for m in range(4)])
        s.dma("sp", lambda E: E.dma_start(sinS, sinT_h[:, :]),
              writes=["sh16"] + [f"yall:{m}" for m in range(4, 8)])
        for tt in range(NT):
            def writer(j, tt=tt):
                return (xnfB[:, sl(j * T, tt)], f"xnf:{j}")
            emit_lnt_tile(p, lnt, xin[b, ts(tt), :], psT, "psT", writer)
        for (col0, dstbuf, keyf, bcol0) in (
                (D, KfB, (lambda mt: f"Kf:{mt}"), 8),
                (0, QaB, (lambda mt: f"qa:{mt}"), 0)):
            ldw_cols(WqkvT, col0, wb, "wb")
            for c in range(NG):
                for mt in range(PT):
                    def mmK(E, c=c, mt=mt):
                        last = None
                        for kt in range(PT):
                            last = E.matmul(psP[mt % 2][:, 0:512],
                                            wb[:, sl(kt * D, mt)],
                                            xnfB[:, kt * T + c * 512:
                                                 kt * T + (c + 1) * 512],
                                            start=(kt == 0), stop=(kt == PT - 1))
                        return last
                    s.op("pe", mmK, reads=["wb"] + [f"xnf:{k}" for k in range(PT)],
                         writes=[f"psP{mt % 2}"])
                    s.op("dve", lambda E, c=c, mt=mt, dstbuf=dstbuf, keyf=keyf,
                         bcol0=bcol0: E.tensor_scalar(
                             dstbuf[:, mt * T + c * 512:mt * T + (c + 1) * 512],
                             psP[mt % 2][:, 0:512], btab[:, bcol0 + mt:bcol0 + mt + 1],
                             None, ALU.add),
                         reads=[f"psP{mt % 2}", "btab"], writes=[keyf(mt)])
                rope_chunk(dstbuf, keyf, c * 512, c)
        ldw_cols(WqkvT, 2 * D, wb, "wb")
        for tt in range(NT):
            for nh in range(2):
                def mmV(E, tt=tt, nh=nh):
                    last = None
                    for kt in range(PT):
                        last = E.matmul(psP[nh][:, 0:512],
                                        xnfB[:, sl(kt * T, tt)],
                                        wb[:, kt * D + nh * 512:kt * D + (nh + 1) * 512],
                                        start=(kt == 0), stop=(kt == PT - 1))
                    return last
                s.op("pe", mmV, reads=["wb"] + [f"xnf:{k}" for k in range(PT)],
                     writes=[f"psP{nh}"])
                s.op("dve", lambda E, tt=tt, nh=nh: E.tensor_tensor(
                    VtB[:, tt * D + nh * 512:tt * D + (nh + 1) * 512],
                    psP[nh][:, 0:512], vbbc[:, nh * 512:(nh + 1) * 512], ALU.add),
                    reads=[f"psP{nh}", "vbbc"], writes=[f"Vt:{tt}"])
        for tt in range(NT):
            first = (tt == 0)
            spanw = 128 if first else 256
            for h in range(16):
                fp0 = h // 2
                po = (h % 2) * 64

                def mmS(E, tt=tt, fp0=fp0, po=po, first=first, spanw=spanw):
                    k0 = (0 if first else (tt - 1) * 128)
                    return E.matmul(psS[0][:, 0:spanw],
                                    QaB[po:po + 64, fp0 * T + tt * 128:
                                        fp0 * T + (tt + 1) * 128],
                                    KfB[po:po + 64, fp0 * T + k0:fp0 * T + k0 + spanw],
                                    start=True, stop=True)
                s.op("pe", mmS, reads=[f"qa:{fp0}", f"Kf:{fp0}"], writes=["psS0"])
                s.op("dve", lambda E, first=first, spanw=spanw: E.scalar_tensor_tensor(
                    Ssb[:, 0:spanw], psS[0][:, 0:spanw], SCALE,
                    (wmsk0 if first else wmsk)[:, 0:spanw], ALU.mult, ALU.add),
                    reads=["psS0", "wmsk", "wmsk0"], writes=["Ssbw"])
                s.op("dve", lambda E, spanw=spanw: E.tensor_reduce(
                    nmx[:, 0:1], Ssb[:, 0:spanw], AX.X, ALU.max, negate=True),
                    reads=["Ssbw"], writes=["nmx"])
                s.op("act", lambda E, spanw=spanw: E.activation(
                    Pbf[:, 0:spanw], Ssb[:, 0:spanw], AF.Exp, bias=nmx[:, 0:1],
                    accum_out=den[:, 0:1]),
                    reads=["Ssbw", "nmx"], writes=["Pbfw", "den"])
                s.op("dve", lambda E, h=h: E.reciprocal(rcp[:, h:h + 1], den[:, 0:1]),
                     reads=["den"], writes=["rcp"])

                def mmT(E, spanw=spanw):
                    last = E.transpose(psT[:, 0:128], Pbf[:, 0:128],
                                       p._cst["ident_bf"][:, :])
                    if spanw > 128:
                        last = E.transpose(psT[:, 128:256], Pbf[:, 128:256],
                                           p._cst["ident_bf"][:, :])
                    return last
                s.op("pe", mmT, reads=["Pbfw", "c:ident_bf"], writes=["psT"])
                s.op("act", lambda E, spanw=spanw: E.activation(
                    PTsb[:, 0:spanw], psT[:, 0:spanw], AF.Copy),
                    reads=["psT"], writes=["PTsb"])

                def mmPV(E, tt=tt, h=h, first=first):
                    if first:
                        return E.matmul(psO[:, h * 64:(h + 1) * 64], PTsb[:, 0:128],
                                        VtB[:, tt * D + h * 64:tt * D + (h + 1) * 64],
                                        start=True, stop=True)
                    E.matmul(psO[:, h * 64:(h + 1) * 64], PTsb[:, 0:128],
                             VtB[:, (tt - 1) * D + h * 64:(tt - 1) * D + (h + 1) * 64],
                             start=True, stop=False)
                    return E.matmul(psO[:, h * 64:(h + 1) * 64], PTsb[:, 128:256],
                                    VtB[:, tt * D + h * 64:tt * D + (h + 1) * 64],
                                    start=False, stop=True)
                vreads = [f"Vt:{tt}"] if first else [f"Vt:{tt - 1}", f"Vt:{tt}"]
                s.op("pe", mmPV, reads=["PTsb"] + vreads, writes=["psO"])
            for h in range(16):
                s.op("dve", lambda E, tt=tt, h=h: E.tensor_scalar(
                    xnfB[:, tt * D + h * 64:tt * D + (h + 1) * 64],
                    psO[:, h * 64:(h + 1) * 64], rcp[:, h:h + 1], None, ALU.mult),
                    reads=["psO", "rcp"], writes=[f"xnf:{tt // 2}"])
        for tt in range(NT):
            def mmTA(E, tt=tt):
                last = None
                for j in range(PT):
                    last = E.transpose(psT[:, ts(j)], xnfB[:, sl(tt * D, j)],
                                       p._cst["ident_bf"][:, :])
                return last
            s.op("pe", mmTA, reads=[f"xnf:{tt // 2}", "c:ident_bf"], writes=["psT"])
            for j in range(PT):
                s.op("act", lambda E, tt=tt, j=j: E.activation(
                    KfB[:, sl(j * T, tt)], psT[:, ts(j)], AF.Copy),
                    reads=["psT"], writes=[f"Kf:{j}"])
        for (wsrc, bcol0, dstbuf, dkeys) in (
                (WqaT, 16, QaB, lambda mt: [f"qa:{mt}"]),
                (WkaT, 24, VtB, lambda mt: [f"Vt:{2 * mt}", f"Vt:{2 * mt + 1}"])):
            ldw_cols(wsrc, 0, wb, "wb")
            for c in range(NG):
                for mt in range(PT):
                    def mmA(E, c=c, mt=mt):
                        last = None
                        for kt in range(PT):
                            last = E.matmul(psP[mt % 2][:, 0:512],
                                            wb[:, sl(kt * D, mt)],
                                            KfB[:, kt * T + c * 512:
                                                kt * T + (c + 1) * 512],
                                            start=(kt == 0), stop=(kt == PT - 1))
                        return last
                    s.op("pe", mmA, reads=["wb"] + [f"Kf:{k}" for k in range(PT)],
                         writes=[f"psP{mt % 2}"])
                    s.op("dve", lambda E, c=c, mt=mt, dstbuf=dstbuf, bcol0=bcol0:
                         E.tensor_scalar(
                             dstbuf[:, mt * T + c * 512:mt * T + (c + 1) * 512],
                             psP[mt % 2][:, 0:512],
                             btab[:, bcol0 + mt:bcol0 + mt + 1], None, ALU.add),
                         reads=[f"psP{mt % 2}", "btab"], writes=dkeys(mt))
        ldw_cols(WdT, 0, wb, "wb")
        for g in range(NG):
            for qi4 in range(4):
                qi = g * 4 + qi4
                nkc = qi // 4 + 1
                for h in range(16):
                    fp0 = h // 2
                    po = (h % 2) * 64
                    for c in range(nkc):
                        diag = (c == nkc - 1)

                        def mmS2(E, qi=qi, c=c, fp0=fp0, po=po):
                            return E.matmul(
                                psS[c % 2][:, 0:512],
                                QaB[po:po + 64, sl(fp0 * T, qi)],
                                VtB[po:po + 64, fp0 * T + c * 512:
                                    fp0 * T + (c + 1) * 512],
                                start=True, stop=True)
                        s.op("pe", mmS2,
                             reads=[f"qa:{fp0}", f"Vt:{2 * fp0}", f"Vt:{2 * fp0 + 1}"],
                             writes=[f"psS{c % 2}"])
                        if diag:
                            s.op("dve", lambda E, c=c, qi4=qi4: E.scalar_tensor_tensor(
                                Ssb[:, c * 512:(c + 1) * 512], psS[c % 2][:, 0:512],
                                SCALE,
                                p._cst["cmask4_bf"][:, qi4 * 512:(qi4 + 1) * 512],
                                ALU.mult, ALU.add),
                                reads=[f"psS{c % 2}", "c:cmask4_bf"], writes=[f"Ssb:{c}"])
                            s.op("dve", lambda E, c=c: E.tensor_reduce(
                                mx[:, c:c + 1], Ssb[:, c * 512:(c + 1) * 512], AX.X,
                                ALU.max),
                                reads=[f"Ssb:{c}"], writes=[f"mx:{c}"])
                        else:
                            s.op("dve", lambda E, c=c: E.tensor_scalar(
                                Ssb[:, c * 512:(c + 1) * 512], psS[c % 2][:, 0:512],
                                SCALE, None, ALU.mult, ALU.max,
                                accum_out=mx[:, c:c + 1]),
                                reads=[f"psS{c % 2}"], writes=[f"Ssb:{c}", f"mx:{c}"])
                    s.op("dve", lambda E, nkc=nkc: E.tensor_reduce(
                        nmx[:, 0:1], mx[:, 0:nkc], AX.X, ALU.max, negate=True),
                        reads=[f"mx:{c}" for c in range(nkc)], writes=["nmx"])
                    for c in range(nkc):
                        s.op("act", lambda E, c=c: E.activation(
                            Pbf[:, c * 512:(c + 1) * 512],
                            Ssb[:, c * 512:(c + 1) * 512],
                            AF.Exp, bias=nmx[:, 0:1], accum_out=den[:, c:c + 1]),
                            reads=[f"Ssb:{c}", "nmx"], writes=[f"Pbf:{c}", f"den:{c}"])
                    s.op("dve", lambda E, nkc=nkc: E.tensor_reduce(
                        den[:, 6:7], den[:, 0:nkc], AX.X, ALU.add),
                        reads=[f"den:{c}" for c in range(nkc)], writes=["densum"])
                    s.op("dve", lambda E: E.reciprocal(rcp[:, 0:1], den[:, 6:7]),
                         reads=["densum"], writes=["rcp"])
                    for c in range(nkc):
                        def mmT2(E, c=c):
                            last = None
                            for j in range(4):
                                last = E.transpose(psT[:, ts(j)],
                                                   Pbf[:, sl(c * 512, j)],
                                                   p._cst["ident_bf"][:, :])
                            return last
                        s.op("pe", mmT2, reads=[f"Pbf:{c}", "c:ident_bf"],
                             writes=["psT"])
                        s.op("act", lambda E: E.activation(PTsb[:, 0:512],
                                                           psT[:, 0:512], AF.Copy),
                             reads=["psT"], writes=["PTsb"])

                        def mmPV2(E, c=c, h=h, nkc=nkc):
                            last = None
                            for j in range(4):
                                tt2 = c * 4 + j
                                st = (c == 0 and j == 0)
                                sp = (c == nkc - 1 and j == 3)
                                last = E.matmul(psO[:, h * 64:(h + 1) * 64],
                                                PTsb[:, ts(j)],
                                                xnfB[:, tt2 * D + h * 64:
                                                     tt2 * D + (h + 1) * 64],
                                                start=st, stop=sp)
                            return last
                        s.op("pe", mmPV2,
                             reads=["PTsb"] + [f"xnf:{(c * 4 + j) // 2}"
                                               for j in range(4)],
                             writes=["psO"])
                    s.op("dve", lambda E, h=h: E.tensor_scalar(
                        Ot[:, (h % 2) * 64:(h % 2) * 64 + 64],
                        psO[:, h * 64:(h + 1) * 64], rcp[:, 0:1], None, ALU.mult),
                        reads=["psO", "rcp"], writes=["Ot128"])
                    if h % 2 == 1:
                        fp = h // 2
                        s.op("pe", lambda E: E.transpose(
                            psT[:, 0:128], Ot[:, :], p._cst["ident_bf"][:, :]),
                            reads=["Ot128", "c:ident_bf"], writes=["psT"])
                        s.op("act", lambda E, qi4=qi4, fp=fp: E.activation(
                            Of[:, sl(fp * 512, qi4)], psT[:, 0:128], AF.Copy),
                            reads=["psT"], writes=[f"Of:{qi4}:{fp}"])
            for mt in range(PT):
                def mmproj(E, mt=mt):
                    last = None
                    for kt in range(PT):
                        last = E.matmul(psR[:, 0:512],
                                        wb[:, sl(kt * D, mt)],
                                        Of[:, kt * 512:(kt + 1) * 512],
                                        start=(kt == 0), stop=(kt == PT - 1))
                    return last
                s.op("pe", mmproj,
                     reads=["wb"] + [f"Of:{q4}:{kt}" for q4 in range(4)
                                     for kt in range(PT)],
                     writes=["psR"])
                s.op("dve", lambda E, mt=mt: E.tensor_scalar(
                    yall[:, ts(mt, 512)], psR[:, 0:512], btab[:, 32 + mt:33 + mt],
                    None, ALU.add),
                    reads=["psR", "btab"], writes=[f"yall:{mt}", "sh16"])
            emit_group_residual(p, b, g, yall, xin, xout, psS[0], "psS0", (rx, ry))
    if fused:
        p.s.barrier()
    _sc.__exit__(None, None, None)
    return p


_PROGS = {}


def _prog(name, builder):
    if name not in _PROGS:
        _PROGS[name] = builder()
        _PROGS[name].build()
    return _PROGS[name]


def _tr(a):
    return np.ascontiguousarray(np.asarray(a).T)





MOE_SLOTS = 14336
NCHK = MOE_SLOTS // 512


def build_conv(p, W):
    with p.scope():
        s = p.s
        cv = [p.sb(f"cv{i}", [128, 4096], BF16) for i in range(4)]
        jobs = [
            (W["We1g"].rearrange("e j p c -> (e j p) c"), W["We1b"], 4096, 144),
            (W["We2g"], W["We2b"], 1024, 288),
            (W["Ws1"], W["Ws1b"], 4096, 8),
            (W["Ws2"], W["Ws2b"], 1024, 16),
        ]
        i = 0
        for src, dst, nc_, nt_ in jobs:
            for r in range(nt_):
                buf = cv[i % 4]
                key = f"cv{i % 4}"
                s.dma("pool", lambda E, buf=buf, src=src, r=r, nc_=nc_: E.dma_start(
                    buf[:, 0:nc_], src[r * 128:(r + 1) * 128, :]), writes=[key])
                s.dma(("sp", "act")[i % 2],
                      lambda E, buf=buf, dst=dst, r=r, nc_=nc_: E.dma_start(
                          dst[r * 128:(r + 1) * 128, :], buf[:, 0:nc_]),
                      reads=[key])
                i += 1
        s.barrier()


def build_moe_sparse(p=None, io=None, W=None):
    fused = p is not None
    if not fused:
        p = Prog("moes")
    _sc = p.scope() if fused else contextlib.nullcontext()
    _sc.__enter__()
    s = p.s
    p._cst = load_consts(p, ["ident_bf", "ident_f32", "ones_col_bf"])
    xin = io[0] if fused else p.dram_in("xin", shape=[B_CORE, T, D], dtype=F32)
    xout = io[1] if fused else p.dram_out("xout", [B_CORE, T, D], F32)

    def gw(name, shape):
        return W[name] if W else p.dram_in(name, shape=shape, dtype=F32)
    bs1 = gw("bs1", [1, 4096])
    bs2 = gw("bs2", [1, D])
    be1 = gw("be1", [12, 6144])
    be2 = gw("be2", [12, D])
    Wr = gw("Wr", [D, 12])
    br = gw("br", [1, 12])
    if W:
        Ws1b, Ws2b = W["Ws1b"], W["Ws2b"]
        We1b, We2b = W["We1b"], W["We2b"]
    else:
        Ws1b = p.dram_in("Ws1b", shape=[D, 4096], dtype=BF16)
        Ws2b = p.dram_in("Ws2b", shape=[2048, D], dtype=BF16)
        We1b = p.dram_in("We1b", shape=[12 * 12 * 128, 4096], dtype=BF16)
        We2b = p.dram_in("We2b", shape=[12 * 128 * 24, D], dtype=BF16)

    NTOK = B_CORE * T
    ltri = np.triu(np.ones((128, 128), np.float32), 1)
    ltri_h = p.dram_in("ltri", ltri.astype(BF), dtype=BF16)
    iota1 = np.arange(128, dtype=np.float32).reshape(128, 1)
    iota1_h = p.dram_in("iota1", iota1, dtype=F32)
    c512_h = p.dram_in("c512", (512.0 * np.arange(NCHK, dtype=np.float32)
                                ).reshape(1, NCHK), dtype=F32)
    onesr_h = p.dram_in("onesr", np.ones((1, 128), np.float32), dtype=F32)
    i24k_np = (24.0 * np.arange(128, dtype=np.float32)[:, None]
               + np.arange(25, dtype=np.float32)[None, :])
    i24k_h = p.dram_in("i24k", i24k_np, dtype=F32)
    jp128_np = (np.arange(128, dtype=np.float32)[:, None]
                + 128.0 * np.arange(12, dtype=np.float32)[None, :])
    jp128_h = p.dram_in("jp128", jp128_np, dtype=F32)
    e8 = np.zeros((8, 8 * 128), np.float32)
    for pp in range(8):
        e8[pp, pp * 128:(pp + 1) * 128] = 1.0
    esel8_h = p.dram_in("esel8", e8, dtype=F32)

    tg = p.tag
    xntm_d = p.nc.dram_tensor(tg + "xntm", [NTOK, D], BF16, kind="Internal").ap()
    xmid_d = p.nc.dram_tensor(tg + "xmid", [B_CORE, T, D], F32, kind="Internal").ap()
    stok_d = p.nc.dram_tensor(tg + "stok", [MOE_SLOTS, 1], U32, kind="Internal").ap()
    ye_d = p.nc.dram_tensor(tg + "ye", [MOE_SLOTS, D], F32, kind="Internal").ap()
    cnt_d = p.nc.dram_tensor(tg + "cntd", [12, 32], F32, kind="Internal").ap()
    pad_d = p.nc.dram_tensor(tg + "padd", [12, 1], F32, kind="Internal").ap()
    segb_d = p.nc.dram_tensor(tg + "segbd", [1, 12], F32, kind="Internal").ap()
    ibase_d = p.nc.dram_tensor(tg + "ibased", [32, 12], F32, kind="Internal").ap()

    lnt = alloc_lnt(p)
    xnf = p.sb("xnf", [128, PT * 512], BF16)
    xnf32 = p.sb("xnf32", [128, PT * 512], F32)
    actb = p.sb("actb", [128, 24 * 512], BF16)
    wA = [p.sb(f"wA{i}", [128, PT * 512], BF16) for i in range(3)]
    wG = [p.sb(f"wG{i}", [128, PT * 512], BF16) for i in range(3)]
    w2b = [p.sb("w2b0", [128, 24 * 128], BF16), p.sb("w2b1", [128, 24 * 128], BF16)]
    w2r = p.sb("w2r", [128, 8 * 512], BF16)
    wr32 = p.sb("wr32", [128, PT * 12], F32)
    logt = p.sb("logt", [128, 12], F32)
    lwk = p.sb("lwk", [128, 12], F32)
    oh1 = p.sb("oh1", [128, 12], F32)
    oh2 = p.sb("oh2", [128, 12], F32)
    cs2 = p.sb("cs2", [128, 12], F32)
    m12 = p.sb("m12", [128, 4], F32)
    fst12 = p.sb("fst12", [128, 32 * 24], F32)
    w12 = p.sb("w12", [128, 64], F32)
    slotsU = p.sb("slotsU", [128, 64], U32)
    ohbf = p.sb("ohbf", [128, 12], BF16)
    ibrow = p.sb("ibrow", [1, 12 * 32], F32)
    ibr12 = p.sb("ibr12", [1, 12], F32)
    cnts = p.sb("cnts", [12, 32], F32)
    cntx = p.sb("cntx", [12, 32], F32)
    tot = p.sb("tot", [12, 4], F32)
    totu = p.sb("totu", [12, 1], U32)
    segrow = p.sb("segrow", [1, 12], F32)
    segcol = p.sb("segcol", [12, 1], F32)
    chrow = p.sb("chrow", [1, NCHK], F32)
    chbc = p.sb("chbc", [128, NCHK], F32)
    chu = p.sb("chu", [1, NCHK], U32)
    cmp12 = p.sb("cmp12", [12, NCHK], BF16)
    posb = p.sb("posb", [128, 12], F32)
    selt = p.sb("selt", [128, 12], F32)
    scol = p.sb("scol", [128, 2], F32)
    tidq = p.sb("tidq", [128, 4], U32)
    idq1 = p.sb("idq1", [128, 3], U32)
    idqw = p.sb("idqw", [128, 12], U32)
    idqwf = p.sb("idqwf", [128, 12], F32)
    jpc = p.sb("jpc", [128, 12], F32)
    idq24 = p.sb("idq24", [128, 24], U32)
    idq24f = p.sb("idq24f", [128, 24], F32)
    be2g = p.sb("be2g", [8, 128], F32)
    i24k = p.sb("i24k_s", [128, 25], F32)
    esel8 = p.sb("esel8_s", [8, 8 * 128], F32)
    xg = [p.sb("xg0", [128, D], BF16), p.sb("xg1", [128, D], BF16)]
    g1 = p.sb("g1", [128, D], F32)
    g2 = p.sb("g2", [128, D], F32)
    xmt = p.sb("xmt", [128, D], F32)
    ytmp = p.sb("ytmp", [128, 2 * 512], F32)
    glb = [p.sb(f"glb{i}", [128, 512], BF16) for i in range(3)]
    yacc = p.sb("yacc", [128, PT * 512], F32)
    bscol = p.sb("bscol", [128, 48], F32)
    be1r = p.sb("be1r", [48, 128], F32)
    be1c = p.sb("be1c", [128, 48], F32)
    be2r = p.sb("be2r", [1, D], F32)
    be2bc = p.sb("be2bc", [128, D], F32)
    brbc = p.sb("brbc", [128, 12], F32)
    rowb = p.sb("rowb", [48, 128], F32)
    ltris = p.sb("ltris", [128, 128], BF16)
    iotas = p.sb("iotas", [128, 1], F32)
    c512s = p.sb("c512s", [1, NCHK], F32)
    onesr = p.sb("onesr_s", [1, 128], F32)
    dumm = p.sb("dumm", [128, MOE_SLOTS // 128], U32)
    rx = p.sb("rx", [128, D], F32)
    ry = p.sb("ry", [128, D], F32)

    psA = p.psum("psA", [128, 512], F32)
    psB = p.psum("psB", [128, 512], F32)
    psR = p.psum("psR", [128, 512], F32)
    psT = p.psum("psT", [128, 1024], BF16)
    psY = [p.psum(f"psY{j}", [128, 512], F32) for j in range(4)]

    s.dma("sp", lambda E: E.dma_start(ltris[:, :], ltri_h[:, :]), writes=["ltris"])
    s.dma("sp", lambda E: E.dma_start(iotas[:, :], iota1_h[:, :]), writes=["iotas"])
    s.dma("sp", lambda E: E.dma_start(c512s[:, :], c512_h[:, :]), writes=["c512s"])
    s.dma("sp", lambda E: E.dma_start(onesr[:, :], onesr_h[:, :]), writes=["onesr"])
    s.dma("sp", lambda E: E.dma_start(i24k[:, :], i24k_h[:, :]), writes=["i24k"])
    s.dma("sp", lambda E: E.dma_start(jpc[:, :], jp128_h[:, :]), writes=["jpc"])
    s.dma("sp", lambda E: E.dma_start(esel8[:, :], esel8_h[:, :]), writes=["esel8"])
    src_wr = Wr.rearrange("(kt p) c -> p kt c", p=128)
    s.dma("sp", lambda E, src=src_wr: E.dma_start(
        bass.AP(wr32, 0, [[wr32.shape[1], 128], [12, PT], [1, 12]]), src),
        writes=["wr32"])

    def coltab(hbm_row_ap, nrow, dst, dstk, col0):
        s.dma("sp", lambda E, hbm_row_ap=hbm_row_ap, nrow=nrow: E.dma_start(
            rowb[0:nrow, :], hbm_row_ap.rearrange("o (r x) -> (o r) x", x=128)),
            writes=["rowb"])
        s.op("pe", lambda E, nrow=nrow: E.transpose(
            psR[0:128, 0:nrow], rowb[0:nrow, 0:128],
            p._cst["ident_f32"][0:nrow, 0:nrow]),
            reads=["rowb", "c:ident_f32"], writes=["psR"])
        s.op("dve", lambda E, col0=col0, nrow=nrow: E.tensor_copy(
            dst[:, col0:col0 + nrow], psR[0:128, 0:nrow]),
            reads=["psR"], writes=[dstk])

    coltab(bs1, 32, bscol, "bscol", 0)
    coltab(bs2, 8, bscol, "bscol", 32)
    s.dma("sp", lambda E: E.dma_start(rowb[0:1, 0:12], br[:, :]), writes=["rowb"])
    s.op("pe", lambda E: E.matmul(psR[:, 0:12], onesr[0:1, :], rowb[0:1, 0:12],
                                  start=True, stop=True),
         reads=["rowb", "onesr"], writes=["psR"])
    s.op("dve", lambda E: E.tensor_copy(brbc[:, :], psR[:, 0:12]),
         reads=["psR"], writes=["brbc"])
    s.op("dve", lambda E: E.memset(dumm[:, :], float(NTOK - 1)), writes=["dumm"])
    s.dma("sp", lambda E: E.dma_start(
        stok_d.rearrange("(q j) o -> q j", q=128), dumm[:, :]),
        reads=["dumm"], writes=["stok"])

    def ldw_cols(hbm, col0, buf, bk, ncols=512, qeng="pool"):
        src = hbm[:, col0:col0 + ncols].rearrange("(kt p) c -> p kt c", p=128)
        dst = bass.AP(buf, 0, [[buf.shape[1], 128], [ncols, PT], [1, ncols]])
        s.dma(qeng, lambda E, src=src, dst=dst: E.dma_start(dst, src), writes=[bk])

    def ldw_lhs(hbm, mt, buf, bk, kt_n, qeng="pool"):
        src = hbm[:, ts(mt)].rearrange("(kt p) c -> p kt c", p=128)
        dst = bass.AP(buf, 0, [[buf.shape[1], 128], [128, kt_n], [1, 128]])
        s.dma(qeng, lambda E, src=src, dst=dst: E.dma_start(dst, src), writes=[bk])

    NG8 = NTOK // 512

    for c in range(NG8):
        b = c // (T // 512)
        g = c % (T // 512)
        for j4 in range(4):
            tt = g * 4 + j4
            ti = c * 4 + j4

            def writer(j, j4=j4):
                return (xnf[:, sl(j * 512, j4)], f"xnf:{j}")
            emit_lnt_tile(p, lnt, xin[b, ts(tt), :], psT, "psT", writer)
            s.dma("sp", lambda E, ti=ti: E.dma_start(
                xntm_d[ti * 128:(ti + 1) * 128, :], lnt["xn_tm"][:, :]),
                reads=["lnt_xn"], writes=["xntm"])
            s.op("pool", lambda E: E.tensor_scalar(
                lnt["xt"][:, :], lnt["xt"][:, :], lnt["st"][:, 2:3],
                lnt["st"][:, 7:8], ALU.subtract, ALU.mult),
                reads=["lnt_xt", "lnt_mean_msq", "lnt_rstd"], writes=["lnt_xt"])
            for w2_ in range(2):
                def mmT32(E, w2_=w2_):
                    last = None
                    for j in range(4):
                        last = E.transpose(psA[:, ts(j)],
                                           lnt["xt"][:, sl(w2_ * 512, j)],
                                           p._cst["ident_f32"][:, :])
                    return last
                s.op("pe", mmT32, reads=["lnt_xt", "c:ident_f32"], writes=["psA"])
                for j in range(4):
                    jj = w2_ * 4 + j
                    s.op("dve", lambda E, jj=jj, j=j, j4=j4: E.tensor_copy(
                        xnf32[:, sl(jj * 512, j4)], psA[:, ts(j)]),
                        reads=["psA"], writes=[f"xnf32:{jj}"])
        for j4 in range(4):
            ti = c * 4 + j4

            def mmr(E, j4=j4):
                last = None
                for kt in range(PT):
                    last = E.matmul(psB[:, 0:12], xnf32[:, sl(kt * 512, j4)],
                                    wr32[:, ts(kt, 12)],
                                    start=(kt == 0), stop=(kt == PT - 1))
                return last
            s.op("pe", mmr, reads=["wr32"] + [f"xnf32:{k}" for k in range(PT)],
                 writes=["psB"])
            s.op("dve", lambda E: E.tensor_tensor(
                logt[:, :], psB[:, 0:12], brbc[:, :], ALU.add),
                reads=["psB", "brbc"], writes=["logt"])
            s.op("dve", lambda E: E.tensor_copy(lwk[:, :], logt[:, :]),
                 reads=["logt"], writes=["lwk"])
            for kk in range(2):
                ohx = oh1 if kk == 0 else oh2
                ohk = "oh1" if kk == 0 else "oh2"
                s.op("dve", lambda E, kk=kk: E.tensor_reduce(
                    m12[:, kk:kk + 1], lwk[:, :], AX.X, ALU.max),
                    reads=["lwk"], writes=["m12"])
                s.op("dve", lambda E, ohx=ohx, kk=kk: E.tensor_scalar(
                    ohx[:, :], lwk[:, :], m12[:, kk:kk + 1], None, ALU.is_equal),
                    reads=["lwk", "m12"], writes=[ohk])
                s.op("dve", lambda E, ohx=ohx: E.tensor_tensor_scan(
                    cs2[:, :], ohx[:, :], ohx[:, :], 0.0, ALU.add, ALU.bypass),
                    reads=[ohk], writes=["cs2"])
                s.op("dve", lambda E: E.tensor_scalar(
                    cs2[:, :], cs2[:, :], 1.0, None, ALU.is_equal),
                    reads=["cs2"], writes=["cs2"])
                s.op("dve", lambda E, ohx=ohx, ohk=ohk: E.tensor_tensor(
                    ohx[:, :], ohx[:, :], cs2[:, :], ALU.mult),
                    reads=[ohk, "cs2"], writes=[ohk])
                if kk == 0:
                    s.op("dve", lambda E: E.scalar_tensor_tensor(
                        lwk[:, :], oh1[:, :], NEG, lwk[:, :], ALU.mult, ALU.add),
                        reads=["oh1", "lwk"], writes=["lwk"])
            s.op("dve", lambda E: E.tensor_tensor(
                m12[:, 2:3], m12[:, 0:1], m12[:, 1:2], ALU.subtract),
                reads=["m12"], writes=["m12"])
            s.op("act", lambda E, ti=ti: E.activation(
                w12[:, 2 * ti:2 * ti + 1], m12[:, 2:3], AF.Sigmoid, scale=1.0 / 1.5),
                reads=["m12"], writes=["w12"])
            s.op("act", lambda E, ti=ti: E.activation(
                w12[:, 2 * ti + 1:2 * ti + 2], m12[:, 2:3], AF.Sigmoid,
                scale=-1.0 / 1.5),
                reads=["m12"], writes=["w12"])
            s.op("dve", lambda E, ti=ti: E.tensor_copy(
                fst12[:, ti * 24:ti * 24 + 12], oh1[:, :]),
                reads=["oh1"], writes=["fst12"])
            s.op("dve", lambda E, ti=ti: E.tensor_copy(
                fst12[:, ti * 24 + 12:ti * 24 + 24], oh2[:, :]),
                reads=["oh2"], writes=["fst12"])
            s.op("dve", lambda E: E.tensor_tensor(
                ohbf[:, :], oh1[:, :], oh2[:, :], ALU.add),
                reads=["oh1", "oh2"], writes=["ohbf"])
            s.op("pe", lambda E: E.matmul(psB[0:1, 128:140],
                                          p._cst["ones_col_bf"][:, 0:1],
                                          ohbf[:, :], start=True, stop=True),
                 reads=["ohbf", "c:ones_col_bf"], writes=["psB"])
            s.op("dve", lambda E, ti=ti: E.tensor_copy(
                bass.AP(ibrow, ti, [[ibrow.shape[1], 1], [32, 12]]),
                psB[0:1, 128:140]),
                reads=["psB"], writes=["ibrow"])
        for mt2 in range(4):
            db = mt2 % 3
            wAx, wGx, ka, kg = wA[db], wG[db], f"wA{db}", f"wG{db}"
            ldw_cols(Ws1b, mt2 * 512, wAx, ka, qeng=("pool", "sp")[mt2 % 2])
            ldw_cols(Ws1b, 2048 + mt2 * 512, wGx, kg, qeng=("act", "pool")[mt2 % 2])
            for m4 in range(4):
                mtg = mt2 * 4 + m4
                pm = m4 % 3
                pbA, kA = [(psA, "psA"), (psR, "psR"), (psY[2], "psY2")][pm]
                pbG, kG = [(psB, "psB"), (psY[0], "psY0"), (psY[1], "psY1")][pm]
                glx, kgl = glb[pm], f"glb{pm}"

                def mma(E, m4=m4, wAx=wAx, pbA=pbA):
                    last = None
                    for kt in range(PT):
                        last = E.matmul(pbA[:, 0:512], wAx[:, sl(kt * 512, m4)],
                                        xnf[:, ts(kt, 512)],
                                        start=(kt == 0), stop=(kt == PT - 1))
                    return last

                def mmg(E, m4=m4, wGx=wGx, pbG=pbG):
                    last = None
                    for kt in range(PT):
                        last = E.matmul(pbG[:, 0:512], wGx[:, sl(kt * 512, m4)],
                                        xnf[:, ts(kt, 512)],
                                        start=(kt == 0), stop=(kt == PT - 1))
                    return last
                s.op("pe", mma, reads=[ka] + [f"xnf:{k}" for k in range(PT)],
                     writes=[kA])
                s.op("pe", mmg, reads=[kg] + [f"xnf:{k}" for k in range(PT)],
                     writes=[kG])
                s.op("act", lambda E, mtg=mtg, pbG=pbG, glx=glx: E.activation(
                    glx[:, :], pbG[:, 0:512], AF.Gelu,
                    bias=bscol[:, 16 + mtg:17 + mtg]),
                    reads=[kG, "bscol"], writes=[kgl])
                s.op("dve", lambda E, mtg=mtg, pbA=pbA, glx=glx:
                     E.scalar_tensor_tensor(
                         actb[:, ts(mtg, 512)], pbA[:, 0:512],
                         bscol[:, mtg:mtg + 1], glx[:, :], ALU.add, ALU.mult),
                     reads=[kA, "bscol", kgl], writes=[f"actb:{mtg}"])
        for mt in range(PT):
            db2 = mt % 2
            w2x, k2 = w2b[db2], f"w2b{db2}"
            pbS, kS = (psR, "psR") if db2 == 0 else (psY[1], "psY1")
            ldw_lhs(Ws2b, mt, w2x, k2, 16, qeng=("pool", "sp", "act")[mt % 3])

            def mms2(E, mt=mt, w2x=w2x, pbS=pbS):
                last = None
                for kt in range(16):
                    last = E.matmul(pbS[:, 0:512], w2x[:, ts(kt)],
                                    actb[:, ts(kt, 512)],
                                    start=(kt == 0), stop=(kt == 15))
                return last
            s.op("pe", mms2, reads=[k2] + [f"actb:{k}" for k in range(16)],
                 writes=[kS])
            s.op("dve", lambda E, mt=mt, pbS=pbS: E.tensor_scalar(
                yacc[:, ts(mt, 512)], pbS[:, 0:512], bscol[:, 32 + mt:33 + mt],
                None, ALU.add),
                reads=[kS, "bscol"], writes=[f"yacc:{mt}"])
        emit_group_residual(p, b, g, yacc, xin, xmid_d, psA, "psA", (rx, ry),
                            ykey="yacc", out_key="xmidd")

    s.dma("sp", lambda E: E.dma_start(cnt_d.rearrange("(o e) i -> o (e i)", o=1),
                                      ibrow[:, :]), reads=["ibrow"], writes=["cntd"])
    s.dma("sp", lambda E: E.dma_start(cnts[:, :], cnt_d[:, :]),
          reads=["cntd"], writes=["cnts"])
    s.op("dve", lambda E: E.tensor_tensor_scan(
        cntx[:, :], cnts[:, :], cnts[:, :], 0.0, ALU.add, ALU.bypass),
        reads=["cnts"], writes=["cntx"])
    s.op("dve", lambda E: E.tensor_copy(tot[0:12, 0:1], cntx[:, 31:32]),
         reads=["cntx"], writes=["tot"])
    s.op("dve", lambda E: E.tensor_tensor(
        cntx[:, :], cntx[:, :], cnts[:, :], ALU.subtract),
        reads=["cntx", "cnts"], writes=["cntx"])
    s.op("pe", lambda E: E.matmul(psB[0:12, 0:NCHK], onesr[0:1, 0:12],
                                  c512s[0:1, :], start=True, stop=True),
         reads=["onesr", "c512s"], writes=["psB"])
    s.op("dve", lambda E: E.tensor_scalar(
        cmp12[0:12, :], psB[0:12, 0:NCHK], tot[0:12, 0:1], None, ALU.is_lt),
        reads=["psB", "tot"], writes=["cmp12"])
    s.op("dve", lambda E: E.tensor_reduce(
        tot[0:12, 3:4], cmp12[0:12, :], AX.X, ALU.add),
        reads=["cmp12"], writes=["tot"])
    s.op("dve", lambda E: E.tensor_scalar(
        tot[0:12, 1:2], tot[0:12, 3:4], 512.0, None, ALU.mult),
        reads=["tot"], writes=["tot"])
    s.dma("sp", lambda E: E.dma_start(pad_d[:, :], tot[0:12, 1:2]),
          reads=["tot"], writes=["padd"])
    s.dma("sp", lambda E: E.dma_start(
        segrow[:, :], pad_d.rearrange("e o -> o e")),
        reads=["padd"], writes=["segrow"])
    s.op("dve", lambda E: E.tensor_tensor_scan(
        segrow[:, :], segrow[:, :], segrow[:, :], 0.0, ALU.add, ALU.bypass),
        reads=["segrow"], writes=["segrow"])
    s.dma("sp", lambda E: E.dma_start(segb_d[:, :], segrow[:, :]),
          reads=["segrow"], writes=["segbd"])
    s.dma("sp", lambda E: E.dma_start(
        segcol[:, :], segb_d.rearrange("o e -> e o")),
        reads=["segbd"], writes=["segcol"])
    s.op("dve", lambda E: E.tensor_tensor(
        segcol[:, :], segcol[:, :], tot[0:12, 1:2], ALU.subtract),
        reads=["segcol", "tot"], writes=["segcol"])
    s.op("dve", lambda E: E.tensor_scalar(
        cntx[:, :], cntx[:, :], segcol[:, 0:1], None, ALU.add),
        reads=["cntx", "segcol"], writes=["cntx"])
    s.dma("sp", lambda E: E.dma_start(ibase_d.rearrange("i e -> e i"), cntx[:, :],
                                      allow_slow_non_contiguous=True),
          reads=["cntx"], writes=["ibased"])
    s.dma("sp", lambda E: E.dma_start(
        ibrow[:, :], ibase_d.rearrange("(o i) e -> o (i e)", o=1)),
        reads=["ibased"], writes=["ibrow"])
    s.op("pe", lambda E: E.matmul(psB[0:12, 0:NCHK], onesr[0:1, 0:12],
                                  c512s[0:1, :], start=True, stop=True),
         reads=["onesr", "c512s"], writes=["psB"])
    s.op("dve", lambda E: E.tensor_scalar(
        cmp12[0:12, :], psB[0:12, 0:NCHK], segcol[:, 0:1], None, ALU.is_ge),
        reads=["psB", "segcol"], writes=["cmp12"])
    s.op("pe", lambda E: E.matmul(psB[0:1, 256:256 + NCHK],
                                  p._cst["ones_col_bf"][0:12, 0:1],
                                  cmp12[0:12, :], start=True, stop=True),
         reads=["cmp12", "c:ones_col_bf"], writes=["psB"])
    s.op("dve", lambda E: E.tensor_scalar(
        chrow[:, :], psB[0:1, 256:256 + NCHK], -1.0, None, ALU.add),
        reads=["psB"], writes=["chrow"])
    s.op("dve", lambda E: E.tensor_copy(chu[:, :], chrow[:, :]),
         reads=["chrow"], writes=["chu"])
    s.op("pe", lambda E: E.matmul(psB[:, 0:NCHK], onesr[0:1, :], chrow[0:1, :],
                                  start=True, stop=True),
         reads=["chrow", "onesr"], writes=["psB"])
    s.op("dve", lambda E: E.tensor_copy(chbc[:, :], psB[:, 0:NCHK]),
         reads=["psB"], writes=["chbc"])
    for ti in range(NTOK // 128):
        s.op("dve", lambda E, ti=ti: E.tensor_tensor(
            posb[:, :], fst12[:, ti * 24:ti * 24 + 12],
            fst12[:, ti * 24 + 12:ti * 24 + 24], ALU.add),
            reads=["fst12"], writes=["posb"])
        s.op("dve", lambda E: E.tensor_copy(ohbf[:, :], posb[:, :]),
             reads=["posb"], writes=["ohbf"])
        s.op("dve", lambda E, ti=ti: E.tensor_copy(
            ibr12[0:1, :], ibrow[0:1, ti * 12:(ti + 1) * 12]),
            reads=["ibrow"], writes=["ibr12"])

        def mmpos(E, ti=ti):
            E.matmul(psB[:, 0:12], ltris[:, :], ohbf[:, :], start=True, stop=False)
            return E.matmul(psB[:, 0:12], onesr[0:1, :], ibr12[0:1, :],
                            start=False, stop=True)
        s.op("pe", mmpos, reads=["ltris", "ohbf", "onesr", "ibr12"],
             writes=["psB"])
        for kk in range(2):
            s.op("dve", lambda E, ti=ti, kk=kk: E.tensor_tensor(
                selt[:, :], fst12[:, ti * 24 + kk * 12:ti * 24 + kk * 12 + 12],
                psB[:, 0:12], ALU.mult),
                reads=["fst12", "psB"], writes=["selt"])
            s.op("dve", lambda E, kk=kk: E.tensor_reduce(
                scol[:, kk:kk + 1], selt[:, :], AX.X, ALU.add),
                reads=["selt"], writes=["scol"])
        s.op("dve", lambda E, ti=ti: E.tensor_copy(
            slotsU[:, 2 * ti:2 * ti + 2], scol[:, 0:2]),
            reads=["scol"], writes=["slotsU"])
        s.op("dve", lambda E, ti=ti: E.tensor_scalar(
            tidq[:, 0:1], iotas[:, 0:1], float(ti * 128), None, ALU.add),
            reads=["iotas"], writes=["tidq"])
        for kk in range(2):
            s.dma("pool", lambda E, ti=ti, kk=kk: E.indirect_dma_start(
                stok_d[:, :],
                bass.IndirectOffsetOnAxis(
                    slotsU[:, 2 * ti + kk:2 * ti + kk + 1], 0),
                tidq[:, 0:1], None),
                reads=["slotsU", "tidq"], writes=["stok"])

    be1_flat = be1.rearrange("e (r x) -> (e r) x", x=128)
    be2_flat = be2.rearrange("e (r x) -> (e r) x", x=128)
    for ck in range(NCHK):
        ckc = chbc[:, ck:ck + 1]
        s.op("dve", lambda E, ckc=ckc: E.tensor_scalar(
            selt[:, 0:1], ckc, 1536.0, None, ALU.mult),
            reads=["chbc"], writes=["selt"])
        s.op("dve", lambda E: E.tensor_scalar(
            idqwf[:, :], jpc[:, :], selt[:, 0:1], None, ALU.add),
            reads=["jpc", "selt"], writes=["idqwf"])
        s.op("dve", lambda E: E.tensor_copy(idqw[:, :], idqwf[:, :]),
             reads=["idqwf"], writes=["idqw12"])
        s.op("dve", lambda E, ckc=ckc: E.tensor_scalar(
            selt[:, 1:2], ckc, 3072.0, None, ALU.mult),
            reads=["chbc"], writes=["selt"])
        s.op("dve", lambda E: E.tensor_scalar(
            idq24f[:, :], i24k[:, 0:24], selt[:, 1:2], None, ALU.add),
            reads=["i24k", "selt"], writes=["idq24f"])
        s.op("dve", lambda E: E.tensor_copy(idq24[:, :], idq24f[:, :]),
             reads=["idq24f"], writes=["idq24"])
        s.op("dve", lambda E, ckc=ckc: E.scalar_tensor_tensor(
            selt[0:48, 2:3], ckc[0:48, 0:1], 48.0, iotas[0:48, 0:1],
            ALU.mult, ALU.add),
            reads=["chbc", "iotas"], writes=["selt"])
        s.op("dve", lambda E: E.tensor_copy(idq1[0:48, 1:2], selt[0:48, 2:3]),
             reads=["selt"], writes=["idq1"])
        s.op("dve", lambda E, ckc=ckc: E.scalar_tensor_tensor(
            selt[0:8, 3:4], ckc[0:8, 0:1], 8.0, iotas[0:8, 0:1],
            ALU.mult, ALU.add),
            reads=["chbc", "iotas"], writes=["selt"])
        s.op("dve", lambda E: E.tensor_copy(idq1[0:8, 2:3], selt[0:8, 3:4]),
             reads=["selt"], writes=["idq1"])
        s.dma("pool", lambda E: E.indirect_dma_start(
            be1r[:, :], None, be1_flat[:, :],
            bass.IndirectOffsetOnAxis(idq1[0:48, 1:2], 0)),
            reads=["idq1"], writes=["be1r"])
        s.op("pe", lambda E: E.transpose(psR[0:128, 0:48], be1r[:, :],
                                         p._cst["ident_f32"][0:48, 0:48]),
             reads=["be1r", "c:ident_f32"], writes=["psR"])
        s.op("dve", lambda E: E.tensor_copy(be1c[:, :], psR[0:128, 0:48]),
             reads=["psR"], writes=["be1c"])
        s.dma("pool", lambda E: E.indirect_dma_start(
            be2g[:, :], None, be2_flat[:, :],
            bass.IndirectOffsetOnAxis(idq1[0:8, 2:3], 0)),
            reads=["idq1"], writes=["be2g"])

        def mmbe2(E):
            last = None
            for pp in range(8):
                tgt = psR if pp < 4 else psB
                col = (pp % 4) * 128
                last = E.matmul(tgt[:, col:col + 128], esel8[0:8, ts(pp)],
                                be2g[0:8, :], start=True, stop=True)
            return last
        s.op("pe", mmbe2, reads=["be2g", "esel8"], writes=["psR", "psB"])

        def evbe2(E):
            E.tensor_copy(be2bc[:, 0:512], psR[:, 0:512])
            return E.tensor_copy(be2bc[:, 512:1024], psB[:, 0:512])
        s.op("dve", evbe2, reads=["psR", "psB"], writes=["be2bc"])
        for j4 in range(4):
            s.dma("sp", lambda E, ck=ck, j4=j4: E.dma_start(
                tidq[:, j4:j4 + 1],
                stok_d[ck * 512 + j4 * 128:ck * 512 + (j4 + 1) * 128, :]),
                writes=["tidq"], reads=["stok"])
        for j4 in range(4):
            gx, kx = xg[j4 % 2], f"xg{j4 % 2}"
            s.dma("pool", lambda E, j4=j4, gx=gx: E.indirect_dma_start(
                gx[:, :], None, xntm_d[:, :],
                bass.IndirectOffsetOnAxis(tidq[:, j4:j4 + 1], 0)),
                reads=["tidq", "xntm"], writes=[kx])

            def mmtg(E, gx=gx):
                last = None
                for j in range(PT):
                    last = E.transpose(psT[:, ts(j)], gx[:, ts(j)],
                                       p._cst["ident_bf"][:, :])
                return last
            s.op("pe", mmtg, reads=[kx, "c:ident_bf"], writes=["psT"])
            for j in range(PT):
                s.op("act", lambda E, j=j, j4=j4: E.activation(
                    xnf[:, j * 512 + j4 * 128:j * 512 + (j4 + 1) * 128],
                    psT[:, ts(j)], AF.Copy),
                    reads=["psT"], writes=[f"xnf:{j}"])
        for j6 in range(6):
            db = j6 % 3
            wAx, wGx, ka, kg = wA[db], wG[db], f"wA{db}", f"wG{db}"
            for part, buf, bk in ((0, wAx, ka), (1, wGx, kg)):
                jp = part * 6 + j6
                for half in range(2):
                    s.dma("pool",
                          lambda E, buf=buf, half=half, jp=jp:
                          E.indirect_dma_start(
                              bass.AP(buf, half * 2048,
                                      [[buf.shape[1], 128], [1, 2048]]),
                              None, We1b[:, :],
                              bass.IndirectOffsetOnAxis(idqw[:, jp:jp + 1], 0),
                              element_offset=half * 2048),
                          reads=["idqw12"], writes=[bk])
            for m4 in range(4):
                mtg = j6 * 4 + m4

                pm = m4 % 3
                pbA, kA = [(psA, "psA"), (psR, "psR"), (psY[2], "psY2")][pm]
                pbG, kG = [(psB, "psB"), (psY[0], "psY0"), (psY[1], "psY1")][pm]
                glx, kgl = glb[pm], f"glb{pm}"

                def mma(E, m4=m4, wAx=wAx, pbA=pbA):
                    last = None
                    for kt in range(PT):
                        last = E.matmul(pbA[:, 0:512], wAx[:, sl(kt * 512, m4)],
                                        xnf[:, ts(kt, 512)],
                                        start=(kt == 0), stop=(kt == PT - 1))
                    return last

                def mmg(E, m4=m4, wGx=wGx, pbG=pbG):
                    last = None
                    for kt in range(PT):
                        last = E.matmul(pbG[:, 0:512], wGx[:, sl(kt * 512, m4)],
                                        xnf[:, ts(kt, 512)],
                                        start=(kt == 0), stop=(kt == PT - 1))
                    return last
                s.op("pe", mma, reads=[ka] + [f"xnf:{k}" for k in range(PT)],
                     writes=[kA])
                s.op("pe", mmg, reads=[kg] + [f"xnf:{k}" for k in range(PT)],
                     writes=[kG])
                s.op("act", lambda E, mtg=mtg, pbG=pbG, glx=glx: E.activation(
                    glx[:, :], pbG[:, 0:512], AF.Gelu,
                    bias=be1c[:, 24 + mtg:25 + mtg]),
                    reads=[kG, "be1c"], writes=[kgl])
                s.op("dve", lambda E, mtg=mtg, pbA=pbA, glx=glx:
                     E.scalar_tensor_tensor(
                         actb[:, ts(mtg, 512)], pbA[:, 0:512],
                         be1c[:, mtg:mtg + 1], glx[:, :], ALU.add, ALU.mult),
                     reads=[kA, "be1c", kgl], writes=[f"actb:{mtg}"])
        for nh in range(2):
            for kt in range(24):
                wsl = kt % 8
                s.dma("pool",
                      lambda E, kt=kt, wsl=wsl, nh=nh: E.indirect_dma_start(
                          w2r[:, wsl * 512:(wsl + 1) * 512],
                          None, We2b[:, :],
                          bass.IndirectOffsetOnAxis(idq24[:, kt:kt + 1], 0),
                          element_offset=nh * 512),
                      reads=["idq24"], writes=[f"w2r{wsl}"])

                def mmy(E, kt=kt, nh=nh, wsl=wsl):
                    last = None
                    for sl4 in range(4):
                        last = E.matmul(psY[sl4][:, 0:512],
                                        actb[:, kt * 512 + sl4 * 128:
                                             kt * 512 + (sl4 + 1) * 128],
                                        w2r[:, wsl * 512:(wsl + 1) * 512],
                                        start=(kt == 0), stop=(kt == 23))
                    return last
                s.op("pe", mmy, reads=[f"w2r{wsl}"] + [f"actb:{kt}"],
                     writes=["psY0", "psY1", "psY2", "psY3"])
            for sl4 in range(4):
                s.op("dve", lambda E, sl4=sl4, nh=nh: E.tensor_tensor(
                    ytmp[:, (sl4 % 2) * 512:(sl4 % 2) * 512 + 512],
                    psY[sl4][:, 0:512], be2bc[:, nh * 512:(nh + 1) * 512],
                    ALU.add),
                    reads=[f"psY{sl4}", "be2bc"], writes=[f"ytmp{sl4 % 2}"])
                s.dma("sp", lambda E, ck=ck, sl4=sl4, nh=nh: E.dma_start(
                    ye_d[ck * 512 + sl4 * 128:ck * 512 + (sl4 + 1) * 128,
                         nh * 512:(nh + 1) * 512],
                    ytmp[:, (sl4 % 2) * 512:(sl4 % 2) * 512 + 512]),
                    reads=[f"ytmp{sl4 % 2}"], writes=["yed"])

    for ti in range(NTOK // 128):
        b = ti // (T // 128)
        tt = ti % (T // 128)
        pb = ti % 2
        xm = xmt[:, :] if pb == 0 else rx[:, :]
        ga = g1[:, :] if pb == 0 else ry[:, :]
        gb = g2[:, :] if pb == 0 else ytmp[:, 0:D]
        kx, k1, k2_ = f"xmt{pb}", f"g1{pb}", f"g2{pb}"
        wx = [kx] + (["res_rx"] if pb else [])
        w1 = [k1] + ([f"res_ry:{j}" for j in range(PT)] if pb else [])
        w2 = [k2_] + (["ytmp0", "ytmp1"] if pb else [])
        s.dma(("sp", "act")[pb], lambda E, b=b, tt=tt, xm=xm: E.dma_start(
            xm, xmid_d[b, ts(tt), :]), reads=["xmidd"], writes=wx)
        s.dma("pool", lambda E, ti=ti, ga=ga: E.indirect_dma_start(
            ga, None, ye_d[:, :],
            bass.IndirectOffsetOnAxis(slotsU[:, 2 * ti:2 * ti + 1], 0)),
            reads=["slotsU", "yed"], writes=w1)
        s.dma("pool", lambda E, ti=ti, gb=gb: E.indirect_dma_start(
            gb, None, ye_d[:, :],
            bass.IndirectOffsetOnAxis(slotsU[:, 2 * ti + 1:2 * ti + 2], 0)),
            reads=["slotsU", "yed"], writes=w2)
        s.op("dve", lambda E, ti=ti, xm=xm, ga=ga: E.scalar_tensor_tensor(
            xm, ga, w12[:, 2 * ti:2 * ti + 1], xm,
            ALU.mult, ALU.add),
            reads=[k1, "w12", kx], writes=[kx])
        s.op("dve", lambda E, ti=ti, xm=xm, gb=gb: E.scalar_tensor_tensor(
            xm, gb, w12[:, 2 * ti + 1:2 * ti + 2], xm,
            ALU.mult, ALU.add),
            reads=[k2_, "w12", kx], writes=[kx])
        s.dma(("sp", "act")[pb], lambda E, b=b, tt=tt, xm=xm: E.dma_start(
            xout[b, ts(tt), :], xm), reads=[kx])
    if fused:
        s.barrier()
    _sc.__exit__(None, None, None)
    return p



def build_fused():
    p = Prog("fused")
    load_consts(p, ["ident_bf", "ident_f32", "ones_col_bf", "cmask4_bf"])
    p.constv(1.0)
    p.constv(1e-5)
    xin = p.dram_in("xin", shape=[B_CORE, T, D], dtype=F32)
    xout = p.dram_out("xout", [B_CORE, T, D], F32)
    xa = p.nc.dram_tensor("xa_i", [B_CORE, T, D], F32, kind="Internal").ap()
    xb = p.nc.dram_tensor("xb_i", [B_CORE, T, D], F32, kind="Internal").ap()
    Mint = p.nc.dram_tensor("M_i", [D, D], F32, kind="Internal").ap()
    We1b = p.nc.dram_tensor("We1b_i", [12 * 12 * 128, 4096], BF16,
                            kind="Internal").ap()
    We2b = p.nc.dram_tensor("We2b_i", [12 * 128 * 24, D], BF16,
                            kind="Internal").ap()
    Ws1b = p.nc.dram_tensor("Ws1b_i", [D, 4096], BF16, kind="Internal").ap()
    Ws2b = p.nc.dram_tensor("Ws2b_i", [2048, D], BF16, kind="Internal").ap()
    W = {}
    for name, shape in [
            ("w_qkv", [D, 3 * D]), ("w_proj", [D, D]),
            ("wfr", [D, D]), ("wfi", [D, D]), ("ffn_bias", [1, D]),
            ("WinT", [D, 2336]), ("WhT", [1056, 40]), ("WoT", [D, D]),
            ("convw", [1280, 4]), ("convb", [1, 1280]), ("dtb", [1, 1]),
            ("gam", [1, 16]), ("alog", [16, 1]), ("dd", [16, 1]), ("nw", [1, D]),
            ("Wq", [D, D]), ("Wk", [D, D]), ("Wv", [D, D]), ("Wo", [D, D]),
            ("Ws1", [D, 4096]), ("bs1", [1, 4096]), ("Ws2", [2048, D]),
            ("bs2", [1, D]), ("We1g", [12, 12, 128, 4096]), ("be1", [12, 6144]),
            ("We2g", [12 * 128 * 24, D]), ("be2", [12, D]), ("Wr", [D, 12]),
            ("br", [1, 12]),
            ("WqkvT", [D, 3 * D]), ("bqkv", [1, 3 * D]), ("WqaT", [D, D]),
            ("bqa", [1, D]), ("WkaT", [D, D]), ("bka", [1, D]),
            ("WdT", [D, D]), ("bd", [1, D])]:
        W[name] = p.dram_in(name, shape=shape, dtype=F32)
    W["Min"] = Mint
    W["We1b"], W["We2b"] = We1b, We2b
    W["Ws1b"], W["Ws2b"] = Ws1b, Ws2b
    seq = [
        ("cv_", lambda io: build_conv(p, W), None),
        ("mm_", lambda io: build_mmat(p=p, W=W, mout=Mint), None),
        ("s0_", lambda io: build_attn_causal(0, p=p, io=io, W=W), (xin, xa)),
        ("s1_", lambda io: build_ffn0(p=p, io=io, W=W), (xa, xb)),
        ("s2_", lambda io: build_hades(p=p, io=io, W=W), (xb, xa)),
        ("s3_", lambda io: build_ffn0(p=p, io=io, W=W), (xa, xb)),
        ("s4_", lambda io: build_attn_causal(1, p=p, io=io, W=W), (xb, xa)),
        ("s5_", lambda io: build_moe_sparse(p=p, io=io, W=W), (xa, xb)),
        ("s6_", lambda io: build_attn_b1(p=p, io=io, W=W), (xb, xa)),
        ("s7_", lambda io: build_moe_sparse(p=p, io=io, W=W), (xa, xout)),
    ]
    for tag, fn, io in seq:
        p.tag = tag
        fn(io)
        p.tag = ""
    return p


def kernel(**d):
    x = np.ascontiguousarray(np.asarray(d["L0_attn_a_x"], dtype=np.float32))
    nb = x.shape[0]
    ncores = nb // B_CORE
    p = _prog("fused", build_fused)
    w = {
        "w_qkv": d["L0_attn_a_w_qkv"], "w_proj": d["L0_attn_a_w_proj"],
        "wfr": d["L0_ffn_w_freq_real"], "wfi": d["L0_ffn_w_freq_imag"],
        "ffn_bias": np.asarray(d["L0_ffn_bias"]).reshape(1, -1),
        "WinT": _tr(d["L0_attn_b_in_proj_w"]),
        "WhT": _tr(d["L0_attn_b_h_proj_w"]),
        "WoT": _tr(d["L0_attn_b_out_proj_w"]),
        "convw": d["L0_attn_b_conv_w"],
        "convb": np.asarray(d["L0_attn_b_conv_b"]).reshape(1, -1),
        "dtb": np.asarray(d["L0_attn_b_dt_bias"]).reshape(1, 1),
        "gam": np.asarray(d["L0_attn_b_gamma"]).reshape(1, -1),
        "alog": np.asarray(d["L0_attn_b_A_log"]).reshape(-1, 1),
        "dd": np.asarray(d["L0_attn_b_D"]).reshape(-1, 1),
        "nw": np.asarray(d["L0_attn_b_norm_w"]).reshape(1, -1),
        "Wq": d["L1_attn_a_Wq"], "Wk": d["L1_attn_a_Wk"],
        "Wv": d["L1_attn_a_Wv"], "Wo": d["L1_attn_a_Wo"],
        "Ws1": d["L1_ffn_Ws1"], "bs1": np.asarray(d["L1_ffn_bs1"]).reshape(1, -1),
        "Ws2": d["L1_ffn_Ws2"], "bs2": np.asarray(d["L1_ffn_bs2"]).reshape(1, -1),
        "We1g": np.ascontiguousarray(
            np.asarray(d["L1_ffn_We1"]).reshape(12, 8, 128, 2, 6, 512)
            .transpose(0, 3, 4, 2, 1, 5).reshape(12, 12, 128, 4096)),
        "be1": d["L1_ffn_be1"],
        "We2g": np.ascontiguousarray(
            np.asarray(d["L1_ffn_We2"]).reshape(12, 24, 128, 1024)
            .transpose(0, 2, 1, 3).reshape(12 * 128 * 24, 1024)),
        "be2": d["L1_ffn_be2"],
        "Wr": d["L1_ffn_Wr"], "br": np.asarray(d["L1_ffn_br"]).reshape(1, -1),
        "WqkvT": _tr(d["L1_attn_b_Wqkv"]),
        "bqkv": np.asarray(d["L1_attn_b_bqkv"]).reshape(1, -1),
        "WqaT": _tr(d["L1_attn_b_Wqa"]),
        "bqa": np.asarray(d["L1_attn_b_bqa"]).reshape(1, -1),
        "WkaT": _tr(d["L1_attn_b_Wka"]),
        "bka": np.asarray(d["L1_attn_b_bka"]).reshape(1, -1),
        "WdT": _tr(d["L1_attn_b_Wd"]),
        "bd": np.asarray(d["L1_attn_b_bd"]).reshape(1, -1),
    }
    per_core = [dict(w, xin=x[i * B_CORE:(i + 1) * B_CORE]) for i in range(ncores)]
    res = p.run(per_core, ["xout"])
    return np.concatenate([r["xout"] for r in res], axis=0).astype(np.float32)


# version 15
# speedup vs baseline: 1.4051x; 1.0056x over previous
import contextlib
import numpy as np
import ml_dtypes
import concourse.bass as bass
import concourse.mybir as mybir
from concourse.bass_utils import run_bass_kernel_spmd

F32 = mybir.dt.float32
BF16 = mybir.dt.bfloat16
U32 = mybir.dt.uint32
AF = mybir.ActivationFunctionType
ALU = mybir.AluOpType
AX = mybir.AxisListType

NCORES = 8
B_CORE = 2
T = 2048
D = 1024
PT = D // 128
NT = T // 128
NEG = -1.0e30
BF = ml_dtypes.bfloat16

ENGS = ("pe", "dve", "act", "pool", "sp")
DMAQS = ("sp", "act", "pool")
NSLOT = 16
_ATTR = {"pe": "tensor", "dve": "vector", "act": "scalar", "pool": "gpsimd", "sp": "sync"}


def ts(i, n=128):
    return slice(i * n, (i + 1) * n)


def sl(base, i, n=128):
    return slice(base + i * n, base + (i + 1) * n)


class Em:

    def __init__(self, nc, ctx):
        self.nc = nc
        self.sem = {e: ctx.enter_context(nc.semaphore(f"S{e}")) for e in ENGS}
        self.dsem = {}
        self.dcnt = {}
        self.dslot = {q: 0 for q in DMAQS}
        for q in DMAQS:
            for sl_ in range(NSLOT):
                self.dsem[f"d_{q}_{sl_}"] = ctx.enter_context(
                    nc.semaphore(f"Dm{q}{sl_}"))
                self.dcnt[f"d_{q}_{sl_}"] = 0
        self.cnt = {e: 0 for e in ENGS}
        self.q = {e: [] for e in ENGS}
        self.lastw = {}
        self.readers = {}
        self.hiw = {e: {} for e in ENGS}
        self.nops = 0

    def _sem_of(self, key):
        return self.sem[key] if key in self.sem else self.dsem[key]

    def _waits_for(self, eng, reads, writes):
        toks = {}

        def add(t):
            if t is None:
                return
            k, v = t
            if k == "pe" and eng == "pe":
                return
            if toks.get(k, -1) < v:
                toks[k] = v

        for r in reads:
            add(self.lastw.get(r))
        for w in writes:
            add(self.lastw.get(w))
            for t in self.readers.get(w, ()):
                add(t)
        out = []
        hw = self.hiw[eng]
        for k, v in toks.items():
            if hw.get(k, -1) >= v:
                continue
            hw[k] = v
            out.append((k, v))
        return out

    def _emit_waits(self, eng, waits):
        for k, v in waits:
            sem = self._sem_of(k)
            self.q[eng].append(lambda E, s=sem, vv=v: E.wait_ge(s, vv))

    def op(self, eng, fn, reads=(), writes=()):
        self._emit_waits(eng, self._waits_for(eng, reads, writes))
        self.cnt[eng] += 1
        tok = (eng, self.cnt[eng])
        sem = self.sem[eng]

        def doit(E, fn=fn, sem=sem):
            fn(E).then_inc(sem, 1)

        self.q[eng].append(doit)
        self.nops += 1
        for r in reads:
            self.readers.setdefault(r, []).append(tok)
        for w in writes:
            self.lastw[w] = tok
            self.readers[w] = []
        return tok

    def dma(self, qeng, fn, reads=(), writes=(), n=1):
        slot = self.dslot[qeng]
        self.dslot[qeng] = (slot + 1) % NSLOT
        key = f"d_{qeng}_{slot}"
        waits = self._waits_for(qeng, reads, writes)
        hw = self.hiw[qeng]
        if self.dcnt[key] > 0 and hw.get(key, -1) < self.dcnt[key]:
            waits = waits + [(key, self.dcnt[key])]
            hw[key] = self.dcnt[key]
        self._emit_waits(qeng, waits)
        self.dcnt[key] += 16 * n
        tok = (key, self.dcnt[key])
        dsem = self.dsem[key]

        def doit(E, fn=fn, dsem=dsem):
            insts = fn(E)
            if not isinstance(insts, (list, tuple)):
                insts = [insts]
            for inst in insts:
                inst.then_inc(dsem, 16)

        self.q[qeng].append(doit)
        self.nops += n
        for r in reads:
            self.readers.setdefault(r, []).append(tok)
        for w in writes:
            self.lastw[w] = tok
            self.readers[w] = []
        return tok

    def barrier(self):
        targets = {e: self.cnt[e] for e in ENGS}
        dtargets = dict(self.dcnt)
        for e in ENGS:
            hw = self.hiw[e]
            for o in ENGS:
                if o == e or targets[o] == 0 or hw.get(o, -1) >= targets[o]:
                    continue
                sem = self.sem[o]
                self.q[e].append(lambda E, s=sem, v=targets[o]: E.wait_ge(s, v))
                hw[o] = targets[o]
            for k2, v2 in dtargets.items():
                if v2 > 0 and hw.get(k2, -1) < v2:
                    sem = self.dsem[k2]
                    self.q[e].append(lambda E, s=sem, vv=v2: E.wait_ge(s, vv))
                    hw[k2] = v2
        self.lastw = {}
        self.readers = {}

    def finish(self):
        for qq in DMAQS:
            for sl_ in range(NSLOT):
                key = f"d_{qq}_{sl_}"
                v = self.dcnt[key]
                if v > 0:
                    sem = self.dsem[key]
                    self.q[qq].append(lambda E, s=sem, vv=v: E.wait_ge(s, vv))

    def finalize(self):
        with self.nc.Block() as block:
            for e in ENGS:
                deco = getattr(block, _ATTR[e])
                ops = self.q[e]

                def body(E, ops=ops):
                    for f in ops:
                        f(E)

                deco(body)


class Prog:
    def __init__(self, name):
        self.nc = bass.Bass("TRN2", target_bir_lowering=False, debug=False)
        self.ctx = contextlib.ExitStack()
        self._root_ctx = self.ctx
        self.s = Em(self.nc, self.ctx)
        self.name = name
        self.tag = ""
        self.inputs = {}
        self.pp_bytes = 0

    @contextlib.contextmanager
    def scope(self):
        outer = self.ctx
        self.ctx = contextlib.ExitStack()
        try:
            yield
        finally:
            self.ctx.close()
            self.ctx = outer

    def constv(self, val):
        if not hasattr(self, "_cvals"):
            self._cvals = {}
            old_ctx, self.ctx = self.ctx, getattr(self, "_root_ctx", self.ctx)
            self._cpool = self.sb("cpool", [128, 16], F32)
            self.ctx = old_ctx
        if val not in self._cvals:
            i = len(self._cvals)
            assert i < 16
            ap = self._cpool[:, i:i + 1]
            self.s.op("pool", lambda E, ap=ap, val=val: E.memset(ap, float(val)),
                      writes=["cpool"])
            self._cvals[val] = ap
        return self._cvals[val]

    def dram_in(self, name, arr=None, shape=None, dtype=F32):
        name = self.tag + name
        if arr is not None:
            npdt = {F32: np.float32, BF16: BF, U32: np.uint32}[dtype]
            arr = np.ascontiguousarray(np.asarray(arr).astype(npdt))
            shape = list(arr.shape)
            self.inputs[name] = arr
        h = self.nc.dram_tensor(name, list(shape), dtype, kind="ExternalInput")
        return h.ap()

    def dram_out(self, name, shape, dtype=F32):
        return self.nc.dram_tensor(self.tag + name, list(shape), dtype,
                                   kind="ExternalOutput").ap()

    def sb(self, name, shape, dtype):
        name = self.tag + name
        esz = {F32: 4, BF16: 2, U32: 4}[dtype]
        n = esz
        for d_ in shape[1:]:
            n *= d_
        self.pp_bytes += n
        return self.ctx.enter_context(self.nc.sbuf_tensor(name, list(shape), dtype))

    def psum(self, name, shape, dtype=F32):
        return self.ctx.enter_context(
            self.nc.psum_tensor(self.tag + name, list(shape), dtype))

    def build(self):
        if getattr(self, "_built", False):
            return
        self._built = True
        self.s.finish()
        self.s.finalize()

    def run(self, per_core_inputs, out_names):
        self.build()
        in_maps = []
        for ci in range(len(per_core_inputs)):
            m = dict(self.inputs)
            for k, v in per_core_inputs[ci].items():
                m[k] = np.ascontiguousarray(v)
            in_maps.append(m)
        res = run_bass_kernel_spmd(self.nc, in_maps, core_ids=list(range(len(in_maps))))
        return [{k: np.asarray(r[k]) for k in out_names} for r in res.results]



def consts_common():
    c = {}
    c["ident_bf"] = np.eye(128, dtype=BF)
    c["ident_f32"] = np.eye(128, dtype=np.float32)
    c["ones_col_bf"] = np.ones((128, 1), dtype=BF)
    m = np.zeros((128, 4 * 512), dtype=np.float32)
    for a in range(4):
        qq = np.arange(128)[:, None] + a * 128
        kk = np.arange(512)[None, :]
        m[:, a * 512:(a + 1) * 512] = np.where(kk <= qq, 0.0, NEG)
    c["cmask4"] = m
    c["cmask4_bf"] = m.astype(BF)
    return c


def load_consts(p, names):
    if not hasattr(p, "_cst"):
        p._cst = {}
    cn = consts_common()
    old_ctx, old_tag = p.ctx, p.tag
    p.ctx, p.tag = p._root_ctx, ""
    try:
        for k in names:
            if k in p._cst:
                continue
            v = cn[k]
            dt = F32 if v.dtype == np.float32 else BF16
            ap = p.dram_in("c_" + k, v, dtype=dt)
            t = p.sb("cs_" + k, list(v.shape), dt)
            p.s.dma("sp", lambda E, t=t, ap=ap: E.dma_start(t[:, :], ap[:, :]),
                    writes=[f"c:{k}"])
            p._cst[k] = t
    finally:
        p.ctx, p.tag = old_ctx, old_tag
    return p._cst



def alloc_lnt(p):
    d = {}
    d["xt"] = p.sb("lnt_xt", [128, D], F32)
    d["st"] = p.sb("lnt_st", [128, 8], F32)
    d["xn_tm"] = p.sb("lnt_xn", [128, D], BF16)
    d["sq"] = d["xn_tm"]
    return d


def emit_lnt_tile(p, lnt, x_src_ap, psT, psTk, xn_fm_writer, extra_tm=None):
    xkey = "lnt_xt"
    """LayerNorm one token-major tile then transpose to feature-major.
    x_src_ap: HBM AP [128, D] for this tile. xn_fm_writer(j) -> (dst_ap, key)
    for the j-th transposed [128,128] block. extra_tm: optional (dst_ap, key)
    to also keep the token-major normalized tile (bf16)."""
    s = p.s
    d = lnt
    s.dma("sp", lambda E: E.dma_start(d["xt"][:, :], x_src_ap), writes=[xkey])
    s.op("act", lambda E: E.activation(d["sq"][:, :], d["xt"][:, :], AF.Square,
                                       accum_out=d["st"][:, 1:2]),
         reads=[xkey], writes=["lnt_xn", "lnt_ssq"])
    s.op("dve", lambda E: E.tensor_reduce(d["st"][:, 0:1], d["xt"][:, :], AX.X, ALU.add),
         reads=[xkey], writes=["lnt_sum"])
    s.op("dve", lambda E: E.tensor_scalar(d["st"][:, 2:4], d["st"][:, 0:2], 1.0 / D, None,
                                          ALU.mult),
         reads=["lnt_sum", "lnt_ssq"], writes=["lnt_mean_msq"])
    s.op("act", lambda E: E.activation(d["st"][:, 4:5], d["st"][:, 2:3], AF.Square),
         reads=["lnt_mean_msq"], writes=["lnt_m2"])
    s.op("dve", lambda E: E.tensor_tensor(d["st"][:, 5:6], d["st"][:, 3:4], d["st"][:, 4:5],
                                          ALU.subtract),
         reads=["lnt_mean_msq", "lnt_m2"], writes=["lnt_var"])
    eps = p.constv(1e-5)
    s.op("act", lambda E: E.activation(d["st"][:, 6:7], d["st"][:, 5:6], AF.Sqrt, bias=eps),
         reads=["lnt_var", "cpool"], writes=["lnt_std"])
    s.op("dve", lambda E: E.reciprocal(d["st"][:, 7:8], d["st"][:, 6:7]),
         reads=["lnt_std"], writes=["lnt_rstd"])
    s.op("dve", lambda E: E.tensor_scalar(d["xn_tm"][:, :], d["xt"][:, :],
                                          d["st"][:, 2:3], d["st"][:, 7:8],
                                          ALU.subtract, ALU.mult),
         reads=[xkey, "lnt_mean_msq", "lnt_rstd"], writes=["lnt_xn"])
    ident = p._cst["ident_bf"]

    def mmT(E):
        last = None
        for j in range(PT):
            last = E.transpose(psT[:, ts(j)], d["xn_tm"][:, ts(j)], ident[:, :])
        return last

    s.op("pe", mmT, reads=["lnt_xn", "c:ident_bf"], writes=[psTk])
    for j in range(PT):
        dst, key = xn_fm_writer(j)
        s.op("act", lambda E, dst=dst, j=j: E.activation(dst, psT[:, ts(j)], AF.Copy),
             reads=[psTk], writes=[key])
    if extra_tm is not None:
        dst, key = extra_tm
        s.op("dve", lambda E, dst=dst: E.tensor_copy(dst, d["xn_tm"][:, :]),
             reads=["lnt_xn"], writes=[key])


def emit_residual_tile(p, yfm_fn, x_src_ap, x_dst_ap, psT32, psT32k, rbufs,
                       out_key=None):
    s = p.s
    xb, yb = rbufs
    s.dma("sp", lambda E: E.dma_start(xb[:, :], x_src_ap), writes=["res_rx"])
    ident = p._cst["ident_f32"]
    srcs = [yfm_fn(j) for j in range(PT)]
    for w in range(2):
        def mmTw(E, w=w):
            last = None
            for j in range(4):
                last = E.transpose(psT32[:, ts(j)], srcs[w * 4 + j][0], ident[:, :])
            return last

        s.op("pe", mmTw, reads=[srcs[w * 4 + j][1] for j in range(4)] + ["c:ident_f32"],
             writes=[psT32k])
        for j in range(4):
            jj = w * 4 + j
            s.op("dve", lambda E, j=j, jj=jj: E.tensor_copy(yb[:, ts(jj)], psT32[:, ts(j)]),
                 reads=[psT32k], writes=[f"res_ry:{jj}"])
    s.op("dve", lambda E: E.tensor_tensor(xb[:, :], xb[:, :], yb[:, :], ALU.add),
         reads=["res_rx"] + [f"res_ry:{jj}" for jj in range(PT)],
         writes=["res_rx"])
    s.dma("sp", lambda E: E.dma_start(x_dst_ap, xb[:, :]), reads=["res_rx"],
          writes=([out_key] if out_key else []))


def load_weight_kt(p, name, hbm_ap, rows, cols, col0, ncols, sb, sbk, qeng="pool"):
    kt_n = rows // 128
    src = hbm_ap[:, col0:col0 + ncols].rearrange("(kt p) c -> p kt c", p=128)
    dst = bass.AP(sb, 0, [[sb.shape[1], 128], [ncols, kt_n], [1, ncols]])
    return p.s.dma(qeng, lambda E: E.dma_start(dst, src), writes=[sbk])


def emit_group_residual(p, b, g, yall, xin, xout, psT32, psT32k, rbufs,
                        ykey="yall", out_key=None):
    for j4 in range(4):
        tt = g * 4 + j4

        def yfm_fn(jj, j4=j4):
            return (yall[:, sl(jj * 512, j4)], f"{ykey}:{jj}")

        emit_residual_tile(p, yfm_fn, xin[b, ts(tt), :], xout[b, ts(tt), :],
                           psT32, psT32k, rbufs, out_key=out_key)



def build_attn_causal(layer, p=None, io=None, W=None, conv=False):
    fused = p is not None
    if not fused:
        p = Prog("attnX")
    _sc = p.scope() if fused else contextlib.nullcontext()
    _sc.__enter__()
    s = p.s
    p._cst = load_consts(p, ["ident_bf", "ident_f32", "cmask4_bf"])
    _bgi = iter(conv_jobs(p, W)) if conv else None

    def bgd(n):
        if _bgi is None:
            return
        for _ in range(n):
            job = next(_bgi, None)
            if job is None:
                return
            job()

    xin = io[0] if fused else p.dram_in("xin", shape=[B_CORE, T, D], dtype=F32)
    xout = io[1] if fused else p.dram_out("xout", [B_CORE, T, D], F32)
    def gw(name, shape):
        return W[name] if W else p.dram_in(name, shape=shape, dtype=F32)
    if layer == 0:
        wqkv = gw("w_qkv", [D, 3 * D])
        wproj = gw("w_proj", [D, D])
        NH, HD, SCALE = 16, 64, 1.0 / 8.0
        wsrc = {0: (wqkv, 0), 1: (wqkv, D), 2: (wqkv, 2 * D), 3: (wproj, 0)}
    else:
        NH, HD, SCALE = 1, 1024, 1.0 / 32.0
        wsrc = {0: (gw("Wq", [D, D]), 0), 1: (gw("Wk", [D, D]), 0),
                2: (gw("Wv", [D, D]), 0), 3: (gw("Wo", [D, D]), 0)}

    lnt = alloc_lnt(p)
    xnf = p.sb("xnf", [128, PT * T], BF16)
    Kf = p.sb("Kf", [128, PT * T], BF16)
    Vt = p.sb("Vt", [128, NT * D], BF16)
    Qf = p.sb("Qf", [128, PT * 512], BF16)
    Ot = p.sb("Ot", [128, 4 * D], BF16)
    Of = p.sb("Of", [128, PT * 512], BF16)
    Ssb = p.sb("Ssb", [128, T], F32)
    Pbf = p.sb("Pbf", [128, T], BF16)
    PTsb = p.sb("PTsb", [128, 512], BF16)
    wb = [p.sb("wbA", [128, PT * D], BF16),
          p.sb("wbB", [128, PT * D], BF16)]
    yall = p.sb("yall", [128, PT * 512], F32)
    rx = p.sb("rx", [128, D], F32)
    ry = p.sb("ry", [128, D], F32)
    mx = p.sb("mx", [128, 8], F32)
    den = p.sb("den", [128, 8], F32)
    rcp = p.sb("rcp", [128, 1], F32)
    nmx = p.sb("nmx", [128, 1], F32)

    psS = [p.psum("psS0", [128, 512], F32), p.psum("psS1", [128, 512], F32)]
    psO = p.psum("psO", [128, 1024], F32)
    psT = p.psum("psT", [128, 1024], BF16)
    psP = [p.psum("psP0", [128, 512], F32), p.psum("psP1", [128, 512], F32)]
    psR = p.psum("psR", [128, 512], F32)

    def wload(which, slot):
        hbm, col0 = wsrc[which]
        return load_weight_kt(p, f"w{which}", hbm, D, D, col0, D, wb[slot],
                              f"wb{slot}")

    for b in range(B_CORE):
        for tt in range(NT):
            def writer(j, tt=tt):
                return (xnf[:, sl(j * T, tt)], f"xnf:{j}")
            emit_lnt_tile(p, lnt, xin[b, ts(tt), :], psT, "psT", writer)
            bgd(4)
        wload(1, 0)
        for c in range(T // 512):
            for mt in range(PT):
                def mmK(E, c=c, mt=mt):
                    last = None
                    for kt in range(PT):
                        last = E.matmul(psP[mt % 2][:, 0:512],
                                        wb[0][:, sl(kt * D, mt)],
                                        xnf[:, kt * T + c * 512:kt * T + (c + 1) * 512],
                                        start=(kt == 0), stop=(kt == PT - 1))
                    return last
                s.op("pe", mmK, reads=["wb0"] + [f"xnf:{k}" for k in range(PT)],
                     writes=[f"psP{mt % 2}"])
                s.op("act", lambda E, c=c, mt=mt: E.activation(
                    Kf[:, mt * T + c * 512:mt * T + (c + 1) * 512],
                    psP[mt % 2][:, 0:512], AF.Copy),
                    reads=[f"psP{mt % 2}"], writes=[f"Kf:{mt}"])
            bgd(12)
        wload(2, 1)
        for tt in range(NT):
            for nh in range(2):
                def mmV(E, tt=tt, nh=nh):
                    last = None
                    for kt in range(PT):
                        last = E.matmul(psP[nh][:, 0:512],
                                        xnf[:, sl(kt * T, tt)],
                                        wb[1][:, kt * D + nh * 512:kt * D + (nh + 1) * 512],
                                        start=(kt == 0), stop=(kt == PT - 1))
                    return last
                s.op("pe", mmV, reads=["wb1"] + [f"xnf:{k}" for k in range(PT)],
                     writes=[f"psP{nh}"])
                s.op("act", lambda E, tt=tt, nh=nh: E.activation(
                    Vt[:, tt * D + nh * 512:tt * D + (nh + 1) * 512],
                    psP[nh][:, 0:512], AF.Copy),
                    reads=[f"psP{nh}"], writes=[f"Vt:{tt}"])
            bgd(8)
        wload(0, 0)
        wload(3, 1)
        for g in range(T // 512):
            for mt in range(PT):
                def mmQ(E, g=g, mt=mt):
                    last = None
                    for kt in range(PT):
                        last = E.matmul(psP[mt % 2][:, 0:512],
                                        wb[0][:, sl(kt * D, mt)],
                                        xnf[:, kt * T + g * 512:kt * T + (g + 1) * 512],
                                        start=(kt == 0), stop=(kt == PT - 1))
                    return last
                s.op("pe", mmQ, reads=["wb0"] + [f"xnf:{k}" for k in range(PT)],
                     writes=[f"psP{mt % 2}"])
                s.op("act", lambda E, mt=mt: E.activation(
                    Qf[:, ts(mt, 512)], psP[mt % 2][:, 0:512], AF.Copy),
                    reads=[f"psP{mt % 2}"], writes=["Qf"])
            bgd(8)
            for qi4 in range(4):
                qi = g * 4 + qi4
                nkc = qi // 4 + 1
                for h in range(NH):
                    fp0 = (h * HD) // 128
                    po = (h * HD) % 128
                    for c in range(nkc):
                        diag = (c == nkc - 1)

                        def mmS(E, qi4=qi4, c=c, fp0=fp0, po=po):
                            if HD == 64:
                                return E.matmul(
                                    psS[c % 2][:, 0:512],
                                    Qf[po:po + 64, sl(fp0 * 512, qi4)],
                                    Kf[po:po + 64, fp0 * T + c * 512:fp0 * T + (c + 1) * 512],
                                    start=True, stop=True)
                            last = None
                            for kt in range(PT):
                                last = E.matmul(
                                    psS[c % 2][:, 0:512],
                                    Qf[:, sl(kt * 512, qi4)],
                                    Kf[:, kt * T + c * 512:kt * T + (c + 1) * 512],
                                    start=(kt == 0), stop=(kt == PT - 1))
                            return last

                        kreads = [f"Kf:{fp0}"] if HD == 64 else [f"Kf:{k}" for k in range(PT)]
                        s.op("pe", mmS, reads=["Qf"] + kreads, writes=[f"psS{c % 2}"])
                        if diag:
                            s.op("dve", lambda E, c=c, qi4=qi4: E.scalar_tensor_tensor(
                                Ssb[:, c * 512:(c + 1) * 512], psS[c % 2][:, 0:512], SCALE,
                                p._cst["cmask4_bf"][:, qi4 * 512:(qi4 + 1) * 512],
                                ALU.mult, ALU.add),
                                reads=[f"psS{c % 2}", "c:cmask4_bf"], writes=[f"Ssb:{c}"])
                            s.op("dve", lambda E, c=c: E.tensor_reduce(
                                mx[:, c:c + 1], Ssb[:, c * 512:(c + 1) * 512], AX.X, ALU.max),
                                reads=[f"Ssb:{c}"], writes=[f"mx:{c}"])
                        else:
                            s.op("dve", lambda E, c=c: E.tensor_scalar(
                                Ssb[:, c * 512:(c + 1) * 512], psS[c % 2][:, 0:512],
                                SCALE, None, ALU.mult, ALU.max, accum_out=mx[:, c:c + 1]),
                                reads=[f"psS{c % 2}"], writes=[f"Ssb:{c}", f"mx:{c}"])
                    s.op("dve", lambda E, nkc=nkc: E.tensor_reduce(
                        nmx[:, 0:1], mx[:, 0:nkc], AX.X, ALU.max, negate=True),
                        reads=[f"mx:{c}" for c in range(nkc)], writes=["nmx"])
                    for c in range(nkc):
                        s.op("act", lambda E, c=c: E.activation(
                            Pbf[:, c * 512:(c + 1) * 512], Ssb[:, c * 512:(c + 1) * 512],
                            AF.Exp, bias=nmx[:, 0:1], accum_out=den[:, c:c + 1]),
                            reads=[f"Ssb:{c}", "nmx"], writes=[f"Pbf:{c}", f"den:{c}"])
                    s.op("dve", lambda E, nkc=nkc: E.tensor_reduce(
                        den[:, 6:7], den[:, 0:nkc], AX.X, ALU.add),
                        reads=[f"den:{c}" for c in range(nkc)], writes=["densum"])
                    s.op("dve", lambda E: E.reciprocal(rcp[:, 0:1], den[:, 6:7]),
                         reads=["densum"], writes=["rcp"])
                    for c in range(nkc):
                        def mmT(E, c=c):
                            last = None
                            for j in range(4):
                                last = E.transpose(psT[:, ts(j)], Pbf[:, sl(c * 512, j)],
                                                   p._cst["ident_bf"][:, :])
                            return last

                        s.op("pe", mmT, reads=[f"Pbf:{c}", "c:ident_bf"], writes=["psT"])
                        s.op("act", lambda E: E.activation(PTsb[:, 0:512], psT[:, 0:512],
                                                           AF.Copy),
                             reads=["psT"], writes=["PTsb"])

                        def mmPV(E, c=c, h=h, nkc=nkc):
                            last = None
                            for j in range(4):
                                tt = c * 4 + j
                                st = (c == 0 and j == 0)
                                sp = (c == nkc - 1 and j == 3)
                                if HD == 64:
                                    last = E.matmul(psO[:, h * 64:(h + 1) * 64],
                                                    PTsb[:, ts(j)],
                                                    Vt[:, tt * D + h * 64:tt * D + (h + 1) * 64],
                                                    start=st, stop=sp)
                                else:
                                    for nh in range(2):
                                        last = E.matmul(
                                            psO[:, nh * 512:(nh + 1) * 512],
                                            PTsb[:, ts(j)],
                                            Vt[:, tt * D + nh * 512:tt * D + (nh + 1) * 512],
                                            start=st, stop=sp)
                            return last

                        s.op("pe", mmPV,
                             reads=["PTsb"] + [f"Vt:{c * 4 + j}" for j in range(4)],
                             writes=["psO"])
                    if HD == 64:
                        s.op("dve", lambda E, qi4=qi4, h=h: E.tensor_scalar(
                            Ot[:, qi4 * D + h * 64:qi4 * D + (h + 1) * 64],
                            psO[:, h * 64:(h + 1) * 64], rcp[:, 0:1], None, ALU.mult),
                            reads=["psO", "rcp"], writes=[f"Ot:{qi4}"])
                    else:
                        def nrm(E, qi4=qi4):
                            E.tensor_scalar(Ot[:, qi4 * D:qi4 * D + 512], psO[:, 0:512],
                                            rcp[:, 0:1], None, ALU.mult)
                            return E.tensor_scalar(Ot[:, qi4 * D + 512:(qi4 + 1) * D],
                                                   psO[:, 512:1024], rcp[:, 0:1], None,
                                                   ALU.mult)
                        s.op("dve", nrm, reads=["psO", "rcp"], writes=[f"Ot:{qi4}"])
                bgd(8)
            for qi4 in range(4):
                def mmTO(E, qi4=qi4):
                    last = None
                    for j in range(PT):
                        last = E.transpose(psT[:, ts(j)], Ot[:, sl(qi4 * D, j)],
                                           p._cst["ident_bf"][:, :])
                    return last

                s.op("pe", mmTO, reads=[f"Ot:{qi4}", "c:ident_bf"], writes=["psT"])
                for j in range(PT):
                    s.op("act", lambda E, qi4=qi4, j=j: E.activation(
                        Of[:, sl(j * 512, qi4)], psT[:, ts(j)], AF.Copy),
                        reads=["psT"], writes=[f"Of:{qi4}:{j}"])
            for mt in range(PT):
                def mmproj(E, mt=mt):
                    last = None
                    for kt in range(PT):
                        last = E.matmul(psR[:, 0:512],
                                        wb[1][:, sl(kt * D, mt)],
                                        Of[:, kt * 512:(kt + 1) * 512],
                                        start=(kt == 0), stop=(kt == PT - 1))
                    return last

                s.op("pe", mmproj,
                     reads=["wb1"] + [f"Of:{q4}:{kt}" for q4 in range(4) for kt in range(PT)],
                     writes=["psR"])
                s.op("dve", lambda E, mt=mt: E.tensor_copy(yall[:, ts(mt, 512)],
                                                           psR[:, 0:512]),
                     reads=["psR"], writes=[f"yall:{mt}"])
            emit_group_residual(p, b, g, yall, xin, xout, psS[0], "psS0", (rx, ry))
            bgd(8)
    if _bgi is not None:
        for job in _bgi:
            job()
    if fused:
        p.s.barrier()
    _sc.__exit__(None, None, None)
    return p





def build_mmat(p=None, W=None, mout=None):
    fused = p is not None
    if not fused:
        p = Prog("mmat")
    _sc = p.scope() if fused else contextlib.nullcontext()
    _sc.__enter__()
    s = p.s
    n_ = np.arange(D)
    ang = 2.0 * np.pi / D * np.outer(n_, n_)
    Chbm = p.dram_in("Cmat", np.cos(ang).astype(np.float32), dtype=F32)
    Shbm = p.dram_in("Smat", np.sin(ang).astype(np.float32), dtype=F32)
    wfr = W["wfr"] if W else p.dram_in("wfr", shape=[D, D], dtype=F32)
    wfi = W["wfi"] if W else p.dram_in("wfi", shape=[D, D], dtype=F32)
    Mout = mout if mout is not None else p.dram_out("Mout", [D, D], F32)
    PTh = p.nc.dram_tensor(p.tag + "PTh", [D, D], F32, kind="Internal").ap()
    QNh = p.nc.dram_tensor(p.tag + "QNh", [D, D], F32, kind="Internal").ap()

    lbuf = p.sb("lbuf", [128, PT * 128], F32)
    lbuf2 = p.sb("lbuf2", [128, PT * 128], F32)
    rbuf = p.sb("rbuf", [128, PT * 512], F32)
    rbuf2 = p.sb("rbuf2", [128, PT * 512], F32)
    tmp = p.sb("tmp", [128, 512], F32)
    obuf = p.sb("obuf", [128, 512], F32)
    psA = p.psum("psA", [128, 512], F32)
    psB = p.psum("psB", [128, 512], F32)

    def ldl(buf, key, hbm, mt):
        src = hbm[:, ts(mt)].rearrange("(kt p) c -> p kt c", p=128)
        dst = bass.AP(buf, 0, [[buf.shape[1], 128], [128, PT], [1, 128]])
        p.s.dma("pool", lambda E: E.dma_start(dst, src), writes=[key])

    def ldr(buf, key, hbm, nc2):
        src = hbm[:, nc2 * 512:(nc2 + 1) * 512].rearrange("(kt p) c -> p kt c", p=128)
        dst = bass.AP(buf, 0, [[buf.shape[1], 128], [512, PT], [1, 512]])
        p.s.dma("pool", lambda E: E.dma_start(dst, src), writes=[key])

    def mmpass(lA, lB, rA, rB, out_h, mode, scale=None):
        for nc2 in range(2):
            ldr(rbuf, "rbuf", rA, nc2)
            ldr(rbuf2, "rbuf2", rB, nc2)
            for mt in range(PT):
                ldl(lbuf, "lbuf", lA, mt)
                ldl(lbuf2, "lbuf2", lB, mt)
                if mode == "acc":
                    def mm1b(E):
                        last = None
                        for kt in range(PT):
                            last = E.matmul(psA[:, 0:512], lbuf[:, ts(kt)],
                                            rbuf[:, ts(kt, 512)],
                                            start=(kt == 0), stop=False)
                        for kt in range(PT):
                            last = E.matmul(psA[:, 0:512], lbuf2[:, ts(kt)],
                                            rbuf2[:, ts(kt, 512)],
                                            start=False, stop=(kt == PT - 1))
                        return last
                    s.op("pe", mm1b, reads=["lbuf", "lbuf2", "rbuf", "rbuf2"],
                         writes=["psA"])
                    if scale is None:
                        s.op("dve", lambda E: E.tensor_copy(obuf[:, :], psA[:, 0:512]),
                             reads=["psA"], writes=["obuf"])
                    else:
                        s.op("dve", lambda E: E.tensor_scalar(
                            obuf[:, :], psA[:, 0:512], scale, None, ALU.mult),
                            reads=["psA"], writes=["obuf"])
                else:
                    def mm1(E):
                        last = None
                        for kt in range(PT):
                            last = E.matmul(psA[:, 0:512], lbuf[:, ts(kt)],
                                            rbuf[:, ts(kt, 512)],
                                            start=(kt == 0), stop=(kt == PT - 1))
                        return last
                    s.op("pe", mm1, reads=["lbuf", "rbuf"], writes=["psA"])
                    s.op("dve", lambda E: E.tensor_copy(tmp[:, :], psA[:, 0:512]),
                         reads=["psA"], writes=["tmp"])

                    def mm2(E):
                        last = None
                        for kt in range(PT):
                            last = E.matmul(psB[:, 0:512], lbuf2[:, ts(kt)],
                                            rbuf2[:, ts(kt, 512)],
                                            start=(kt == 0), stop=(kt == PT - 1))
                        return last
                    s.op("pe", mm2, reads=["lbuf2", "rbuf2"], writes=["psB"])
                    s.op("dve", lambda E: E.tensor_tensor(obuf[:, :], tmp[:, :],
                                                          psB[:, 0:512], ALU.subtract),
                         reads=["tmp", "psB"], writes=["obuf"])
                s.dma("sp", lambda E, mt=mt, nc2=nc2: E.dma_start(
                    out_h[ts(mt), nc2 * 512:(nc2 + 1) * 512], obuf[:, :]),
                    reads=["obuf"])

    mmpass(wfr, wfi, Chbm, Shbm, PTh, "acc")
    mmpass(wfr, wfi, Shbm, Chbm, QNh, "sub")
    mmpass(PTh, QNh, Chbm, Shbm, Mout, "acc", scale=1.0 / D)
    if fused:
        p.s.barrier()
    _sc.__exit__(None, None, None)
    return p


def build_ffn0(p=None, io=None, W=None):
    fused = p is not None
    if not fused:
        p = Prog("ffn0")
    _sc = p.scope() if fused else contextlib.nullcontext()
    _sc.__enter__()
    s = p.s
    p._cst = load_consts(p, ["ident_bf", "ident_f32"])
    xin = io[0] if fused else p.dram_in("xin", shape=[B_CORE, T, D], dtype=F32)
    xout = io[1] if fused else p.dram_out("xout", [B_CORE, T, D], F32)
    Min = W["Min"] if W else p.dram_in("Min", shape=[D, D], dtype=F32)
    bias = W["ffn_bias"] if W else p.dram_in("bias", shape=[1, D], dtype=F32)

    lnt = alloc_lnt(p)
    xnf = p.sb("xnf", [128, PT * T], BF16)
    Msb = p.sb("Msb", [128, PT * D], BF16)
    yall = p.sb("yall", [128, PT * 512], F32)
    rx = p.sb("rx", [128, D], F32)
    ry = p.sb("ry", [128, D], F32)
    brow = p.sb("brow", [8, 128], F32)
    bcol = p.sb("bcol", [128, 8], F32)
    psA = p.psum("psA", [128, 512], F32)
    psB = p.psum("psB", [128, 512], F32)
    psT = p.psum("psT", [128, 1024], BF16)
    psR = p.psum("psR", [128, 512], F32)

    load_weight_kt(p, "M", Min, D, D, 0, D, Msb, "Msb")
    s.dma("sp", lambda E: E.dma_start(
        brow[:, :], bias.rearrange("o (pt x) -> (o pt) x", x=128)), writes=["brow"])
    s.op("pe", lambda E: E.transpose(psB[0:128, 0:8], brow[0:8, 0:128],
                                     p._cst["ident_f32"][0:8, 0:8]),
         reads=["brow", "c:ident_f32"], writes=["psB"])
    s.op("dve", lambda E: E.tensor_copy(bcol[:, :], psB[0:128, 0:8]),
         reads=["psB"], writes=["bcol"])

    for b in range(B_CORE):
        for tt in range(NT):
            def writer(j, tt=tt):
                return (xnf[:, sl(j * T, tt)], f"xnf:{j}")
            emit_lnt_tile(p, lnt, xin[b, ts(tt), :], psT, "psT", writer)
        for g in range(T // 512):
            for mt in range(PT):
                def mmM(E, g=g, mt=mt):
                    last = None
                    for kt in range(PT):
                        last = E.matmul(psR[:, 0:512],
                                        Msb[:, sl(kt * D, mt)],
                                        xnf[:, kt * T + g * 512:kt * T + (g + 1) * 512],
                                        start=(kt == 0), stop=(kt == PT - 1))
                    return last

                s.op("pe", mmM, reads=["Msb"] + [f"xnf:{k}" for k in range(PT)],
                     writes=["psR"])
                s.op("dve", lambda E, mt=mt: E.tensor_scalar(
                    yall[:, ts(mt, 512)], psR[:, 0:512], bcol[:, mt:mt + 1], None, ALU.add),
                    reads=["psR", "bcol"], writes=[f"yall:{mt}"])
            emit_group_residual(p, b, g, yall, xin, xout, psA, "psA", (rx, ry))
    if fused:
        p.s.barrier()
    _sc.__exit__(None, None, None)
    return p





def build_hades(p=None, io=None, W=None):
    fused = p is not None
    if not fused:
        p = Prog("hades")
    _sc = p.scope() if fused else contextlib.nullcontext()
    _sc.__enter__()
    s = p.s
    p._cst = load_consts(p, ["ident_bf", "ident_f32", "ones_col_bf"])
    one = p.constv(1.0)

    xin = io[0] if fused else p.dram_in("xin", shape=[B_CORE, T, D], dtype=F32)
    xout = io[1] if fused else p.dram_out("xout", [B_CORE, T, D], F32)
    def gw(name, shape):
        return W[name] if W else p.dram_in(name, shape=shape, dtype=F32)
    WinT = gw("WinT", [D, 2336])
    WhT = gw("WhT", [1056, 40])
    WoT = gw("WoT", [D, D])
    convw = gw("convw", [1280, 4])
    convb = gw("convb", [1, 1280])
    dtb_h = gw("dtb", [1, 1])
    gam_h = gw("gam", [1, 16])
    alog_h = gw("alog", [16, 1])
    dd_h = gw("dd", [16, 1])
    nw_h = gw("nw", [1, D])
    trilnp = np.where(np.arange(128)[None, :] >= np.arange(128)[:, None],
                      0.0, NEG).astype(np.float32)
    tril_h = p.dram_in("tril", trilnp, dtype=F32)
    esel = np.zeros((16, 16 * 128), np.float32)
    for h in range(16):
        esel[h, h * 128:(h + 1) * 128] = 1.0
    esel_h = p.dram_in("esel", esel, dtype=F32)
    eexp = np.zeros((16, PT * 128), np.float32)
    for pt in range(PT):
        for m in range(128):
            eexp[2 * pt + m // 64, pt * 128 + m] = 1.0
    eexp_h = p.dram_in("eexp", eexp, dtype=F32)
    rd_h = p.dram_in("rdrow", (1.0 / np.arange(1, T + 1, dtype=np.float64)
                               ).astype(np.float32).reshape(1, T), dtype=F32)
    onesr_h = p.dram_in("onesr", np.ones((1, 128), np.float32), dtype=F32)

    lnt = alloc_lnt(p)
    xnf = p.sb("xnf", [128, PT * 512], BF16)
    xnf32 = p.sb("xnf32", [128, PT * 512], F32)
    cum = p.sb("cum", [128, PT * 512], F32)
    wbuf = p.sb("wbuf", [128, PT * 1280], BF16)
    woT = p.sb("woT", [128, PT * D], BF16)
    wdt32 = p.sb("wdt32", [128, PT * 32], F32)
    hp32 = p.sb("hp32", [128, 9 * 40], F32)
    zc = p.sb("zc", [128, PT * 512], F32)
    xbc = p.sb("xbc", [128, 10 * 512], F32)
    dta = p.sb("dta", [128, 4 * 32], F32)
    dtaf = p.sb("dtaf", [32, 4 * 128], F32)
    hb = p.sb("hb", [128, 4 * 40], F32)
    hw_ = p.sb("hw_", [128, 28], F32)
    ohb = p.sb("ohb", [128, 28], F32)
    csb = p.sb("csb", [128, 28], F32)
    fst = p.sb("fst", [128, 28], F32)
    gth = p.sb("gth", [128, 28], F32)
    dt16 = p.sb("dt16", [128, 4 * 16], F32)
    tmpA = p.sb("tmpA", [128, 16], F32)
    tmpB = p.sb("tmpB", [128, 16], F32)
    dtf = p.sb("dtf", [16, 512], F32)
    Dtc = p.sb("Dtc", [16, 128], F32)
    ADc = p.sb("ADc", [16, 128], F32)
    grow = p.sb("grow", [16, 128], F32)
    arg2 = p.sb("arg2", [16, 128], F32)
    dlf = p.sb("dlf", [16, 128], F32)
    ADt = p.sb("ADt", [128, 16], F32)
    dlt = p.sb("dlt", [128, 16], F32)
    w4 = p.sb("w4", [128, 10 * 4], F32)
    cbcol = p.sb("cbcol", [128, 10], F32)
    nwcol = p.sb("nwcol", [128, 8], F32)
    tail3 = p.sb("tail3", [128, 10 * 3], F32)
    xe = p.sb("xe", [128, 515], F32)
    cvt = p.sb("cvt", [128, 512], F32)
    rowb = p.sb("rowb", [10, 128], F32)
    Bb = p.sb("Bb", [128, 128], BF16)
    Cb = p.sb("Cb", [128, 128], BF16)
    Btb = p.sb("Btb", [128, 128], BF16)
    CBt = p.sb("CBt", [128, 128], BF16)
    gbf = p.sb("gbf", [128, 128], BF16)
    Mtb = p.sb("Mtb", [128, 128], BF16)
    Ctl = p.sb("Ctl", [128, 128], BF16)
    argsb = p.sb("argsb", [128, 128], F32)
    xtf = p.sb("xtf", [128, PT * 128], BF16)
    xtt = p.sb("xtt", [128, PT * 128], BF16)
    xds = p.sb("xds", [128, 64], BF16)
    Sst = p.sb("Sst", [128, 16 * 64], F32)
    Ssb = p.sb("Ssb", [128, 16 * 64], BF16)
    bsb = p.sb("bsb", [128, 16], F32)
    yc = cum
    szt = p.sb("szt", [128, 512], F32)
    sqb = p.sb("sqb", [128, 512], BF16)
    rrow = p.sb("rrow", [1, 2 * 512], F32)
    rmsbc = p.sb("rmsbc", [128, 512], F32)
    rdbc = p.sb("rdbc", [128, 512], F32)
    rdrow = p.sb("rdrow_s", [1, T], F32)
    gmbc = p.sb("gmbc", [128, 16], F32)
    dtbc = p.sb("dtbc", [128, 1], F32)
    Acol = p.sb("Acol", [16, 1], F32)
    Dxc = p.sb("Dxc", [128, 8], F32)
    carry = p.sb("carry", [128, 8], F32)
    trils = p.sb("trils", [128, 128], F32)
    esels = p.sb("esels", [16, 16 * 128], F32)
    eexps = p.sb("eexps", [16, PT * 128], F32)
    grow16 = p.sb("grow16", [1, 16], F32)
    d16 = p.sb("d16", [16, 1], F32)
    onesr = p.sb("onesr_s", [1, 128], F32)
    yall = xnf32
    rx = p.sb("rx", [128, D], F32)
    ry = p.sb("ry", [128, D], F32)

    psT = p.psum("psT", [128, 1024], BF16)
    psA = p.psum("psA", [128, 512], F32)
    psB = p.psum("psB", [128, 512], F32)
    psR = p.psum("psR", [128, 512], F32)
    psY0 = p.psum("psY0", [128, 512], F32)
    psY1 = p.psum("psY1", [128, 512], F32)
    psC = p.psum("psC", [128, 512], F32)

    s.dma("sp", lambda E: E.dma_start(trils[:, :], tril_h[:, :]), writes=["trils"])
    s.dma("sp", lambda E: E.dma_start(esels[:, :], esel_h[:, :]), writes=["esels"])
    s.dma("sp", lambda E: E.dma_start(eexps[:, :], eexp_h[:, :]), writes=["eexps"])
    s.dma("sp", lambda E: E.dma_start(rdrow[:, :], rd_h[:, :]), writes=["rdrow"])
    s.dma("sp", lambda E: E.dma_start(Acol[:, :], alog_h[:, :]), writes=["Acol"])
    s.dma("sp", lambda E: E.dma_start(d16[:, :], dd_h[:, :]), writes=["d16"])
    s.dma("sp", lambda E: E.dma_start(onesr[:, :], onesr_h[:, :]), writes=["onesr"])
    load_weight_kt(p, "woT", WoT, D, D, 0, D, woT, "woT")
    src_dt = WinT[:, 2304:2336].rearrange("(kt p) c -> p kt c", p=128)
    s.dma("sp", lambda E, src=src_dt: E.dma_start(
        bass.AP(wdt32, 0, [[wdt32.shape[1], 128], [32, PT], [1, 32]]), src),
        writes=["wdt32"])
    src_hp = WhT[0:1024, :].rearrange("(kt p) c -> p kt c", p=128)
    s.dma("sp", lambda E, src=src_hp: E.dma_start(
        bass.AP(hp32, 0, [[hp32.shape[1], 128], [40, 8], [1, 40]]), src),
        writes=["hp32"])
    s.dma("sp", lambda E: E.dma_start(hp32[0:32, 8 * 40:9 * 40], WhT[1024:1056, :]),
          writes=["hp32b"])
    s.dma("sp", lambda E: E.dma_start(
        bass.AP(w4, 0, [[w4.shape[1], 128], [4, 10], [1, 4]]),
        convw.rearrange("(ct p) s_ -> p ct s_", p=128)), writes=["w4"])
    s.dma("sp", lambda E: E.dma_start(
        rowb[:, :], convb.rearrange("o (ct x) -> (o ct) x", x=128)), writes=["rowb"])
    s.op("pe", lambda E: E.transpose(psC[0:128, 0:10], rowb[0:10, 0:128],
                                     p._cst["ident_f32"][0:10, 0:10]),
         reads=["rowb", "c:ident_f32"], writes=["psC"])
    s.op("dve", lambda E: E.tensor_copy(cbcol[:, :], psC[0:128, 0:10]),
         reads=["psC"], writes=["cbcol"])
    s.dma("sp", lambda E: E.dma_start(
        rowb[0:8, :], nw_h.rearrange("o (pt x) -> (o pt) x", x=128)), writes=["rowb"])
    s.op("pe", lambda E: E.transpose(psC[0:128, 16:24], rowb[0:8, 0:128],
                                     p._cst["ident_f32"][0:8, 0:8]),
         reads=["rowb", "c:ident_f32"], writes=["psC"])
    s.op("dve", lambda E: E.tensor_copy(nwcol[:, :], psC[0:128, 16:24]),
         reads=["psC"], writes=["nwcol"])
    s.op("act", lambda E: E.activation(Acol[:, :], Acol[:, :], AF.Exp),
         reads=["Acol"], writes=["Acol"])
    s.op("dve", lambda E: E.tensor_scalar(Acol[:, :], Acol[:, :], -1.0, None, ALU.mult),
         reads=["Acol"], writes=["Acol"])
    s.dma("sp", lambda E: E.dma_start(grow16[:, :], gam_h[:, :]), writes=["grow16"])
    s.dma("sp", lambda E: E.dma_start(rowb[0:1, 0:1], dtb_h[:, :]), writes=["rowb"])

    def bc16b(E):
        E.matmul(psC[:, 0:16], onesr[0:1, :], grow16[0:1, :], start=True, stop=True)
        return E.matmul(psC[:, 16:17], onesr[0:1, :], rowb[0:1, 0:1],
                        start=True, stop=True)
    s.op("pe", bc16b, reads=["grow16", "rowb", "onesr"], writes=["psC"])
    s.op("dve", lambda E: E.tensor_copy(gmbc[:, :], psC[:, 0:16]),
         reads=["psC"], writes=["gmbc"])
    s.op("dve", lambda E: E.tensor_copy(dtbc[:, :], psC[:, 16:17]),
         reads=["psC"], writes=["dtbc"])

    def dexp_mm(E):
        last = None
        for pt in range(PT):
            last = E.matmul(psC[:, 32 + pt:33 + pt], eexps[:, ts(pt)], d16[:, 0:1],
                            start=True, stop=True)
        return last
    s.op("pe", dexp_mm, reads=["eexps", "d16"], writes=["psC"])
    s.op("dve", lambda E: E.tensor_copy(Dxc[:, :], psC[:, 32:40]),
         reads=["psC"], writes=["Dxc"])

    NCH = T // 512
    for b in range(B_CORE):
        s.op("dve", lambda E: E.memset(carry[:, :], 0.0), writes=["carry"])
        s.op("dve", lambda E: E.memset(tail3[:, :], 0.0), writes=["tail3"])
        s.op("dve", lambda E: E.memset(Sst[:, :], 0.0), writes=["Sst"])
        s.op("dve", lambda E: E.memset(Ssb[:, :], 0.0), writes=["Ssb"])
        for c5 in range(NCH):
            t0 = c5 * 512
            for j4 in range(4):
                tt = c5 * 4 + j4

                def writer(j, j4=j4):
                    return (xnf[:, sl(j * 512, j4)], f"xnf:{j}")
                emit_lnt_tile(p, lnt, xin[b, ts(tt), :], psT, "psT", writer)
                s.op("pool", lambda E: E.tensor_scalar(
                    lnt["xt"][:, :], lnt["xt"][:, :], lnt["st"][:, 2:3],
                    lnt["st"][:, 7:8], ALU.subtract, ALU.mult),
                    reads=["lnt_xt", "lnt_mean_msq", "lnt_rstd"], writes=["lnt_xt"])
                for w2 in range(2):
                    def mmT32(E, w2=w2):
                        last = None
                        for j in range(4):
                            last = E.transpose(psA[:, ts(j)],
                                               lnt["xt"][:, sl(w2 * 512, j)],
                                               p._cst["ident_f32"][:, :])
                        return last
                    s.op("pe", mmT32, reads=["lnt_xt", "c:ident_f32"], writes=["psA"])
                    for j in range(4):
                        jj = w2 * 4 + j
                        s.op("dve", lambda E, jj=jj, j=j, j4=j4: E.tensor_copy(
                            xnf32[:, sl(jj * 512, j4)], psA[:, ts(j)]),
                            reads=["psA"], writes=[f"xnf32:{jj}"])
            load_weight_kt(p, "wz", WinT, D, 2336, 0, 1024, wbuf, "wbuf")
            for mt in range(PT):
                def mmz(E, mt=mt):
                    last = None
                    for kt in range(PT):
                        last = E.matmul(psR[:, 0:512], wbuf[:, sl(kt * 1024, mt)],
                                        xnf[:, ts(kt, 512)],
                                        start=(kt == 0), stop=(kt == PT - 1))
                    return last
                s.op("pe", mmz, reads=["wbuf"] + [f"xnf:{k}" for k in range(PT)],
                     writes=["psR"])
                s.op("dve", lambda E, mt=mt: E.tensor_copy(zc[:, ts(mt, 512)],
                                                           psR[:, 0:512]),
                     reads=["psR"], writes=[f"zc:{mt}"])
            load_weight_kt(p, "wx", WinT, D, 2336, 1024, 1280, wbuf, "wbuf")
            for mt in range(10):
                def mmx(E, mt=mt):
                    last = None
                    for kt in range(PT):
                        last = E.matmul(psR[:, 0:512], wbuf[:, sl(kt * 1280, mt)],
                                        xnf[:, ts(kt, 512)],
                                        start=(kt == 0), stop=(kt == PT - 1))
                    return last
                s.op("pe", mmx, reads=["wbuf"] + [f"xnf:{k}" for k in range(PT)],
                     writes=["psR"])
                s.op("dve", lambda E, mt=mt: E.tensor_copy(xbc[:, ts(mt, 512)],
                                                           psR[:, 0:512]),
                     reads=["psR"], writes=[f"xbc:{mt}"])
            for j4 in range(4):
                def mmdt(E, j4=j4):
                    last = None
                    for kt in range(PT):
                        last = E.matmul(psB[:, 0:32], xnf32[:, sl(kt * 512, j4)],
                                        wdt32[:, ts(kt, 32)],
                                        start=(kt == 0), stop=(kt == PT - 1))
                    return last
                s.op("pe", mmdt, reads=["wdt32"] + [f"xnf32:{k}" for k in range(PT)],
                     writes=["psB"])
                s.op("dve", lambda E, j4=j4: E.tensor_copy(dta[:, ts(j4, 32)],
                                                           psB[:, 0:32]),
                     reads=["psB"], writes=[f"dta:{j4}"])
                s.op("pe", lambda E, j4=j4: E.transpose(
                    psB[0:32, 128:256], dta[:, ts(j4, 32)],
                    p._cst["ident_f32"][:, :]),
                    reads=[f"dta:{j4}", "c:ident_f32"], writes=["psB"])
                s.op("dve", lambda E, j4=j4: E.tensor_copy(dtaf[0:32, ts(j4)],
                                                           psB[0:32, 128:256]),
                     reads=["psB"], writes=[f"dtaf:{j4}"])
            s.op("pe", lambda E, t0=t0: E.matmul(psA[:, 0:512], onesr[0:1, :],
                                                 rdrow[0:1, t0:t0 + 512],
                                                 start=True, stop=True),
                 reads=["rdrow", "onesr"], writes=["psA"])
            s.op("dve", lambda E: E.tensor_copy(rdbc[:, :], psA[:, 0:512]),
                 reads=["psA"], writes=["rdbc"])
            for pt in range(PT):
                s.op("dve", lambda E, pt=pt: E.tensor_tensor_scan(
                    cum[:, ts(pt, 512)], xnf32[:, ts(pt, 512)],
                    xnf32[:, ts(pt, 512)], carry[:, pt:pt + 1], ALU.add, ALU.bypass),
                    reads=[f"xnf32:{pt}", "carry"], writes=[f"cum:{pt}"])
                s.op("dve", lambda E, pt=pt: E.tensor_copy(
                    carry[:, pt:pt + 1], cum[:, pt * 512 + 511:pt * 512 + 512]),
                    reads=[f"cum:{pt}"], writes=["carry"])
                s.op("pool", lambda E, pt=pt: E.tensor_tensor(
                    cum[:, ts(pt, 512)], cum[:, ts(pt, 512)], rdbc[:, :], ALU.mult),
                    reads=[f"cum:{pt}", "rdbc"], writes=[f"cum:{pt}"])
                s.op("pool", lambda E, pt=pt: E.tensor_tensor(
                    cum[:, ts(pt, 512)], xnf32[:, ts(pt, 512)], cum[:, ts(pt, 512)],
                    ALU.subtract),
                    reads=[f"xnf32:{pt}", f"cum:{pt}"], writes=[f"cum:{pt}"])
            for j4 in range(4):
                def mmh(E, j4=j4):
                    for kt in range(PT):
                        E.matmul(psB[:, 0:40], cum[:, sl(kt * 512, j4)],
                                 hp32[:, kt * 40:(kt + 1) * 40],
                                 start=(kt == 0), stop=False)
                    return E.matmul(psB[:, 0:40], dtaf[0:32, ts(j4)],
                                    hp32[0:32, 8 * 40:9 * 40], start=False, stop=True)
                s.op("pe", mmh,
                     reads=[f"cum:{k}" for k in range(PT)] + [f"dtaf:{j4}", "hp32",
                                                              "hp32b"],
                     writes=["psB"])
                s.op("dve", lambda E, j4=j4: E.tensor_copy(hb[:, ts(j4, 40)],
                                                           psB[:, 0:40]),
                     reads=["psB"], writes=[f"hb:{j4}"])
            for j4 in range(4):
                s.op("dve", lambda E, j4=j4: E.tensor_copy(
                    hw_[:, :], hb[:, j4 * 40:j4 * 40 + 28]),
                    reads=[f"hb:{j4}"], writes=["hw_"])
                for k in range(12):
                    s.op("dve", lambda E: E.tensor_reduce(
                        tmpA[:, 0:1], hw_[:, :], AX.X, ALU.max),
                        reads=["hw_"], writes=["selmax"])
                    s.op("dve", lambda E: E.tensor_scalar(
                        ohb[:, :], hw_[:, :], tmpA[:, 0:1], None, ALU.is_equal),
                        reads=["hw_", "selmax"], writes=["ohb"])
                    s.op("dve", lambda E: E.tensor_tensor_scan(
                        csb[:, :], ohb[:, :], ohb[:, :], 0.0, ALU.add, ALU.bypass),
                        reads=["ohb"], writes=["csb"])
                    s.op("dve", lambda E: E.tensor_scalar(
                        csb[:, :], csb[:, :], 1.0, None, ALU.is_equal),
                        reads=["csb"], writes=["csb"])
                    s.op("dve", lambda E: E.tensor_tensor(
                        fst[:, :], ohb[:, :], csb[:, :], ALU.mult),
                        reads=["ohb", "csb"], writes=["fst"])
                    s.op("dve", lambda E, j4=j4: E.tensor_tensor(
                        gth[:, :], fst[:, :], dta[:, j4 * 32:j4 * 32 + 28], ALU.mult),
                        reads=["fst", f"dta:{j4}"], writes=["gth"])
                    s.op("dve", lambda E, j4=j4, k=k: E.tensor_reduce(
                        dt16[:, j4 * 16 + k:j4 * 16 + k + 1], gth[:, :], AX.X, ALU.add),
                        reads=["gth"], writes=[f"dt16:{j4}"])
                    s.op("dve", lambda E: E.scalar_tensor_tensor(
                        hw_[:, :], fst[:, :], NEG, hw_[:, :], ALU.mult, ALU.add),
                        reads=["fst", "hw_"], writes=["hw_"])
                s.op("dve", lambda E, j4=j4: E.tensor_copy(
                    dt16[:, j4 * 16 + 12:j4 * 16 + 16],
                    dta[:, j4 * 32 + 28:j4 * 32 + 32]),
                    reads=[f"dta:{j4}"], writes=[f"dt16:{j4}"])
                s.op("dve", lambda E, j4=j4: E.tensor_tensor(
                    tmpA[:, 0:12], gmbc[:, 0:12], hb[:, j4 * 40 + 28:j4 * 40 + 40],
                    ALU.mult),
                    reads=["gmbc", f"hb:{j4}"], writes=["tmpA"])
                s.op("dve", lambda E, j4=j4: E.tensor_tensor(
                    dt16[:, j4 * 16:j4 * 16 + 12], dt16[:, j4 * 16:j4 * 16 + 12],
                    tmpA[:, 0:12], ALU.add),
                    reads=["tmpA", f"dt16:{j4}"], writes=[f"dt16:{j4}"])
                s.op("dve", lambda E, j4=j4: E.tensor_scalar(
                    dt16[:, ts(j4, 16)], dt16[:, ts(j4, 16)], dtbc[:, 0:1], None,
                    ALU.add),
                    reads=["dtbc", f"dt16:{j4}"], writes=[f"dt16:{j4}"])
                s.op("act", lambda E, j4=j4: E.activation(
                    tmpA[:, :], dt16[:, ts(j4, 16)], AF.Abs),
                    reads=[f"dt16:{j4}"], writes=["tmpA"])
                s.op("act", lambda E: E.activation(
                    tmpA[:, :], tmpA[:, :], AF.Exp, scale=-1.0),
                    reads=["tmpA"], writes=["tmpA"])
                s.op("act", lambda E: E.activation(
                    tmpA[:, :], tmpA[:, :], AF.Ln, bias=one),
                    reads=["tmpA", "cpool"], writes=["tmpA"])
                s.op("act", lambda E, j4=j4: E.activation(
                    tmpB[:, :], dt16[:, ts(j4, 16)], AF.Relu),
                    reads=[f"dt16:{j4}"], writes=["tmpB"])
                s.op("dve", lambda E, j4=j4: E.tensor_tensor(
                    dt16[:, ts(j4, 16)], tmpA[:, :], tmpB[:, :], ALU.add),
                    reads=["tmpA", "tmpB"], writes=[f"dt16:{j4}"])
                s.op("pe", lambda E, j4=j4: E.transpose(
                    psB[0:16, 256:384], dt16[:, ts(j4, 16)],
                    p._cst["ident_f32"][:, :]),
                    reads=[f"dt16:{j4}", "c:ident_f32"], writes=["psB"])
                s.op("dve", lambda E, j4=j4: E.tensor_copy(
                    dtf[0:16, ts(j4)], psB[0:16, 256:384]),
                    reads=["psB"], writes=[f"dtf:{j4}"])
            for ct in range(10):
                s.op("dve", lambda E, ct=ct: E.tensor_copy(
                    xe[:, 0:3], tail3[:, ct * 3:(ct + 1) * 3]),
                    reads=["tail3"], writes=["xe"])
                s.op("dve", lambda E, ct=ct: E.tensor_copy(
                    xe[:, 3:515], xbc[:, ts(ct, 512)]),
                    reads=[f"xbc:{ct}"], writes=["xe"])
                s.op("dve", lambda E, ct=ct: E.tensor_copy(
                    tail3[:, ct * 3:(ct + 1) * 3], xe[:, 512:515]),
                    reads=["xe"], writes=["tail3"])
                s.op("dve", lambda E, ct=ct: E.tensor_scalar(
                    cvt[:, :], xe[:, 0:512], w4[:, ct * 4:ct * 4 + 1], None, ALU.mult),
                    reads=["xe", "w4"], writes=["cvt"])
                for ss in range(1, 4):
                    s.op("dve", lambda E, ct=ct, ss=ss: E.scalar_tensor_tensor(
                        cvt[:, :], xe[:, ss:ss + 512],
                        w4[:, ct * 4 + ss:ct * 4 + ss + 1],
                        cvt[:, :], ALU.mult, ALU.add),
                        reads=["xe", "w4", "cvt"], writes=["cvt"])
                s.op("act", lambda E, ct=ct: E.activation(
                    xbc[:, ts(ct, 512)], cvt[:, :], AF.Silu,
                    bias=cbcol[:, ct:ct + 1]),
                    reads=["cvt", "cbcol"], writes=[f"xbc:{ct}"])
            for cc in range(4):
                s.op("dve", lambda E, cc=cc: E.tensor_tensor_scan(
                    Dtc[0:16, :], dtf[0:16, ts(cc)], dtf[0:16, ts(cc)], 0.0,
                    ALU.add, ALU.bypass),
                    reads=[f"dtf:{cc}"], writes=["Dtc"])
                s.op("dve", lambda E: E.tensor_scalar(
                    ADc[0:16, :], Dtc[0:16, :], Acol[0:16, 0:1], None, ALU.mult),
                    reads=["Dtc", "Acol"], writes=["ADc"])
                s.op("act", lambda E: E.activation(grow[0:16, :], ADc[0:16, :], AF.Exp),
                     reads=["ADc"], writes=["grow"])
                s.op("dve", lambda E: E.tensor_scalar(
                    arg2[0:16, :], ADc[0:16, :], ADc[0:16, 127:128], -1.0,
                    ALU.subtract, ALU.mult),
                    reads=["ADc"], writes=["arg2"])
                s.op("act", lambda E: E.activation(dlf[0:16, :], arg2[0:16, :], AF.Exp),
                     reads=["arg2"], writes=["dlf"])

                def mmt16(E):
                    E.transpose(psB[0:128, 0:16], ADc[0:16, 0:128],
                                p._cst["ident_f32"][0:16, 0:16])
                    return E.transpose(psB[0:128, 16:32], dlf[0:16, 0:128],
                                       p._cst["ident_f32"][0:16, 0:16])
                s.op("pe", mmt16, reads=["ADc", "dlf", "c:ident_f32"], writes=["psB"])
                s.op("dve", lambda E: E.tensor_copy(ADt[:, :], psB[0:128, 0:16]),
                     reads=["psB"], writes=["ADt"])
                s.op("dve", lambda E: E.tensor_copy(dlt[:, :], psB[0:128, 16:32]),
                     reads=["psB"], writes=["dlt"])

                def mmbeta(E):
                    last = None
                    for h in range(16):
                        last = E.matmul(psC[:, 40 + h:41 + h], esels[:, ts(h)],
                                        grow[0:16, 127:128], start=True, stop=True)
                    return last
                s.op("pe", mmbeta, reads=["esels", "grow"], writes=["psC"])
                s.op("dve", lambda E: E.tensor_copy(bsb[:, :], psC[:, 40:56]),
                     reads=["psC"], writes=["bsb"])
                s.op("dve", lambda E, cc=cc: E.tensor_copy(
                    Bb[:, :], xbc[:, sl(8 * 512, cc)]),
                    reads=["xbc:8"], writes=["Bb"])
                s.op("dve", lambda E, cc=cc: E.tensor_copy(
                    Cb[:, :], xbc[:, sl(9 * 512, cc)]),
                    reads=["xbc:9"], writes=["Cb"])
                s.op("pe", lambda E: E.transpose(psT[:, 0:128], Bb[:, :],
                                                 p._cst["ident_bf"][:, :]),
                     reads=["Bb", "c:ident_bf"], writes=["psT"])
                s.op("dve", lambda E: E.tensor_copy(Btb[:, :], psT[:, 0:128]),
                     reads=["psT"], writes=["Btb"])
                s.op("pe", lambda E: E.matmul(psB[:, 0:128], Bb[:, :], Cb[:, :],
                                              start=True, stop=True),
                     reads=["Bb", "Cb"], writes=["psB"])
                s.op("act", lambda E: E.activation(CBt[:, :], psB[:, 0:128], AF.Copy),
                     reads=["psB"], writes=["CBt"])
                for pt in range(PT):
                    s.op("pe", lambda E, pt=pt, cc=cc: E.matmul(
                        psA[:, ts(pt % 4)], eexps[:, ts(pt)], dtf[0:16, ts(cc)],
                        start=True, stop=True),
                        reads=["eexps", f"dtf:{cc}"], writes=["psA"])
                    s.op("dve", lambda E, pt=pt, cc=cc: E.tensor_tensor(
                        xtf[:, ts(pt)], xbc[:, sl(pt * 512, cc)], psA[:, ts(pt % 4)],
                        ALU.mult),
                        reads=[f"xbc:{pt}", "psA"], writes=[f"xtf:{pt}"])

                def mmtx(E):
                    last = None
                    for pt in range(PT):
                        last = E.transpose(psT[:, ts(pt)], xtf[:, ts(pt)],
                                           p._cst["ident_bf"][:, :])
                    return last
                s.op("pe", mmtx, reads=[f"xtf:{pt}" for pt in range(PT)] +
                     ["c:ident_bf"], writes=["psT"])
                for pt in range(PT):
                    s.op("act", lambda E, pt=pt: E.activation(
                        xtt[:, ts(pt)], psT[:, ts(pt)], AF.Copy),
                        reads=["psT"], writes=[f"xtt:{pt}"])
                for h in range(16):
                    pt = h // 2
                    po = (h % 2) * 64
                    psY = psY0 if pt < 4 else psY1
                    psYk = "psY0" if pt < 4 else "psY1"
                    col = (pt % 4) * 128
                    s.op("pe", lambda E, h=h: E.matmul(
                        psA[:, 0:128], esels[:, ts(h)], ADc[0:16, :],
                        start=True, stop=True),
                        reads=["esels", "ADc"], writes=["psA"])
                    s.op("dve", lambda E, h=h: E.scalar_tensor_tensor(
                        argsb[:, :], psA[:, 0:128], ADt[:, h:h + 1], trils[:, :],
                        ALU.subtract, ALU.add),
                        reads=["psA", "ADt", "trils"], writes=["argsb"])
                    s.op("act", lambda E: E.activation(gbf[:, :], argsb[:, :], AF.Exp),
                         reads=["argsb"], writes=["gbf"])
                    s.op("dve", lambda E: E.tensor_tensor(Mtb[:, :], gbf[:, :],
                                                          CBt[:, :], ALU.mult),
                         reads=["gbf", "CBt"], writes=["Mtb"])
                    s.op("pe", lambda E, h=h: E.matmul(
                        psA[:, 128:256], esels[:, ts(h)], grow[0:16, :],
                        start=True, stop=True),
                        reads=["esels", "grow"], writes=["psA"])
                    s.op("dve", lambda E: E.tensor_tensor(
                        Ctl[:, :], Cb[:, :], psA[:, 128:256], ALU.mult),
                        reads=["Cb", "psA"], writes=["Ctl"])

                    def mmy(E, h=h, po=po, col=col, psY=psY):
                        E.matmul(psY[po:po + 64, col:col + 128],
                                 xtt[:, h * 64:(h + 1) * 64], Mtb[:, :],
                                 start=True, stop=False)
                        return E.matmul(psY[po:po + 64, col:col + 128],
                                        Ssb[:, h * 64:(h + 1) * 64], Ctl[:, :],
                                        start=False, stop=True)
                    s.op("pe", mmy, reads=[f"xtt:{h // 2}", "Mtb", "Ssb", "Ctl"],
                         writes=[psYk])
                    s.op("dve", lambda E, h=h: E.tensor_scalar(
                        xds[:, :], xtt[:, h * 64:(h + 1) * 64], dlt[:, h:h + 1],
                        None, ALU.mult),
                        reads=[f"xtt:{h // 2}", "dlt"], writes=["xds"])
                    s.op("pe", lambda E: E.matmul(psC[:, 448:512], Btb[:, :],
                                                  xds[:, :], start=True, stop=True),
                         reads=["Btb", "xds"], writes=["psC"])
                    s.op("dve", lambda E, h=h: E.scalar_tensor_tensor(
                        Sst[:, ts(h, 64)], Sst[:, ts(h, 64)], bsb[:, h:h + 1],
                        psC[:, 448:512], ALU.mult, ALU.add),
                        reads=["Sst", "bsb", "psC"], writes=["Sst"])
                for pt in range(PT):
                    psY = psY0 if pt < 4 else psY1
                    psYk = "psY0" if pt < 4 else "psY1"
                    col = (pt % 4) * 128
                    s.op("dve", lambda E, pt=pt, cc=cc, col=col, psY=psY:
                         E.scalar_tensor_tensor(
                             yc[:, sl(pt * 512, cc)], xbc[:, sl(pt * 512, cc)],
                             Dxc[:, pt:pt + 1], psY[:, col:col + 128],
                             ALU.mult, ALU.add),
                         reads=[f"xbc:{pt}", "Dxc", psYk], writes=[f"cum:{pt}"])
                s.op("dve", lambda E: E.tensor_copy(Ssb[:, :], Sst[:, :]),
                     reads=["Sst"], writes=["Ssb"])
            for pt in range(PT):
                s.op("act", lambda E, pt=pt: E.activation(
                    szt[:, :], zc[:, ts(pt, 512)], AF.Silu),
                    reads=[f"zc:{pt}"], writes=["szt"])
                s.op("dve", lambda E, pt=pt: E.tensor_tensor(
                    yc[:, ts(pt, 512)], yc[:, ts(pt, 512)], szt[:, :], ALU.mult),
                    reads=[f"cum:{pt}", "szt"], writes=[f"cum:{pt}"])
                s.op("act", lambda E, pt=pt: E.activation(
                    sqb[:, :], yc[:, ts(pt, 512)], AF.Square),
                    reads=[f"cum:{pt}"], writes=["sqb"])
                s.op("pe", lambda E, pt=pt: E.matmul(
                    psB[0:1, 0:512], p._cst["ones_col_bf"][:, 0:1], sqb[:, :],
                    start=(pt == 0), stop=(pt == PT - 1)),
                    reads=["sqb", "c:ones_col_bf"], writes=["psB"])
            s.op("dve", lambda E: E.tensor_scalar(
                rrow[0:1, 0:512], psB[0:1, 0:512], 1.0 / D, None, ALU.mult),
                reads=["psB"], writes=["rrow"])
            s.op("act", lambda E: E.activation(
                rrow[0:1, 512:1024], rrow[0:1, 0:512], AF.Sqrt,
                bias=p.constv(1e-5)[0:1, 0:1]),
                reads=["rrow", "cpool"], writes=["rrow2"])
            s.op("dve", lambda E: E.reciprocal(rrow[0:1, 0:512], rrow[0:1, 512:1024]),
                 reads=["rrow2"], writes=["rrow"])
            s.op("pe", lambda E: E.matmul(psA[:, 0:512], onesr[0:1, :],
                                          rrow[0:1, 0:512], start=True, stop=True),
                 reads=["rrow", "onesr"], writes=["psA"])
            s.op("dve", lambda E: E.tensor_copy(rmsbc[:, :], psA[:, 0:512]),
                 reads=["psA"], writes=["rmsbc"])
            for pt in range(PT):
                s.op("dve", lambda E, pt=pt: E.scalar_tensor_tensor(
                    xnf[:, ts(pt, 512)], yc[:, ts(pt, 512)], nwcol[:, pt:pt + 1],
                    rmsbc[:, :], ALU.mult, ALU.mult),
                    reads=[f"cum:{pt}", "nwcol", "rmsbc"], writes=[f"xnf:{pt}"])
            for mt in range(PT):
                def mmo(E, mt=mt):
                    last = None
                    for kt in range(PT):
                        last = E.matmul(psR[:, 0:512], woT[:, sl(kt * 1024, mt)],
                                        xnf[:, ts(kt, 512)],
                                        start=(kt == 0), stop=(kt == PT - 1))
                    return last
                s.op("pe", mmo, reads=["woT"] + [f"xnf:{k}" for k in range(PT)],
                     writes=["psR"])
                s.op("dve", lambda E, mt=mt: E.tensor_copy(yall[:, ts(mt, 512)],
                                                           psR[:, 0:512]),
                     reads=["psR"], writes=[f"xnf32:{mt}"])
            emit_group_residual(p, b, c5, yall, xin, xout, psA, "psA", (rx, ry),
                                ykey="xnf32")
    if fused:
        p.s.barrier()
    _sc.__exit__(None, None, None)
    return p





def build_moe(p=None, io=None, W=None):
    fused = p is not None
    if not fused:
        p = Prog("moe")
    _sc = p.scope() if fused else contextlib.nullcontext()
    _sc.__enter__()
    s = p.s
    p._cst = load_consts(p, ["ident_bf", "ident_f32"])
    xin = io[0] if fused else p.dram_in("xin", shape=[B_CORE, T, D], dtype=F32)
    xout = io[1] if fused else p.dram_out("xout", [B_CORE, T, D], F32)
    def gw(name, shape):
        return W[name] if W else p.dram_in(name, shape=shape, dtype=F32)
    Ws1 = gw("Ws1", [D, 4096])
    bs1 = gw("bs1", [1, 4096])
    Ws2 = gw("Ws2", [2048, D])
    bs2 = gw("bs2", [1, D])
    We1 = gw("We1", [12, D, 6144])
    be1 = gw("be1", [12, 6144])
    We2 = gw("We2", [12, 3072, D])
    be2 = gw("be2", [12, D])
    Wr = gw("Wr", [D, 12])
    br = gw("br", [1, 12])
    esel = np.zeros((16, 16 * 128), np.float32)
    for h in range(16):
        esel[h, h * 128:(h + 1) * 128] = 1.0
    esel_h = p.dram_in("esel", esel, dtype=F32)
    onesr_h = p.dram_in("onesr", np.ones((1, 128), np.float32), dtype=F32)

    lnt = alloc_lnt(p)
    xnf = p.sb("xnf", [128, PT * 512], BF16)
    xnf32 = p.sb("xnf32", [128, PT * 512], F32)
    actb = p.sb("actb", [128, 24 * 512], BF16)
    wA = p.sb("wA", [128, PT * 512], BF16)
    wG = p.sb("wG", [128, PT * 512], BF16)
    w2b = p.sb("w2b", [128, 24 * 128], BF16)
    wr32 = p.sb("wr32", [128, PT * 12], F32)
    logt = p.sb("logt", [128, 4 * 12], F32)
    lwk = p.sb("lwk", [128, 12], F32)
    oh1 = p.sb("oh1", [128, 12], F32)
    oh2 = p.sb("oh2", [128, 12], F32)
    cs2 = p.sb("cs2", [128, 12], F32)
    m12 = p.sb("m12", [128, 4], F32)
    wtm = p.sb("wtm", [128, 4 * 12], F32)
    wfm = p.sb("wfm", [12, 512], F32)
    webc = p.sb("webc", [128, 512], F32)
    glb = p.sb("glb", [128, 512], BF16)
    t1b = p.sb("t1b", [128, 512], F32)
    yacc = p.sb("yacc", [128, PT * 512], F32)
    bscol = p.sb("bscol", [128, 32 + 8 + 8], F32)
    be1c = p.sb("be1c", [128, 12 * 48], F32)
    be2c = p.sb("be2c", [128, 12 * 8], F32)
    brbc = p.sb("brbc", [128, 12], F32)
    rowb = p.sb("rowb", [48, 128], F32)
    esels = p.sb("esels", [16, 16 * 128], F32)
    onesr = p.sb("onesr_s", [1, 128], F32)
    rx = p.sb("rx", [128, D], F32)
    ry = p.sb("ry", [128, D], F32)

    psT = p.psum("psT", [128, 1024], BF16)
    psA = p.psum("psA", [128, 512], F32)
    psB = p.psum("psB", [128, 512], F32)
    psR = p.psum("psR", [128, 512], F32)
    psC = p.psum("psC", [128, 512], F32)

    s.dma("sp", lambda E: E.dma_start(esels[:, :], esel_h[:, :]), writes=["esels"])
    s.dma("sp", lambda E: E.dma_start(onesr[:, :], onesr_h[:, :]), writes=["onesr"])
    src_wr = Wr.rearrange("(kt p) c -> p kt c", p=128)
    s.dma("sp", lambda E, src=src_wr: E.dma_start(
        bass.AP(wr32, 0, [[wr32.shape[1], 128], [12, PT], [1, 12]]), src),
        writes=["wr32"])

    def coltab(hbm_row_ap, nrow, dst, dstk, col0):
        s.dma("sp", lambda E: E.dma_start(
            rowb[0:nrow, :], hbm_row_ap.rearrange("o (r x) -> (o r) x", x=128)),
            writes=["rowb"])
        s.op("pe", lambda E: E.transpose(psC[0:128, 0:nrow], rowb[0:nrow, 0:128],
                                         p._cst["ident_f32"][0:nrow, 0:nrow]),
             reads=["rowb", "c:ident_f32"], writes=["psC"])
        s.op("dve", lambda E: E.tensor_copy(dst[:, col0:col0 + nrow],
                                            psC[0:128, 0:nrow]),
             reads=["psC"], writes=[dstk])

    coltab(bs1, 32, bscol, "bscol", 0)
    coltab(bs2, 8, bscol, "bscol", 32)
    for e in range(12):
        coltab(be1[e:e + 1, :], 48, be1c, "be1c", e * 48)
        coltab(be2[e:e + 1, :], 8, be2c, "be2c", e * 8)
    s.dma("sp", lambda E: E.dma_start(rowb[0:1, 0:12], br[:, :]), writes=["rowb"])
    s.op("pe", lambda E: E.matmul(psC[:, 0:12], onesr[0:1, :], rowb[0:1, 0:12],
                                  start=True, stop=True),
         reads=["rowb", "onesr"], writes=["psC"])
    s.op("dve", lambda E: E.tensor_copy(brbc[:, :], psC[:, 0:12]),
         reads=["psC"], writes=["brbc"])

    def ldw_cols(hbm, col0, buf, bk, kt_n=PT, ncols=512):
        src = hbm[:, col0:col0 + ncols].rearrange("(kt p) c -> p kt c", p=128)
        dst = bass.AP(buf, 0, [[buf.shape[1], 128], [ncols, kt_n], [1, ncols]])
        s.dma("pool", lambda E, src=src, dst=dst: E.dma_start(dst, src), writes=[bk])

    def ldw_lhs(hbm, mt, buf, bk, kt_n):
        src = hbm[:, ts(mt)].rearrange("(kt p) c -> p kt c", p=128)
        dst = bass.AP(buf, 0, [[buf.shape[1], 128], [128, kt_n], [1, 128]])
        s.dma("pool", lambda E, src=src, dst=dst: E.dma_start(dst, src), writes=[bk])

    NCHU = B_CORE * T // 512
    for c in range(NCHU):
        b = c // (T // 512)
        g = c % (T // 512)
        for j4 in range(4):
            tt = g * 4 + j4

            def writer(j, j4=j4):
                return (xnf[:, sl(j * 512, j4)], f"xnf:{j}")
            emit_lnt_tile(p, lnt, xin[b, ts(tt), :], psT, "psT", writer)
            s.op("pool", lambda E: E.tensor_scalar(
                lnt["xt"][:, :], lnt["xt"][:, :], lnt["st"][:, 2:3],
                lnt["st"][:, 7:8], ALU.subtract, ALU.mult),
                reads=["lnt_xt", "lnt_mean_msq", "lnt_rstd"], writes=["lnt_xt"])
            for w2 in range(2):
                def mmT32(E, w2=w2):
                    last = None
                    for j in range(4):
                        last = E.transpose(psA[:, ts(j)],
                                           lnt["xt"][:, sl(w2 * 512, j)],
                                           p._cst["ident_f32"][:, :])
                    return last
                s.op("pe", mmT32, reads=["lnt_xt", "c:ident_f32"], writes=["psA"])
                for j in range(4):
                    jj = w2 * 4 + j
                    s.op("dve", lambda E, jj=jj, j=j, j4=j4: E.tensor_copy(
                        xnf32[:, sl(jj * 512, j4)], psA[:, ts(j)]),
                        reads=["psA"], writes=[f"xnf32:{jj}"])
        for j4 in range(4):
            def mmr(E, j4=j4):
                last = None
                for kt in range(PT):
                    last = E.matmul(psB[:, 0:12], xnf32[:, sl(kt * 512, j4)],
                                    wr32[:, ts(kt, 12)],
                                    start=(kt == 0), stop=(kt == PT - 1))
                return last
            s.op("pe", mmr, reads=["wr32"] + [f"xnf32:{k}" for k in range(PT)],
                 writes=["psB"])
            s.op("dve", lambda E, j4=j4: E.tensor_tensor(
                logt[:, ts(j4, 12)], psB[:, 0:12], brbc[:, :], ALU.add),
                reads=["psB", "brbc"], writes=[f"logt:{j4}"])
            s.op("dve", lambda E, j4=j4: E.tensor_copy(lwk[:, :], logt[:, ts(j4, 12)]),
                 reads=[f"logt:{j4}"], writes=["lwk"])
            s.op("dve", lambda E, j4=j4: E.tensor_reduce(
                m12[:, 0:1], lwk[:, :], AX.X, ALU.max),
                reads=["lwk"], writes=["m12"])
            s.op("dve", lambda E: E.tensor_scalar(
                oh1[:, :], lwk[:, :], m12[:, 0:1], None, ALU.is_equal),
                reads=["lwk", "m12"], writes=["oh1"])
            s.op("dve", lambda E: E.tensor_tensor_scan(
                cs2[:, :], oh1[:, :], oh1[:, :], 0.0, ALU.add, ALU.bypass),
                reads=["oh1"], writes=["cs2"])
            s.op("dve", lambda E: E.tensor_scalar(
                cs2[:, :], cs2[:, :], 1.0, None, ALU.is_equal),
                reads=["cs2"], writes=["cs2"])
            s.op("dve", lambda E: E.tensor_tensor(
                oh1[:, :], oh1[:, :], cs2[:, :], ALU.mult),
                reads=["oh1", "cs2"], writes=["oh1"])
            s.op("dve", lambda E: E.scalar_tensor_tensor(
                lwk[:, :], oh1[:, :], NEG, lwk[:, :], ALU.mult, ALU.add),
                reads=["oh1", "lwk"], writes=["lwk"])
            s.op("dve", lambda E: E.tensor_reduce(
                m12[:, 1:2], lwk[:, :], AX.X, ALU.max),
                reads=["lwk"], writes=["m12"])
            s.op("dve", lambda E: E.tensor_scalar(
                oh2[:, :], lwk[:, :], m12[:, 1:2], None, ALU.is_equal),
                reads=["lwk", "m12"], writes=["oh2"])
            s.op("dve", lambda E: E.tensor_tensor_scan(
                cs2[:, :], oh2[:, :], oh2[:, :], 0.0, ALU.add, ALU.bypass),
                reads=["oh2"], writes=["cs2"])
            s.op("dve", lambda E: E.tensor_scalar(
                cs2[:, :], cs2[:, :], 1.0, None, ALU.is_equal),
                reads=["cs2"], writes=["cs2"])
            s.op("dve", lambda E: E.tensor_tensor(
                oh2[:, :], oh2[:, :], cs2[:, :], ALU.mult),
                reads=["oh2", "cs2"], writes=["oh2"])
            s.op("dve", lambda E: E.tensor_tensor(
                m12[:, 2:3], m12[:, 0:1], m12[:, 1:2], ALU.subtract),
                reads=["m12"], writes=["m12"])
            s.op("act", lambda E: E.activation(
                m12[:, 0:1], m12[:, 2:3], AF.Sigmoid, scale=1.0 / 1.5),
                reads=["m12"], writes=["m12"])
            s.op("act", lambda E: E.activation(
                m12[:, 1:2], m12[:, 2:3], AF.Sigmoid, scale=-1.0 / 1.5),
                reads=["m12"], writes=["m12"])
            s.op("dve", lambda E, j4=j4: E.tensor_scalar(
                wtm[:, ts(j4, 12)], oh1[:, :], m12[:, 0:1], None, ALU.mult),
                reads=["oh1", "m12"], writes=[f"wtm:{j4}"])
            s.op("dve", lambda E, j4=j4: E.scalar_tensor_tensor(
                wtm[:, ts(j4, 12)], oh2[:, :], m12[:, 1:2], wtm[:, ts(j4, 12)],
                ALU.mult, ALU.add),
                reads=["oh2", "m12", f"wtm:{j4}"], writes=[f"wtm:{j4}"])
            s.op("pe", lambda E, j4=j4: E.transpose(
                psB[0:12, 128:256], wtm[:, ts(j4, 12)], p._cst["ident_f32"][:, :]),
                reads=[f"wtm:{j4}", "c:ident_f32"], writes=["psB"])
            s.op("dve", lambda E, j4=j4: E.tensor_copy(
                wfm[0:12, ts(j4)], psB[0:12, 128:256]),
                reads=["psB"], writes=["wfm"])
        for mt2 in range(4):
            ldw_cols(Ws1, mt2 * 512, wA, "wA")
            ldw_cols(Ws1, 2048 + mt2 * 512, wG, "wG")
            for m4 in range(4):
                mtg = mt2 * 4 + m4
                def mma(E, m4=m4):
                    last = None
                    for kt in range(PT):
                        last = E.matmul(psA[:, 0:512], wA[:, sl(kt * 512, m4)],
                                        xnf[:, ts(kt, 512)],
                                        start=(kt == 0), stop=(kt == PT - 1))
                    return last
                def mmg(E, m4=m4):
                    last = None
                    for kt in range(PT):
                        last = E.matmul(psB[:, 0:512], wG[:, sl(kt * 512, m4)],
                                        xnf[:, ts(kt, 512)],
                                        start=(kt == 0), stop=(kt == PT - 1))
                    return last
                s.op("pe", mma, reads=["wA"] + [f"xnf:{k}" for k in range(PT)],
                     writes=["psA"])
                s.op("pe", mmg, reads=["wG"] + [f"xnf:{k}" for k in range(PT)],
                     writes=["psB"])
                s.op("act", lambda E, mtg=mtg: E.activation(
                    glb[:, :], psB[:, 0:512], AF.Gelu,
                    bias=bscol[:, 16 + mtg:17 + mtg]),
                    reads=["psB", "bscol"], writes=["glb"])
                s.op("dve", lambda E, mtg=mtg: E.scalar_tensor_tensor(
                    actb[:, ts(mtg, 512)], psA[:, 0:512], bscol[:, mtg:mtg + 1],
                    glb[:, :], ALU.add, ALU.mult),
                    reads=["psA", "bscol", "glb"], writes=[f"actb:{mtg}"])
        for mt in range(PT):
            ldw_lhs(Ws2, mt, w2b, "w2b", 16)
            def mms2(E, mt=mt):
                last = None
                for kt in range(16):
                    last = E.matmul(psR[:, 0:512], w2b[:, ts(kt)],
                                    actb[:, ts(kt, 512)],
                                    start=(kt == 0), stop=(kt == 15))
                return last
            s.op("pe", mms2, reads=["w2b"] + [f"actb:{k}" for k in range(16)],
                 writes=["psR"])
            s.op("dve", lambda E, mt=mt: E.tensor_scalar(
                yacc[:, ts(mt, 512)], psR[:, 0:512], bscol[:, 32 + mt:33 + mt],
                None, ALU.add),
                reads=["psR", "bscol"], writes=[f"yacc:{mt}"])
        for e in range(12):
            s.op("pe", lambda E, e=e: E.matmul(
                psC[:, 0:512], esels[0:12, ts(e)], wfm[0:12, :],
                start=True, stop=True),
                reads=["esels", "wfm"], writes=["psC"])
            s.op("dve", lambda E: E.tensor_copy(webc[:, :], psC[:, 0:512]),
                 reads=["psC"], writes=["webc"])
            for j in range(6):
                ldw_cols(We1[e], j * 512, wA, "wA")
                ldw_cols(We1[e], 3072 + j * 512, wG, "wG")
                for m4 in range(4):
                    mtg = j * 4 + m4
                    def mma(E, m4=m4):
                        last = None
                        for kt in range(PT):
                            last = E.matmul(psA[:, 0:512], wA[:, sl(kt * 512, m4)],
                                            xnf[:, ts(kt, 512)],
                                            start=(kt == 0), stop=(kt == PT - 1))
                        return last
                    def mmg(E, m4=m4):
                        last = None
                        for kt in range(PT):
                            last = E.matmul(psB[:, 0:512], wG[:, sl(kt * 512, m4)],
                                            xnf[:, ts(kt, 512)],
                                            start=(kt == 0), stop=(kt == PT - 1))
                        return last
                    s.op("pe", mma, reads=["wA"] + [f"xnf:{k}" for k in range(PT)],
                         writes=["psA"])
                    s.op("pe", mmg, reads=["wG"] + [f"xnf:{k}" for k in range(PT)],
                         writes=["psB"])
                    s.op("act", lambda E, e=e, mtg=mtg: E.activation(
                        glb[:, :], psB[:, 0:512], AF.Gelu,
                        bias=be1c[:, e * 48 + 24 + mtg:e * 48 + 25 + mtg]),
                        reads=["psB", "be1c"], writes=["glb"])
                    s.op("dve", lambda E, e=e, mtg=mtg: E.scalar_tensor_tensor(
                        actb[:, ts(mtg, 512)], psA[:, 0:512],
                        be1c[:, e * 48 + mtg:e * 48 + mtg + 1],
                        glb[:, :], ALU.add, ALU.mult),
                        reads=["psA", "be1c", "glb"], writes=[f"actb:{mtg}"])
            for mt in range(PT):
                ldw_lhs(We2[e], mt, w2b, "w2b", 24)
                def mme2(E, mt=mt):
                    last = None
                    for kt in range(24):
                        last = E.matmul(psR[:, 0:512], w2b[:, ts(kt)],
                                        actb[:, ts(kt, 512)],
                                        start=(kt == 0), stop=(kt == 23))
                    return last
                s.op("pe", mme2, reads=["w2b"] + [f"actb:{k}" for k in range(24)],
                     writes=["psR"])
                s.op("dve", lambda E, e=e, mt=mt: E.scalar_tensor_tensor(
                    t1b[:, :], psR[:, 0:512], be2c[:, e * 8 + mt:e * 8 + mt + 1],
                    webc[:, :], ALU.add, ALU.mult),
                    reads=["psR", "be2c", "webc"], writes=["t1b"])
                s.op("dve", lambda E, mt=mt: E.tensor_tensor(
                    yacc[:, ts(mt, 512)], yacc[:, ts(mt, 512)], t1b[:, :], ALU.add),
                    reads=[f"yacc:{mt}", "t1b"], writes=[f"yacc:{mt}"])
        emit_group_residual(p, b, g, yacc, xin, xout, psA, "psA", (rx, ry),
                            ykey="yacc")
    if fused:
        p.s.barrier()
    _sc.__exit__(None, None, None)
    return p





def build_attn_b1(p=None, io=None, W=None):
    fused = p is not None
    if not fused:
        p = Prog("attnb1")
    _sc = p.scope() if fused else contextlib.nullcontext()
    _sc.__enter__()
    s = p.s
    p._cst = load_consts(p, ["ident_bf", "ident_f32", "cmask4_bf"])
    xin = io[0] if fused else p.dram_in("xin", shape=[B_CORE, T, D], dtype=F32)
    xout = io[1] if fused else p.dram_out("xout", [B_CORE, T, D], F32)
    def gw(name, shape):
        return W[name] if W else p.dram_in(name, shape=shape, dtype=F32)
    WqkvT = gw("WqkvT", [D, 3 * D])
    bqkv = gw("bqkv", [1, 3 * D])
    WqaT = gw("WqaT", [D, D])
    bqa = gw("bqa", [1, D])
    WkaT = gw("WkaT", [D, D])
    bka = gw("bka", [1, D])
    WdT = gw("WdT", [D, D])
    bd = gw("bd", [1, D])
    RD = 16
    inv = 1.0 / (10000.0 ** (np.arange(0, RD, 2, dtype=np.float64) / RD))
    tpos = np.arange(T, dtype=np.float64)
    fr = np.outer(inv, tpos)
    emb = np.concatenate([fr, fr], axis=0)
    cosT_h = p.dram_in("cosT", np.cos(emb).astype(np.float32), dtype=F32)
    sinT_h = p.dram_in("sinT", np.sin(emb).astype(np.float32), dtype=F32)
    Rm = np.zeros((128, 128), np.float32)
    for base in (0, 64):
        for i in range(8):
            Rm[base + i + 8, base + i] = -1.0
            Rm[base + i, base + i + 8] = 1.0
    R_h = p.dram_in("Rm", Rm, dtype=F32)
    i_ = np.arange(128)[:, None]
    k_ = np.arange(256)[None, :]
    wm = np.where((k_ >= 113 + i_) & (k_ <= 128 + i_), 0.0, NEG).astype(np.float32)
    wmask_h = p.dram_in("wmask", wm, dtype=F32)
    k0 = np.arange(128)[None, :]
    wm0 = np.where((k0 <= i_) & (k0 >= i_ - 15), 0.0, NEG).astype(np.float32)
    wmask0_h = p.dram_in("wmask0", wm0, dtype=F32)

    SCALE = 1.0 / 8.0
    lnt = alloc_lnt(p)
    xnfB = p.sb("xnfB", [128, PT * T], BF16)
    KfB = p.sb("KfB", [128, PT * T], BF16)
    VtB = p.sb("VtB", [128, NT * D], BF16)
    QaB = p.sb("QaB", [128, PT * T], BF16)
    wb = p.sb("wb", [128, PT * D], BF16)
    Rbf = p.sb("Rbf", [128, 128], BF16)
    vbbc = p.sb("vbbc", [128, D], BF16)
    bqkv_bf = p.sb("bqkv_bf", [1, D], BF16)
    btab = p.sb("btab", [128, 40], F32)
    rowb = p.sb("rowb", [8, 128], F32)
    wmsk = p.sb("wmsk", [128, 256], F32)
    wmsk0 = p.sb("wmsk0", [128, 128], F32)
    Ssb = p.sb("Ssb", [128, T], F32)
    rpq = Ssb[0:16, 0:512]
    rpr = Ssb[0:16, 512:1024]
    u1b = Ssb[0:16, 1024:1536]
    u2b = Ssb[0:16, 1536:2048]
    Pbf = p.sb("Pbf", [128, T], BF16)
    PTsb = p.sb("PTsb", [128, 512], BF16)
    Ot = p.sb("Ot", [128, 128], BF16)
    Of = p.sb("Of", [128, PT * 512], BF16)
    yall = p.sb("yall", [128, PT * 512], F32)
    cosS = yall[0:16, 0:T]
    sinS = yall[0:16, T:2 * T]
    rx = p.sb("rx", [128, D], F32)
    ry = p.sb("ry", [128, D], F32)
    mx = p.sb("mx", [128, 8], F32)
    den = p.sb("den", [128, 8], F32)
    rcp = p.sb("rcp", [128, 16], F32)
    nmx = p.sb("nmx", [128, 1], F32)
    onebf = p.sb("onebf", [1, 512], BF16)

    psS = [p.psum("psS0", [128, 512], F32), p.psum("psS1", [128, 512], F32)]
    psO = p.psum("psO", [128, 1024], F32)
    psT = p.psum("psT", [128, 1024], BF16)
    psP = [p.psum("psP0", [128, 512], F32), p.psum("psP1", [128, 512], F32)]
    psR = p.psum("psR", [128, 512], F32)

    s.dma("pool", lambda E: E.dma_start(Rbf[:, :], R_h[:, :]), writes=["Rbf"])
    s.dma("sp", lambda E: E.dma_start(wmsk[:, :], wmask_h[:, :]), writes=["wmsk"])
    s.dma("sp", lambda E: E.dma_start(wmsk0[:, :], wmask0_h[:, :]), writes=["wmsk0"])
    s.dma("pool", lambda E: E.dma_start(bqkv_bf[:, :], bqkv[:, 2 * D:3 * D]), writes=["bqkv_bf"])

    def coltab(hbm_row_ap, col0):
        s.dma("sp", lambda E, hbm_row_ap=hbm_row_ap: E.dma_start(
            rowb[0:8, :], hbm_row_ap.rearrange("o (r x) -> (o r) x", x=128)),
            writes=["rowb"])
        s.op("pe", lambda E: E.transpose(psR[0:128, 0:8], rowb[0:8, 0:128],
                                         p._cst["ident_f32"][0:8, 0:8]),
             reads=["rowb", "c:ident_f32"], writes=["psR"])
        s.op("dve", lambda E, col0=col0: E.tensor_copy(btab[:, col0:col0 + 8],
                                                       psR[0:128, 0:8]),
             reads=["psR"], writes=["btab"])

    coltab(bqkv[:, 0:D], 0)
    coltab(bqkv[:, D:2 * D], 8)
    coltab(bqa, 16)
    coltab(bka, 24)
    coltab(bd, 32)
    s.op("dve", lambda E: E.memset(onebf[:, :], 1.0), writes=["onebf"])

    def vb_mm(E):
        E.matmul(psO[:, 0:512], onebf[0:1, 0:128], bqkv_bf[0:1, 0:512],
                 start=True, stop=True)
        return E.matmul(psO[:, 512:1024], onebf[0:1, 0:128],
                        bqkv_bf[0:1, 512:1024], start=True, stop=True)
    s.op("pe", vb_mm, reads=["onebf", "bqkv_bf"], writes=["psO"])

    def vb_ev(E):
        E.tensor_copy(vbbc[:, 0:512], psO[:, 0:512])
        return E.tensor_copy(vbbc[:, 512:1024], psO[:, 512:1024])
    s.op("dve", vb_ev, reads=["psO"], writes=["vbbc"])

    def ldw_cols(hbm, col0, buf, bk, ncols=D):
        src = hbm[:, col0:col0 + ncols].rearrange("(kt p) c -> p kt c", p=128)
        dst = bass.AP(buf, 0, [[buf.shape[1], 128], [ncols, PT], [1, ncols]])
        s.dma("pool", lambda E, src=src, dst=dst: E.dma_start(dst, src), writes=[bk])

    def rope_chunk(buf, keyf, t0, c):
        for pt in range(PT):
            ap = buf[:, pt * T + c * 512:pt * T + (c + 1) * 512]
            key = keyf(pt)
            s.op("pe", lambda E, ap=ap: E.matmul(
                psS[1][:, 0:512], Rbf[:, :], ap, start=True, stop=True),
                reads=[key, "Rbf"], writes=["psS1"])
            for base in (0, 64):
                s.op("dve", lambda E, ap=ap, base=base: E.tensor_copy(
                    rpq, ap[base:base + 16, :]),
                    reads=[key], writes=["rpq", "Ssb:0"])
                s.op("act", lambda E, base=base: E.activation(
                    rpr, psS[1][base:base + 16, 0:512], AF.Copy),
                    reads=["psS1"], writes=["rpr", "Ssb:1"])
                s.op("dve", lambda E, t0=t0: E.tensor_tensor(
                    u1b, rpq, yall[0:16, t0:t0 + 512], ALU.mult),
                    reads=["rpq", "sh16"], writes=["u1b", "Ssb:2"])
                s.op("dve", lambda E, t0=t0: E.tensor_tensor(
                    u2b, rpr, yall[0:16, T + t0:T + t0 + 512], ALU.mult),
                    reads=["rpr", "sh16"], writes=["u2b", "Ssb:3"])
                s.op("dve", lambda E, ap=ap, base=base: E.tensor_tensor(
                    ap[base:base + 16, :], u1b, u2b, ALU.add),
                    reads=["u1b", "u2b"], writes=[key])

    NG = T // 512
    for b in range(B_CORE):
        s.dma("sp", lambda E: E.dma_start(cosS, cosT_h[:, :]),
              writes=["sh16"] + [f"yall:{m}" for m in range(4)])
        s.dma("sp", lambda E: E.dma_start(sinS, sinT_h[:, :]),
              writes=["sh16"] + [f"yall:{m}" for m in range(4, 8)])
        for tt in range(NT):
            def writer(j, tt=tt):
                return (xnfB[:, sl(j * T, tt)], f"xnf:{j}")
            emit_lnt_tile(p, lnt, xin[b, ts(tt), :], psT, "psT", writer)
        for (col0, dstbuf, keyf, bcol0) in (
                (D, KfB, (lambda mt: f"Kf:{mt}"), 8),
                (0, QaB, (lambda mt: f"qa:{mt}"), 0)):
            ldw_cols(WqkvT, col0, wb, "wb")
            for c in range(NG):
                for mt in range(PT):
                    def mmK(E, c=c, mt=mt):
                        last = None
                        for kt in range(PT):
                            last = E.matmul(psP[mt % 2][:, 0:512],
                                            wb[:, sl(kt * D, mt)],
                                            xnfB[:, kt * T + c * 512:
                                                 kt * T + (c + 1) * 512],
                                            start=(kt == 0), stop=(kt == PT - 1))
                        return last
                    s.op("pe", mmK, reads=["wb"] + [f"xnf:{k}" for k in range(PT)],
                         writes=[f"psP{mt % 2}"])
                    s.op("dve", lambda E, c=c, mt=mt, dstbuf=dstbuf, keyf=keyf,
                         bcol0=bcol0: E.tensor_scalar(
                             dstbuf[:, mt * T + c * 512:mt * T + (c + 1) * 512],
                             psP[mt % 2][:, 0:512], btab[:, bcol0 + mt:bcol0 + mt + 1],
                             None, ALU.add),
                         reads=[f"psP{mt % 2}", "btab"], writes=[keyf(mt)])
                rope_chunk(dstbuf, keyf, c * 512, c)
        ldw_cols(WqkvT, 2 * D, wb, "wb")
        for tt in range(NT):
            for nh in range(2):
                def mmV(E, tt=tt, nh=nh):
                    last = None
                    for kt in range(PT):
                        last = E.matmul(psP[nh][:, 0:512],
                                        xnfB[:, sl(kt * T, tt)],
                                        wb[:, kt * D + nh * 512:kt * D + (nh + 1) * 512],
                                        start=(kt == 0), stop=(kt == PT - 1))
                    return last
                s.op("pe", mmV, reads=["wb"] + [f"xnf:{k}" for k in range(PT)],
                     writes=[f"psP{nh}"])
                s.op("dve", lambda E, tt=tt, nh=nh: E.tensor_tensor(
                    VtB[:, tt * D + nh * 512:tt * D + (nh + 1) * 512],
                    psP[nh][:, 0:512], vbbc[:, nh * 512:(nh + 1) * 512], ALU.add),
                    reads=[f"psP{nh}", "vbbc"], writes=[f"Vt:{tt}"])
        for tt in range(NT):
            first = (tt == 0)
            spanw = 128 if first else 256
            for h in range(16):
                fp0 = h // 2
                po = (h % 2) * 64

                def mmS(E, tt=tt, fp0=fp0, po=po, first=first, spanw=spanw):
                    k0 = (0 if first else (tt - 1) * 128)
                    return E.matmul(psS[0][:, 0:spanw],
                                    QaB[po:po + 64, fp0 * T + tt * 128:
                                        fp0 * T + (tt + 1) * 128],
                                    KfB[po:po + 64, fp0 * T + k0:fp0 * T + k0 + spanw],
                                    start=True, stop=True)
                s.op("pe", mmS, reads=[f"qa:{fp0}", f"Kf:{fp0}"], writes=["psS0"])
                s.op("dve", lambda E, first=first, spanw=spanw: E.scalar_tensor_tensor(
                    Ssb[:, 0:spanw], psS[0][:, 0:spanw], SCALE,
                    (wmsk0 if first else wmsk)[:, 0:spanw], ALU.mult, ALU.add),
                    reads=["psS0", "wmsk", "wmsk0"], writes=["Ssbw"])
                s.op("dve", lambda E, spanw=spanw: E.tensor_reduce(
                    nmx[:, 0:1], Ssb[:, 0:spanw], AX.X, ALU.max, negate=True),
                    reads=["Ssbw"], writes=["nmx"])
                s.op("act", lambda E, spanw=spanw: E.activation(
                    Pbf[:, 0:spanw], Ssb[:, 0:spanw], AF.Exp, bias=nmx[:, 0:1],
                    accum_out=den[:, 0:1]),
                    reads=["Ssbw", "nmx"], writes=["Pbfw", "den"])
                s.op("dve", lambda E, h=h: E.reciprocal(rcp[:, h:h + 1], den[:, 0:1]),
                     reads=["den"], writes=["rcp"])

                def mmT(E, spanw=spanw):
                    last = E.transpose(psT[:, 0:128], Pbf[:, 0:128],
                                       p._cst["ident_bf"][:, :])
                    if spanw > 128:
                        last = E.transpose(psT[:, 128:256], Pbf[:, 128:256],
                                           p._cst["ident_bf"][:, :])
                    return last
                s.op("pe", mmT, reads=["Pbfw", "c:ident_bf"], writes=["psT"])
                s.op("act", lambda E, spanw=spanw: E.activation(
                    PTsb[:, 0:spanw], psT[:, 0:spanw], AF.Copy),
                    reads=["psT"], writes=["PTsb"])

                def mmPV(E, tt=tt, h=h, first=first):
                    if first:
                        return E.matmul(psO[:, h * 64:(h + 1) * 64], PTsb[:, 0:128],
                                        VtB[:, tt * D + h * 64:tt * D + (h + 1) * 64],
                                        start=True, stop=True)
                    E.matmul(psO[:, h * 64:(h + 1) * 64], PTsb[:, 0:128],
                             VtB[:, (tt - 1) * D + h * 64:(tt - 1) * D + (h + 1) * 64],
                             start=True, stop=False)
                    return E.matmul(psO[:, h * 64:(h + 1) * 64], PTsb[:, 128:256],
                                    VtB[:, tt * D + h * 64:tt * D + (h + 1) * 64],
                                    start=False, stop=True)
                vreads = [f"Vt:{tt}"] if first else [f"Vt:{tt - 1}", f"Vt:{tt}"]
                s.op("pe", mmPV, reads=["PTsb"] + vreads, writes=["psO"])
            for h in range(16):
                s.op("dve", lambda E, tt=tt, h=h: E.tensor_scalar(
                    xnfB[:, tt * D + h * 64:tt * D + (h + 1) * 64],
                    psO[:, h * 64:(h + 1) * 64], rcp[:, h:h + 1], None, ALU.mult),
                    reads=["psO", "rcp"], writes=[f"xnf:{tt // 2}"])
        for tt in range(NT):
            def mmTA(E, tt=tt):
                last = None
                for j in range(PT):
                    last = E.transpose(psT[:, ts(j)], xnfB[:, sl(tt * D, j)],
                                       p._cst["ident_bf"][:, :])
                return last
            s.op("pe", mmTA, reads=[f"xnf:{tt // 2}", "c:ident_bf"], writes=["psT"])
            for j in range(PT):
                s.op("act", lambda E, tt=tt, j=j: E.activation(
                    KfB[:, sl(j * T, tt)], psT[:, ts(j)], AF.Copy),
                    reads=["psT"], writes=[f"Kf:{j}"])
        for (wsrc, bcol0, dstbuf, dkeys) in (
                (WqaT, 16, QaB, lambda mt: [f"qa:{mt}"]),
                (WkaT, 24, VtB, lambda mt: [f"Vt:{2 * mt}", f"Vt:{2 * mt + 1}"])):
            ldw_cols(wsrc, 0, wb, "wb")
            for c in range(NG):
                for mt in range(PT):
                    def mmA(E, c=c, mt=mt):
                        last = None
                        for kt in range(PT):
                            last = E.matmul(psP[mt % 2][:, 0:512],
                                            wb[:, sl(kt * D, mt)],
                                            KfB[:, kt * T + c * 512:
                                                kt * T + (c + 1) * 512],
                                            start=(kt == 0), stop=(kt == PT - 1))
                        return last
                    s.op("pe", mmA, reads=["wb"] + [f"Kf:{k}" for k in range(PT)],
                         writes=[f"psP{mt % 2}"])
                    s.op("dve", lambda E, c=c, mt=mt, dstbuf=dstbuf, bcol0=bcol0:
                         E.tensor_scalar(
                             dstbuf[:, mt * T + c * 512:mt * T + (c + 1) * 512],
                             psP[mt % 2][:, 0:512],
                             btab[:, bcol0 + mt:bcol0 + mt + 1], None, ALU.add),
                         reads=[f"psP{mt % 2}", "btab"], writes=dkeys(mt))
        ldw_cols(WdT, 0, wb, "wb")
        for g in range(NG):
            for qi4 in range(4):
                qi = g * 4 + qi4
                nkc = qi // 4 + 1
                for h in range(16):
                    fp0 = h // 2
                    po = (h % 2) * 64
                    for c in range(nkc):
                        diag = (c == nkc - 1)

                        def mmS2(E, qi=qi, c=c, fp0=fp0, po=po):
                            return E.matmul(
                                psS[c % 2][:, 0:512],
                                QaB[po:po + 64, sl(fp0 * T, qi)],
                                VtB[po:po + 64, fp0 * T + c * 512:
                                    fp0 * T + (c + 1) * 512],
                                start=True, stop=True)
                        s.op("pe", mmS2,
                             reads=[f"qa:{fp0}", f"Vt:{2 * fp0}", f"Vt:{2 * fp0 + 1}"],
                             writes=[f"psS{c % 2}"])
                        if diag:
                            s.op("dve", lambda E, c=c, qi4=qi4: E.scalar_tensor_tensor(
                                Ssb[:, c * 512:(c + 1) * 512], psS[c % 2][:, 0:512],
                                SCALE,
                                p._cst["cmask4_bf"][:, qi4 * 512:(qi4 + 1) * 512],
                                ALU.mult, ALU.add),
                                reads=[f"psS{c % 2}", "c:cmask4_bf"], writes=[f"Ssb:{c}"])
                            s.op("dve", lambda E, c=c: E.tensor_reduce(
                                mx[:, c:c + 1], Ssb[:, c * 512:(c + 1) * 512], AX.X,
                                ALU.max),
                                reads=[f"Ssb:{c}"], writes=[f"mx:{c}"])
                        else:
                            s.op("dve", lambda E, c=c: E.tensor_scalar(
                                Ssb[:, c * 512:(c + 1) * 512], psS[c % 2][:, 0:512],
                                SCALE, None, ALU.mult, ALU.max,
                                accum_out=mx[:, c:c + 1]),
                                reads=[f"psS{c % 2}"], writes=[f"Ssb:{c}", f"mx:{c}"])
                    s.op("dve", lambda E, nkc=nkc: E.tensor_reduce(
                        nmx[:, 0:1], mx[:, 0:nkc], AX.X, ALU.max, negate=True),
                        reads=[f"mx:{c}" for c in range(nkc)], writes=["nmx"])
                    for c in range(nkc):
                        s.op("act", lambda E, c=c: E.activation(
                            Pbf[:, c * 512:(c + 1) * 512],
                            Ssb[:, c * 512:(c + 1) * 512],
                            AF.Exp, bias=nmx[:, 0:1], accum_out=den[:, c:c + 1]),
                            reads=[f"Ssb:{c}", "nmx"], writes=[f"Pbf:{c}", f"den:{c}"])
                    s.op("dve", lambda E, nkc=nkc: E.tensor_reduce(
                        den[:, 6:7], den[:, 0:nkc], AX.X, ALU.add),
                        reads=[f"den:{c}" for c in range(nkc)], writes=["densum"])
                    s.op("dve", lambda E: E.reciprocal(rcp[:, 0:1], den[:, 6:7]),
                         reads=["densum"], writes=["rcp"])
                    for c in range(nkc):
                        def mmT2(E, c=c):
                            last = None
                            for j in range(4):
                                last = E.transpose(psT[:, ts(j)],
                                                   Pbf[:, sl(c * 512, j)],
                                                   p._cst["ident_bf"][:, :])
                            return last
                        s.op("pe", mmT2, reads=[f"Pbf:{c}", "c:ident_bf"],
                             writes=["psT"])
                        s.op("act", lambda E: E.activation(PTsb[:, 0:512],
                                                           psT[:, 0:512], AF.Copy),
                             reads=["psT"], writes=["PTsb"])

                        def mmPV2(E, c=c, h=h, nkc=nkc):
                            last = None
                            for j in range(4):
                                tt2 = c * 4 + j
                                st = (c == 0 and j == 0)
                                sp = (c == nkc - 1 and j == 3)
                                last = E.matmul(psO[:, h * 64:(h + 1) * 64],
                                                PTsb[:, ts(j)],
                                                xnfB[:, tt2 * D + h * 64:
                                                     tt2 * D + (h + 1) * 64],
                                                start=st, stop=sp)
                            return last
                        s.op("pe", mmPV2,
                             reads=["PTsb"] + [f"xnf:{(c * 4 + j) // 2}"
                                               for j in range(4)],
                             writes=["psO"])
                    s.op("dve", lambda E, h=h: E.tensor_scalar(
                        Ot[:, (h % 2) * 64:(h % 2) * 64 + 64],
                        psO[:, h * 64:(h + 1) * 64], rcp[:, 0:1], None, ALU.mult),
                        reads=["psO", "rcp"], writes=["Ot128"])
                    if h % 2 == 1:
                        fp = h // 2
                        s.op("pe", lambda E: E.transpose(
                            psT[:, 0:128], Ot[:, :], p._cst["ident_bf"][:, :]),
                            reads=["Ot128", "c:ident_bf"], writes=["psT"])
                        s.op("act", lambda E, qi4=qi4, fp=fp: E.activation(
                            Of[:, sl(fp * 512, qi4)], psT[:, 0:128], AF.Copy),
                            reads=["psT"], writes=[f"Of:{qi4}:{fp}"])
            for mt in range(PT):
                def mmproj(E, mt=mt):
                    last = None
                    for kt in range(PT):
                        last = E.matmul(psR[:, 0:512],
                                        wb[:, sl(kt * D, mt)],
                                        Of[:, kt * 512:(kt + 1) * 512],
                                        start=(kt == 0), stop=(kt == PT - 1))
                    return last
                s.op("pe", mmproj,
                     reads=["wb"] + [f"Of:{q4}:{kt}" for q4 in range(4)
                                     for kt in range(PT)],
                     writes=["psR"])
                s.op("dve", lambda E, mt=mt: E.tensor_scalar(
                    yall[:, ts(mt, 512)], psR[:, 0:512], btab[:, 32 + mt:33 + mt],
                    None, ALU.add),
                    reads=["psR", "btab"], writes=[f"yall:{mt}", "sh16"])
            emit_group_residual(p, b, g, yall, xin, xout, psS[0], "psS0", (rx, ry))
    if fused:
        p.s.barrier()
    _sc.__exit__(None, None, None)
    return p


_PROGS = {}


def _prog(name, builder):
    if name not in _PROGS:
        _PROGS[name] = builder()
        _PROGS[name].build()
    return _PROGS[name]


def _tr(a):
    return np.ascontiguousarray(np.asarray(a).T)





MOE_SLOTS = 14336
NCHK = MOE_SLOTS // 512


def conv_jobs(p, W):
    s = p.s
    cv = [p.sb(f"cv{i}", [128, 1024], BF16) for i in range(2)]
    specs = [
        (W["We1g"].rearrange("e j p c -> (e j p) c"), W["We1b"], 4096, 144),
        (W["We2g"], W["We2b"], 1024, 288),
        (W["Ws1"], W["Ws1b"], 4096, 8),
        (W["Ws2"], W["Ws2b"], 1024, 16),
    ]
    out = []
    i = 0
    for src, dst, nc_, nt_ in specs:
        for r in range(nt_):
            for c0 in range(0, nc_, 1024):
                cw = min(1024, nc_ - c0)
                def job(i=i, src=src, dst=dst, r=r, c0=c0, cw=cw):
                    buf = cv[i % 2]
                    key = f"cv{i % 2}"
                    s.dma("pool", lambda E, buf=buf: E.dma_start(
                        buf[:, 0:cw], src[r * 128:(r + 1) * 128, c0:c0 + cw]),
                        writes=[key])
                    s.dma("sp",
                          lambda E, buf=buf: E.dma_start(
                              dst[r * 128:(r + 1) * 128, c0:c0 + cw],
                              buf[:, 0:cw]),
                          reads=[key])
                out.append(job)
                i += 1
    return out


def build_conv(p, W):
    with p.scope():
        for job in conv_jobs(p, W):
            job()
        p.s.barrier()


def build_moe_sparse(p=None, io=None, W=None):
    fused = p is not None
    if not fused:
        p = Prog("moes")
    _sc = p.scope() if fused else contextlib.nullcontext()
    _sc.__enter__()
    s = p.s
    p._cst = load_consts(p, ["ident_bf", "ident_f32", "ones_col_bf"])
    xin = io[0] if fused else p.dram_in("xin", shape=[B_CORE, T, D], dtype=F32)
    xout = io[1] if fused else p.dram_out("xout", [B_CORE, T, D], F32)

    def gw(name, shape):
        return W[name] if W else p.dram_in(name, shape=shape, dtype=F32)
    bs1 = gw("bs1", [1, 4096])
    bs2 = gw("bs2", [1, D])
    be1 = gw("be1", [12, 6144])
    be2 = gw("be2", [12, D])
    Wr = gw("Wr", [D, 12])
    br = gw("br", [1, 12])
    if W:
        Ws1b, Ws2b = W["Ws1b"], W["Ws2b"]
        We1b, We2b = W["We1b"], W["We2b"]
    else:
        Ws1b = p.dram_in("Ws1b", shape=[D, 4096], dtype=BF16)
        Ws2b = p.dram_in("Ws2b", shape=[2048, D], dtype=BF16)
        We1b = p.dram_in("We1b", shape=[12 * 12 * 128, 4096], dtype=BF16)
        We2b = p.dram_in("We2b", shape=[12 * 128 * 24, D], dtype=BF16)

    NTOK = B_CORE * T
    ltri = np.triu(np.ones((128, 128), np.float32), 1)
    ltri_h = p.dram_in("ltri", ltri.astype(BF), dtype=BF16)
    iota1 = np.arange(128, dtype=np.float32).reshape(128, 1)
    iota1_h = p.dram_in("iota1", iota1, dtype=F32)
    c512_h = p.dram_in("c512", (512.0 * np.arange(NCHK, dtype=np.float32)
                                ).reshape(1, NCHK), dtype=F32)
    onesr_h = p.dram_in("onesr", np.ones((1, 128), np.float32), dtype=F32)
    i24k_np = (24.0 * np.arange(128, dtype=np.float32)[:, None]
               + np.arange(25, dtype=np.float32)[None, :])
    i24k_h = p.dram_in("i24k", i24k_np, dtype=F32)
    jp128_np = (np.arange(128, dtype=np.float32)[:, None]
                + 128.0 * np.arange(12, dtype=np.float32)[None, :])
    jp128_h = p.dram_in("jp128", jp128_np, dtype=F32)
    e8 = np.zeros((8, 8 * 128), np.float32)
    for pp in range(8):
        e8[pp, pp * 128:(pp + 1) * 128] = 1.0
    esel8_h = p.dram_in("esel8", e8, dtype=F32)

    tg = p.tag
    xntm_d = p.nc.dram_tensor(tg + "xntm", [NTOK, D], BF16, kind="Internal").ap()
    xmid_d = p.nc.dram_tensor(tg + "xmid", [B_CORE, T, D], F32, kind="Internal").ap()
    stok_d = p.nc.dram_tensor(tg + "stok", [MOE_SLOTS, 1], U32, kind="Internal").ap()
    ye_d = p.nc.dram_tensor(tg + "ye", [MOE_SLOTS, D], F32, kind="Internal").ap()
    cnt_d = p.nc.dram_tensor(tg + "cntd", [12, 32], F32, kind="Internal").ap()
    pad_d = p.nc.dram_tensor(tg + "padd", [12, 1], F32, kind="Internal").ap()
    segb_d = p.nc.dram_tensor(tg + "segbd", [1, 12], F32, kind="Internal").ap()
    ibase_d = p.nc.dram_tensor(tg + "ibased", [32, 12], F32, kind="Internal").ap()

    lnt = alloc_lnt(p)
    xnf = p.sb("xnf", [128, PT * 512], BF16)
    xnf32 = p.sb("xnf32", [128, PT * 512], F32)
    actb = p.sb("actb", [128, 24 * 512], BF16)
    wA = [p.sb(f"wA{i}", [128, PT * 512], BF16) for i in range(3)]
    wG = [p.sb(f"wG{i}", [128, PT * 512], BF16) for i in range(3)]
    w2b = [p.sb("w2b0", [128, 24 * 128], BF16), p.sb("w2b1", [128, 24 * 128], BF16)]
    w2r = p.sb("w2r", [128, 8 * 512], BF16)
    wr32 = p.sb("wr32", [128, PT * 12], F32)
    logt = p.sb("logt", [128, 12], F32)
    lwk = p.sb("lwk", [128, 12], F32)
    oh1 = p.sb("oh1", [128, 12], F32)
    oh2 = p.sb("oh2", [128, 12], F32)
    cs2 = p.sb("cs2", [128, 12], F32)
    m12 = p.sb("m12", [128, 4], F32)
    fst12 = p.sb("fst12", [128, 32 * 24], F32)
    w12 = p.sb("w12", [128, 64], F32)
    slotsU = p.sb("slotsU", [128, 64], U32)
    ohbf = p.sb("ohbf", [128, 12], BF16)
    ibrow = p.sb("ibrow", [1, 12 * 32], F32)
    ibr12 = p.sb("ibr12", [1, 12], F32)
    cnts = p.sb("cnts", [12, 32], F32)
    cntx = p.sb("cntx", [12, 32], F32)
    tot = p.sb("tot", [12, 4], F32)
    totu = p.sb("totu", [12, 1], U32)
    segrow = p.sb("segrow", [1, 12], F32)
    segcol = p.sb("segcol", [12, 1], F32)
    chrow = p.sb("chrow", [1, NCHK], F32)
    chbc = p.sb("chbc", [128, NCHK], F32)
    chu = p.sb("chu", [1, NCHK], U32)
    cmp12 = p.sb("cmp12", [12, NCHK], BF16)
    posb = p.sb("posb", [128, 12], F32)
    selt = p.sb("selt", [128, 12], F32)
    scol = p.sb("scol", [128, 2], F32)
    tidq = p.sb("tidq", [128, 4], U32)
    idq1 = p.sb("idq1", [128, 3], U32)
    idqw = p.sb("idqw", [128, 12], U32)
    idqwf = p.sb("idqwf", [128, 12], F32)
    jpc = p.sb("jpc", [128, 12], F32)
    idq24 = p.sb("idq24", [128, 24], U32)
    idq24f = p.sb("idq24f", [128, 24], F32)
    be2g = p.sb("be2g", [8, 128], F32)
    i24k = p.sb("i24k_s", [128, 25], F32)
    esel8 = p.sb("esel8_s", [8, 8 * 128], F32)
    xg = [p.sb("xg0", [128, D], BF16), p.sb("xg1", [128, D], BF16)]
    g1 = p.sb("g1", [128, D], F32)
    g2 = p.sb("g2", [128, D], F32)
    xmt = p.sb("xmt", [128, D], F32)
    ytmp = p.sb("ytmp", [128, 2 * 512], F32)
    glb = [p.sb(f"glb{i}", [128, 512], BF16) for i in range(3)]
    yacc = p.sb("yacc", [128, PT * 512], F32)
    bscol = p.sb("bscol", [128, 48], F32)
    be1r = p.sb("be1r", [48, 128], F32)
    be1c = p.sb("be1c", [128, 48], F32)
    be2r = p.sb("be2r", [1, D], F32)
    be2bc = p.sb("be2bc", [128, D], F32)
    brbc = p.sb("brbc", [128, 12], F32)
    rowb = p.sb("rowb", [48, 128], F32)
    ltris = p.sb("ltris", [128, 128], BF16)
    iotas = p.sb("iotas", [128, 1], F32)
    c512s = p.sb("c512s", [1, NCHK], F32)
    onesr = p.sb("onesr_s", [1, 128], F32)
    dumm = p.sb("dumm", [128, MOE_SLOTS // 128], U32)
    rx = p.sb("rx", [128, D], F32)
    ry = p.sb("ry", [128, D], F32)

    psA = p.psum("psA", [128, 512], F32)
    psB = p.psum("psB", [128, 512], F32)
    psR = p.psum("psR", [128, 512], F32)
    psT = p.psum("psT", [128, 1024], BF16)
    psY = [p.psum(f"psY{j}", [128, 512], F32) for j in range(4)]

    s.dma("sp", lambda E: E.dma_start(ltris[:, :], ltri_h[:, :]), writes=["ltris"])
    s.dma("sp", lambda E: E.dma_start(iotas[:, :], iota1_h[:, :]), writes=["iotas"])
    s.dma("sp", lambda E: E.dma_start(c512s[:, :], c512_h[:, :]), writes=["c512s"])
    s.dma("sp", lambda E: E.dma_start(onesr[:, :], onesr_h[:, :]), writes=["onesr"])
    s.dma("sp", lambda E: E.dma_start(i24k[:, :], i24k_h[:, :]), writes=["i24k"])
    s.dma("sp", lambda E: E.dma_start(jpc[:, :], jp128_h[:, :]), writes=["jpc"])
    s.dma("sp", lambda E: E.dma_start(esel8[:, :], esel8_h[:, :]), writes=["esel8"])
    src_wr = Wr.rearrange("(kt p) c -> p kt c", p=128)
    s.dma("sp", lambda E, src=src_wr: E.dma_start(
        bass.AP(wr32, 0, [[wr32.shape[1], 128], [12, PT], [1, 12]]), src),
        writes=["wr32"])

    def coltab(hbm_row_ap, nrow, dst, dstk, col0):
        s.dma("sp", lambda E, hbm_row_ap=hbm_row_ap, nrow=nrow: E.dma_start(
            rowb[0:nrow, :], hbm_row_ap.rearrange("o (r x) -> (o r) x", x=128)),
            writes=["rowb"])
        s.op("pe", lambda E, nrow=nrow: E.transpose(
            psR[0:128, 0:nrow], rowb[0:nrow, 0:128],
            p._cst["ident_f32"][0:nrow, 0:nrow]),
            reads=["rowb", "c:ident_f32"], writes=["psR"])
        s.op("dve", lambda E, col0=col0, nrow=nrow: E.tensor_copy(
            dst[:, col0:col0 + nrow], psR[0:128, 0:nrow]),
            reads=["psR"], writes=[dstk])

    coltab(bs1, 32, bscol, "bscol", 0)
    coltab(bs2, 8, bscol, "bscol", 32)
    s.dma("sp", lambda E: E.dma_start(rowb[0:1, 0:12], br[:, :]), writes=["rowb"])
    s.op("pe", lambda E: E.matmul(psR[:, 0:12], onesr[0:1, :], rowb[0:1, 0:12],
                                  start=True, stop=True),
         reads=["rowb", "onesr"], writes=["psR"])
    s.op("dve", lambda E: E.tensor_copy(brbc[:, :], psR[:, 0:12]),
         reads=["psR"], writes=["brbc"])
    s.op("dve", lambda E: E.memset(dumm[:, :], float(NTOK - 1)), writes=["dumm"])
    s.dma("sp", lambda E: E.dma_start(
        stok_d.rearrange("(q j) o -> q j", q=128), dumm[:, :]),
        reads=["dumm"], writes=["stok"])

    def ldw_cols(hbm, col0, buf, bk, ncols=512, qeng="pool"):
        src = hbm[:, col0:col0 + ncols].rearrange("(kt p) c -> p kt c", p=128)
        dst = bass.AP(buf, 0, [[buf.shape[1], 128], [ncols, PT], [1, ncols]])
        s.dma(qeng, lambda E, src=src, dst=dst: E.dma_start(dst, src), writes=[bk])

    def ldw_lhs(hbm, mt, buf, bk, kt_n, qeng="pool"):
        src = hbm[:, ts(mt)].rearrange("(kt p) c -> p kt c", p=128)
        dst = bass.AP(buf, 0, [[buf.shape[1], 128], [128, kt_n], [1, 128]])
        s.dma(qeng, lambda E, src=src, dst=dst: E.dma_start(dst, src), writes=[bk])

    NG8 = NTOK // 512

    for c in range(NG8):
        b = c // (T // 512)
        g = c % (T // 512)
        for j4 in range(4):
            tt = g * 4 + j4
            ti = c * 4 + j4

            def writer(j, j4=j4):
                return (xnf[:, sl(j * 512, j4)], f"xnf:{j}")
            emit_lnt_tile(p, lnt, xin[b, ts(tt), :], psT, "psT", writer)
            s.dma("sp", lambda E, ti=ti: E.dma_start(
                xntm_d[ti * 128:(ti + 1) * 128, :], lnt["xn_tm"][:, :]),
                reads=["lnt_xn"], writes=["xntm"])
            s.op("pool", lambda E: E.tensor_scalar(
                lnt["xt"][:, :], lnt["xt"][:, :], lnt["st"][:, 2:3],
                lnt["st"][:, 7:8], ALU.subtract, ALU.mult),
                reads=["lnt_xt", "lnt_mean_msq", "lnt_rstd"], writes=["lnt_xt"])
            for w2_ in range(2):
                def mmT32(E, w2_=w2_):
                    last = None
                    for j in range(4):
                        last = E.transpose(psA[:, ts(j)],
                                           lnt["xt"][:, sl(w2_ * 512, j)],
                                           p._cst["ident_f32"][:, :])
                    return last
                s.op("pe", mmT32, reads=["lnt_xt", "c:ident_f32"], writes=["psA"])
                for j in range(4):
                    jj = w2_ * 4 + j
                    s.op("dve", lambda E, jj=jj, j=j, j4=j4: E.tensor_copy(
                        xnf32[:, sl(jj * 512, j4)], psA[:, ts(j)]),
                        reads=["psA"], writes=[f"xnf32:{jj}"])
        for j4 in range(4):
            ti = c * 4 + j4

            def mmr(E, j4=j4):
                last = None
                for kt in range(PT):
                    last = E.matmul(psB[:, 0:12], xnf32[:, sl(kt * 512, j4)],
                                    wr32[:, ts(kt, 12)],
                                    start=(kt == 0), stop=(kt == PT - 1))
                return last
            s.op("pe", mmr, reads=["wr32"] + [f"xnf32:{k}" for k in range(PT)],
                 writes=["psB"])
            s.op("dve", lambda E: E.tensor_tensor(
                logt[:, :], psB[:, 0:12], brbc[:, :], ALU.add),
                reads=["psB", "brbc"], writes=["logt"])
            s.op("dve", lambda E: E.tensor_copy(lwk[:, :], logt[:, :]),
                 reads=["logt"], writes=["lwk"])
            for kk in range(2):
                ohx = oh1 if kk == 0 else oh2
                ohk = "oh1" if kk == 0 else "oh2"
                s.op("dve", lambda E, kk=kk: E.tensor_reduce(
                    m12[:, kk:kk + 1], lwk[:, :], AX.X, ALU.max),
                    reads=["lwk"], writes=["m12"])
                s.op("dve", lambda E, ohx=ohx, kk=kk: E.tensor_scalar(
                    ohx[:, :], lwk[:, :], m12[:, kk:kk + 1], None, ALU.is_equal),
                    reads=["lwk", "m12"], writes=[ohk])
                s.op("dve", lambda E, ohx=ohx: E.tensor_tensor_scan(
                    cs2[:, :], ohx[:, :], ohx[:, :], 0.0, ALU.add, ALU.bypass),
                    reads=[ohk], writes=["cs2"])
                s.op("dve", lambda E: E.tensor_scalar(
                    cs2[:, :], cs2[:, :], 1.0, None, ALU.is_equal),
                    reads=["cs2"], writes=["cs2"])
                s.op("dve", lambda E, ohx=ohx, ohk=ohk: E.tensor_tensor(
                    ohx[:, :], ohx[:, :], cs2[:, :], ALU.mult),
                    reads=[ohk, "cs2"], writes=[ohk])
                if kk == 0:
                    s.op("dve", lambda E: E.scalar_tensor_tensor(
                        lwk[:, :], oh1[:, :], NEG, lwk[:, :], ALU.mult, ALU.add),
                        reads=["oh1", "lwk"], writes=["lwk"])
            s.op("dve", lambda E: E.tensor_tensor(
                m12[:, 2:3], m12[:, 0:1], m12[:, 1:2], ALU.subtract),
                reads=["m12"], writes=["m12"])
            s.op("act", lambda E, ti=ti: E.activation(
                w12[:, 2 * ti:2 * ti + 1], m12[:, 2:3], AF.Sigmoid, scale=1.0 / 1.5),
                reads=["m12"], writes=["w12"])
            s.op("act", lambda E, ti=ti: E.activation(
                w12[:, 2 * ti + 1:2 * ti + 2], m12[:, 2:3], AF.Sigmoid,
                scale=-1.0 / 1.5),
                reads=["m12"], writes=["w12"])
            s.op("dve", lambda E, ti=ti: E.tensor_copy(
                fst12[:, ti * 24:ti * 24 + 12], oh1[:, :]),
                reads=["oh1"], writes=["fst12"])
            s.op("dve", lambda E, ti=ti: E.tensor_copy(
                fst12[:, ti * 24 + 12:ti * 24 + 24], oh2[:, :]),
                reads=["oh2"], writes=["fst12"])
            s.op("dve", lambda E: E.tensor_tensor(
                ohbf[:, :], oh1[:, :], oh2[:, :], ALU.add),
                reads=["oh1", "oh2"], writes=["ohbf"])
            s.op("pe", lambda E: E.matmul(psB[0:1, 128:140],
                                          p._cst["ones_col_bf"][:, 0:1],
                                          ohbf[:, :], start=True, stop=True),
                 reads=["ohbf", "c:ones_col_bf"], writes=["psB"])
            s.op("dve", lambda E, ti=ti: E.tensor_copy(
                bass.AP(ibrow, ti, [[ibrow.shape[1], 1], [32, 12]]),
                psB[0:1, 128:140]),
                reads=["psB"], writes=["ibrow"])
        for mt2 in range(4):
            db = mt2 % 3
            wAx, wGx, ka, kg = wA[db], wG[db], f"wA{db}", f"wG{db}"
            ldw_cols(Ws1b, mt2 * 512, wAx, ka, qeng=("pool", "sp")[mt2 % 2])
            ldw_cols(Ws1b, 2048 + mt2 * 512, wGx, kg, qeng=("act", "pool")[mt2 % 2])
            for m4 in range(4):
                mtg = mt2 * 4 + m4
                pm = m4 % 3
                pbA, kA = [(psA, "psA"), (psR, "psR"), (psY[2], "psY2")][pm]
                pbG, kG = [(psB, "psB"), (psY[0], "psY0"), (psY[1], "psY1")][pm]
                glx, kgl = glb[pm], f"glb{pm}"

                def mma(E, m4=m4, wAx=wAx, pbA=pbA):
                    last = None
                    for kt in range(PT):
                        last = E.matmul(pbA[:, 0:512], wAx[:, sl(kt * 512, m4)],
                                        xnf[:, ts(kt, 512)],
                                        start=(kt == 0), stop=(kt == PT - 1))
                    return last

                def mmg(E, m4=m4, wGx=wGx, pbG=pbG):
                    last = None
                    for kt in range(PT):
                        last = E.matmul(pbG[:, 0:512], wGx[:, sl(kt * 512, m4)],
                                        xnf[:, ts(kt, 512)],
                                        start=(kt == 0), stop=(kt == PT - 1))
                    return last
                s.op("pe", mma, reads=[ka] + [f"xnf:{k}" for k in range(PT)],
                     writes=[kA])
                s.op("pe", mmg, reads=[kg] + [f"xnf:{k}" for k in range(PT)],
                     writes=[kG])
                s.op("act", lambda E, mtg=mtg, pbG=pbG, glx=glx: E.activation(
                    glx[:, :], pbG[:, 0:512], AF.Gelu,
                    bias=bscol[:, 16 + mtg:17 + mtg]),
                    reads=[kG, "bscol"], writes=[kgl])
                s.op("dve", lambda E, mtg=mtg, pbA=pbA, glx=glx:
                     E.scalar_tensor_tensor(
                         actb[:, ts(mtg, 512)], pbA[:, 0:512],
                         bscol[:, mtg:mtg + 1], glx[:, :], ALU.add, ALU.mult),
                     reads=[kA, "bscol", kgl], writes=[f"actb:{mtg}"])
        for mt in range(PT):
            db2 = mt % 2
            w2x, k2 = w2b[db2], f"w2b{db2}"
            pbS, kS = (psR, "psR") if db2 == 0 else (psY[1], "psY1")
            ldw_lhs(Ws2b, mt, w2x, k2, 16, qeng=("pool", "sp", "act")[mt % 3])

            def mms2(E, mt=mt, w2x=w2x, pbS=pbS):
                last = None
                for kt in range(16):
                    last = E.matmul(pbS[:, 0:512], w2x[:, ts(kt)],
                                    actb[:, ts(kt, 512)],
                                    start=(kt == 0), stop=(kt == 15))
                return last
            s.op("pe", mms2, reads=[k2] + [f"actb:{k}" for k in range(16)],
                 writes=[kS])
            s.op("dve", lambda E, mt=mt, pbS=pbS: E.tensor_scalar(
                yacc[:, ts(mt, 512)], pbS[:, 0:512], bscol[:, 32 + mt:33 + mt],
                None, ALU.add),
                reads=[kS, "bscol"], writes=[f"yacc:{mt}"])
        emit_group_residual(p, b, g, yacc, xin, xmid_d, psA, "psA", (rx, ry),
                            ykey="yacc", out_key="xmidd")

    s.dma("sp", lambda E: E.dma_start(cnt_d.rearrange("(o e) i -> o (e i)", o=1),
                                      ibrow[:, :]), reads=["ibrow"], writes=["cntd"])
    s.dma("sp", lambda E: E.dma_start(cnts[:, :], cnt_d[:, :]),
          reads=["cntd"], writes=["cnts"])
    s.op("dve", lambda E: E.tensor_tensor_scan(
        cntx[:, :], cnts[:, :], cnts[:, :], 0.0, ALU.add, ALU.bypass),
        reads=["cnts"], writes=["cntx"])
    s.op("dve", lambda E: E.tensor_copy(tot[0:12, 0:1], cntx[:, 31:32]),
         reads=["cntx"], writes=["tot"])
    s.op("dve", lambda E: E.tensor_tensor(
        cntx[:, :], cntx[:, :], cnts[:, :], ALU.subtract),
        reads=["cntx", "cnts"], writes=["cntx"])
    s.op("pe", lambda E: E.matmul(psB[0:12, 0:NCHK], onesr[0:1, 0:12],
                                  c512s[0:1, :], start=True, stop=True),
         reads=["onesr", "c512s"], writes=["psB"])
    s.op("dve", lambda E: E.tensor_scalar(
        cmp12[0:12, :], psB[0:12, 0:NCHK], tot[0:12, 0:1], None, ALU.is_lt),
        reads=["psB", "tot"], writes=["cmp12"])
    s.op("dve", lambda E: E.tensor_reduce(
        tot[0:12, 3:4], cmp12[0:12, :], AX.X, ALU.add),
        reads=["cmp12"], writes=["tot"])
    s.op("dve", lambda E: E.tensor_scalar(
        tot[0:12, 1:2], tot[0:12, 3:4], 512.0, None, ALU.mult),
        reads=["tot"], writes=["tot"])
    s.dma("sp", lambda E: E.dma_start(pad_d[:, :], tot[0:12, 1:2]),
          reads=["tot"], writes=["padd"])
    s.dma("sp", lambda E: E.dma_start(
        segrow[:, :], pad_d.rearrange("e o -> o e")),
        reads=["padd"], writes=["segrow"])
    s.op("dve", lambda E: E.tensor_tensor_scan(
        segrow[:, :], segrow[:, :], segrow[:, :], 0.0, ALU.add, ALU.bypass),
        reads=["segrow"], writes=["segrow"])
    s.dma("sp", lambda E: E.dma_start(segb_d[:, :], segrow[:, :]),
          reads=["segrow"], writes=["segbd"])
    s.dma("sp", lambda E: E.dma_start(
        segcol[:, :], segb_d.rearrange("o e -> e o")),
        reads=["segbd"], writes=["segcol"])
    s.op("dve", lambda E: E.tensor_tensor(
        segcol[:, :], segcol[:, :], tot[0:12, 1:2], ALU.subtract),
        reads=["segcol", "tot"], writes=["segcol"])
    s.op("dve", lambda E: E.tensor_scalar(
        cntx[:, :], cntx[:, :], segcol[:, 0:1], None, ALU.add),
        reads=["cntx", "segcol"], writes=["cntx"])
    s.dma("sp", lambda E: E.dma_start(ibase_d.rearrange("i e -> e i"), cntx[:, :],
                                      allow_slow_non_contiguous=True),
          reads=["cntx"], writes=["ibased"])
    s.dma("sp", lambda E: E.dma_start(
        ibrow[:, :], ibase_d.rearrange("(o i) e -> o (i e)", o=1)),
        reads=["ibased"], writes=["ibrow"])
    s.op("pe", lambda E: E.matmul(psB[0:12, 0:NCHK], onesr[0:1, 0:12],
                                  c512s[0:1, :], start=True, stop=True),
         reads=["onesr", "c512s"], writes=["psB"])
    s.op("dve", lambda E: E.tensor_scalar(
        cmp12[0:12, :], psB[0:12, 0:NCHK], segcol[:, 0:1], None, ALU.is_ge),
        reads=["psB", "segcol"], writes=["cmp12"])
    s.op("pe", lambda E: E.matmul(psB[0:1, 256:256 + NCHK],
                                  p._cst["ones_col_bf"][0:12, 0:1],
                                  cmp12[0:12, :], start=True, stop=True),
         reads=["cmp12", "c:ones_col_bf"], writes=["psB"])
    s.op("dve", lambda E: E.tensor_scalar(
        chrow[:, :], psB[0:1, 256:256 + NCHK], -1.0, None, ALU.add),
        reads=["psB"], writes=["chrow"])
    s.op("dve", lambda E: E.tensor_copy(chu[:, :], chrow[:, :]),
         reads=["chrow"], writes=["chu"])
    s.op("pe", lambda E: E.matmul(psB[:, 0:NCHK], onesr[0:1, :], chrow[0:1, :],
                                  start=True, stop=True),
         reads=["chrow", "onesr"], writes=["psB"])
    s.op("dve", lambda E: E.tensor_copy(chbc[:, :], psB[:, 0:NCHK]),
         reads=["psB"], writes=["chbc"])
    for ti in range(NTOK // 128):
        s.op("dve", lambda E, ti=ti: E.tensor_tensor(
            posb[:, :], fst12[:, ti * 24:ti * 24 + 12],
            fst12[:, ti * 24 + 12:ti * 24 + 24], ALU.add),
            reads=["fst12"], writes=["posb"])
        s.op("dve", lambda E: E.tensor_copy(ohbf[:, :], posb[:, :]),
             reads=["posb"], writes=["ohbf"])
        s.op("dve", lambda E, ti=ti: E.tensor_copy(
            ibr12[0:1, :], ibrow[0:1, ti * 12:(ti + 1) * 12]),
            reads=["ibrow"], writes=["ibr12"])

        def mmpos(E, ti=ti):
            E.matmul(psB[:, 0:12], ltris[:, :], ohbf[:, :], start=True, stop=False)
            return E.matmul(psB[:, 0:12], onesr[0:1, :], ibr12[0:1, :],
                            start=False, stop=True)
        s.op("pe", mmpos, reads=["ltris", "ohbf", "onesr", "ibr12"],
             writes=["psB"])
        for kk in range(2):
            s.op("dve", lambda E, ti=ti, kk=kk: E.tensor_tensor(
                selt[:, :], fst12[:, ti * 24 + kk * 12:ti * 24 + kk * 12 + 12],
                psB[:, 0:12], ALU.mult),
                reads=["fst12", "psB"], writes=["selt"])
            s.op("dve", lambda E, kk=kk: E.tensor_reduce(
                scol[:, kk:kk + 1], selt[:, :], AX.X, ALU.add),
                reads=["selt"], writes=["scol"])
        s.op("dve", lambda E, ti=ti: E.tensor_copy(
            slotsU[:, 2 * ti:2 * ti + 2], scol[:, 0:2]),
            reads=["scol"], writes=["slotsU"])
        s.op("dve", lambda E, ti=ti: E.tensor_scalar(
            tidq[:, 0:1], iotas[:, 0:1], float(ti * 128), None, ALU.add),
            reads=["iotas"], writes=["tidq"])
        for kk in range(2):
            s.dma("pool", lambda E, ti=ti, kk=kk: E.indirect_dma_start(
                stok_d[:, :],
                bass.IndirectOffsetOnAxis(
                    slotsU[:, 2 * ti + kk:2 * ti + kk + 1], 0),
                tidq[:, 0:1], None),
                reads=["slotsU", "tidq"], writes=["stok"])

    be1_flat = be1.rearrange("e (r x) -> (e r) x", x=128)
    be2_flat = be2.rearrange("e (r x) -> (e r) x", x=128)
    for ck in range(NCHK):
        ckc = chbc[:, ck:ck + 1]
        s.op("dve", lambda E, ckc=ckc: E.tensor_scalar(
            selt[:, 0:1], ckc, 1536.0, None, ALU.mult),
            reads=["chbc"], writes=["selt"])
        s.op("dve", lambda E: E.tensor_scalar(
            idqwf[:, :], jpc[:, :], selt[:, 0:1], None, ALU.add),
            reads=["jpc", "selt"], writes=["idqwf"])
        s.op("dve", lambda E: E.tensor_copy(idqw[:, :], idqwf[:, :]),
             reads=["idqwf"], writes=["idqw12"])
        s.op("dve", lambda E, ckc=ckc: E.tensor_scalar(
            selt[:, 1:2], ckc, 3072.0, None, ALU.mult),
            reads=["chbc"], writes=["selt"])
        s.op("dve", lambda E: E.tensor_scalar(
            idq24f[:, :], i24k[:, 0:24], selt[:, 1:2], None, ALU.add),
            reads=["i24k", "selt"], writes=["idq24f"])
        s.op("dve", lambda E: E.tensor_copy(idq24[:, :], idq24f[:, :]),
             reads=["idq24f"], writes=["idq24"])
        s.op("dve", lambda E, ckc=ckc: E.scalar_tensor_tensor(
            selt[0:48, 2:3], ckc[0:48, 0:1], 48.0, iotas[0:48, 0:1],
            ALU.mult, ALU.add),
            reads=["chbc", "iotas"], writes=["selt"])
        s.op("dve", lambda E: E.tensor_copy(idq1[0:48, 1:2], selt[0:48, 2:3]),
             reads=["selt"], writes=["idq1"])
        s.op("dve", lambda E, ckc=ckc: E.scalar_tensor_tensor(
            selt[0:8, 3:4], ckc[0:8, 0:1], 8.0, iotas[0:8, 0:1],
            ALU.mult, ALU.add),
            reads=["chbc", "iotas"], writes=["selt"])
        s.op("dve", lambda E: E.tensor_copy(idq1[0:8, 2:3], selt[0:8, 3:4]),
             reads=["selt"], writes=["idq1"])
        s.dma("pool", lambda E: E.indirect_dma_start(
            be1r[:, :], None, be1_flat[:, :],
            bass.IndirectOffsetOnAxis(idq1[0:48, 1:2], 0)),
            reads=["idq1"], writes=["be1r"])
        s.op("pe", lambda E: E.transpose(psR[0:128, 0:48], be1r[:, :],
                                         p._cst["ident_f32"][0:48, 0:48]),
             reads=["be1r", "c:ident_f32"], writes=["psR"])
        s.op("dve", lambda E: E.tensor_copy(be1c[:, :], psR[0:128, 0:48]),
             reads=["psR"], writes=["be1c"])
        s.dma("pool", lambda E: E.indirect_dma_start(
            be2g[:, :], None, be2_flat[:, :],
            bass.IndirectOffsetOnAxis(idq1[0:8, 2:3], 0)),
            reads=["idq1"], writes=["be2g"])

        def mmbe2(E):
            last = None
            for pp in range(8):
                tgt = psR if pp < 4 else psB
                col = (pp % 4) * 128
                last = E.matmul(tgt[:, col:col + 128], esel8[0:8, ts(pp)],
                                be2g[0:8, :], start=True, stop=True)
            return last
        s.op("pe", mmbe2, reads=["be2g", "esel8"], writes=["psR", "psB"])

        def evbe2(E):
            E.tensor_copy(be2bc[:, 0:512], psR[:, 0:512])
            return E.tensor_copy(be2bc[:, 512:1024], psB[:, 0:512])
        s.op("dve", evbe2, reads=["psR", "psB"], writes=["be2bc"])
        for j4 in range(4):
            s.dma("sp", lambda E, ck=ck, j4=j4: E.dma_start(
                tidq[:, j4:j4 + 1],
                stok_d[ck * 512 + j4 * 128:ck * 512 + (j4 + 1) * 128, :]),
                writes=["tidq"], reads=["stok"])
        for j4 in range(4):
            gx, kx = xg[j4 % 2], f"xg{j4 % 2}"
            s.dma("pool", lambda E, j4=j4, gx=gx: E.indirect_dma_start(
                gx[:, :], None, xntm_d[:, :],
                bass.IndirectOffsetOnAxis(tidq[:, j4:j4 + 1], 0)),
                reads=["tidq", "xntm"], writes=[kx])

            def mmtg(E, gx=gx):
                last = None
                for j in range(PT):
                    last = E.transpose(psT[:, ts(j)], gx[:, ts(j)],
                                       p._cst["ident_bf"][:, :])
                return last
            s.op("pe", mmtg, reads=[kx, "c:ident_bf"], writes=["psT"])
            for j in range(PT):
                s.op("act", lambda E, j=j, j4=j4: E.activation(
                    xnf[:, j * 512 + j4 * 128:j * 512 + (j4 + 1) * 128],
                    psT[:, ts(j)], AF.Copy),
                    reads=["psT"], writes=[f"xnf:{j}"])
        for j6 in range(6):
            db = j6 % 3
            wAx, wGx, ka, kg = wA[db], wG[db], f"wA{db}", f"wG{db}"
            for part, buf, bk in ((0, wAx, ka), (1, wGx, kg)):
                jp = part * 6 + j6
                for half in range(2):
                    s.dma("pool",
                          lambda E, buf=buf, half=half, jp=jp:
                          E.indirect_dma_start(
                              bass.AP(buf, half * 2048,
                                      [[buf.shape[1], 128], [1, 2048]]),
                              None, We1b[:, :],
                              bass.IndirectOffsetOnAxis(idqw[:, jp:jp + 1], 0),
                              element_offset=half * 2048),
                          reads=["idqw12"], writes=[bk])
            for m4 in range(4):
                mtg = j6 * 4 + m4

                pm = m4 % 3
                pbA, kA = [(psA, "psA"), (psR, "psR"), (psY[2], "psY2")][pm]
                pbG, kG = [(psB, "psB"), (psY[0], "psY0"), (psY[1], "psY1")][pm]
                glx, kgl = glb[pm], f"glb{pm}"

                def mma(E, m4=m4, wAx=wAx, pbA=pbA):
                    last = None
                    for kt in range(PT):
                        last = E.matmul(pbA[:, 0:512], wAx[:, sl(kt * 512, m4)],
                                        xnf[:, ts(kt, 512)],
                                        start=(kt == 0), stop=(kt == PT - 1))
                    return last

                def mmg(E, m4=m4, wGx=wGx, pbG=pbG):
                    last = None
                    for kt in range(PT):
                        last = E.matmul(pbG[:, 0:512], wGx[:, sl(kt * 512, m4)],
                                        xnf[:, ts(kt, 512)],
                                        start=(kt == 0), stop=(kt == PT - 1))
                    return last
                s.op("pe", mma, reads=[ka] + [f"xnf:{k}" for k in range(PT)],
                     writes=[kA])
                s.op("pe", mmg, reads=[kg] + [f"xnf:{k}" for k in range(PT)],
                     writes=[kG])
                s.op("act", lambda E, mtg=mtg, pbG=pbG, glx=glx: E.activation(
                    glx[:, :], pbG[:, 0:512], AF.Gelu,
                    bias=be1c[:, 24 + mtg:25 + mtg]),
                    reads=[kG, "be1c"], writes=[kgl])
                s.op("dve", lambda E, mtg=mtg, pbA=pbA, glx=glx:
                     E.scalar_tensor_tensor(
                         actb[:, ts(mtg, 512)], pbA[:, 0:512],
                         be1c[:, mtg:mtg + 1], glx[:, :], ALU.add, ALU.mult),
                     reads=[kA, "be1c", kgl], writes=[f"actb:{mtg}"])
        for nh in range(2):
            for kt in range(24):
                wsl = kt % 8
                s.dma("pool",
                      lambda E, kt=kt, wsl=wsl, nh=nh: E.indirect_dma_start(
                          w2r[:, wsl * 512:(wsl + 1) * 512],
                          None, We2b[:, :],
                          bass.IndirectOffsetOnAxis(idq24[:, kt:kt + 1], 0),
                          element_offset=nh * 512),
                      reads=["idq24"], writes=[f"w2r{wsl}"])

                def mmy(E, kt=kt, nh=nh, wsl=wsl):
                    last = None
                    for sl4 in range(4):
                        last = E.matmul(psY[sl4][:, 0:512],
                                        actb[:, kt * 512 + sl4 * 128:
                                             kt * 512 + (sl4 + 1) * 128],
                                        w2r[:, wsl * 512:(wsl + 1) * 512],
                                        start=(kt == 0), stop=(kt == 23))
                    return last
                s.op("pe", mmy, reads=[f"w2r{wsl}"] + [f"actb:{kt}"],
                     writes=["psY0", "psY1", "psY2", "psY3"])
            for sl4 in range(4):
                s.op("dve", lambda E, sl4=sl4, nh=nh: E.tensor_tensor(
                    ytmp[:, (sl4 % 2) * 512:(sl4 % 2) * 512 + 512],
                    psY[sl4][:, 0:512], be2bc[:, nh * 512:(nh + 1) * 512],
                    ALU.add),
                    reads=[f"psY{sl4}", "be2bc"], writes=[f"ytmp{sl4 % 2}"])
                s.dma("sp", lambda E, ck=ck, sl4=sl4, nh=nh: E.dma_start(
                    ye_d[ck * 512 + sl4 * 128:ck * 512 + (sl4 + 1) * 128,
                         nh * 512:(nh + 1) * 512],
                    ytmp[:, (sl4 % 2) * 512:(sl4 % 2) * 512 + 512]),
                    reads=[f"ytmp{sl4 % 2}"], writes=["yed"])

    for ti in range(NTOK // 128):
        b = ti // (T // 128)
        tt = ti % (T // 128)
        pb = ti % 2
        xm = xmt[:, :] if pb == 0 else rx[:, :]
        ga = g1[:, :] if pb == 0 else ry[:, :]
        gb = g2[:, :] if pb == 0 else ytmp[:, 0:D]
        kx, k1, k2_ = f"xmt{pb}", f"g1{pb}", f"g2{pb}"
        wx = [kx] + (["res_rx"] if pb else [])
        w1 = [k1] + ([f"res_ry:{j}" for j in range(PT)] if pb else [])
        w2 = [k2_] + (["ytmp0", "ytmp1"] if pb else [])
        s.dma(("sp", "act")[pb], lambda E, b=b, tt=tt, xm=xm: E.dma_start(
            xm, xmid_d[b, ts(tt), :]), reads=["xmidd"], writes=wx)
        s.dma("pool", lambda E, ti=ti, ga=ga: E.indirect_dma_start(
            ga, None, ye_d[:, :],
            bass.IndirectOffsetOnAxis(slotsU[:, 2 * ti:2 * ti + 1], 0)),
            reads=["slotsU", "yed"], writes=w1)
        s.dma("pool", lambda E, ti=ti, gb=gb: E.indirect_dma_start(
            gb, None, ye_d[:, :],
            bass.IndirectOffsetOnAxis(slotsU[:, 2 * ti + 1:2 * ti + 2], 0)),
            reads=["slotsU", "yed"], writes=w2)
        s.op("dve", lambda E, ti=ti, xm=xm, ga=ga: E.scalar_tensor_tensor(
            xm, ga, w12[:, 2 * ti:2 * ti + 1], xm,
            ALU.mult, ALU.add),
            reads=[k1, "w12", kx], writes=[kx])
        s.op("dve", lambda E, ti=ti, xm=xm, gb=gb: E.scalar_tensor_tensor(
            xm, gb, w12[:, 2 * ti + 1:2 * ti + 2], xm,
            ALU.mult, ALU.add),
            reads=[k2_, "w12", kx], writes=[kx])
        s.dma(("sp", "act")[pb], lambda E, b=b, tt=tt, xm=xm: E.dma_start(
            xout[b, ts(tt), :], xm), reads=[kx])
    if fused:
        s.barrier()
    _sc.__exit__(None, None, None)
    return p



def build_fused():
    p = Prog("fused")
    load_consts(p, ["ident_bf", "ident_f32", "ones_col_bf", "cmask4_bf"])
    p.constv(1.0)
    p.constv(1e-5)
    xin = p.dram_in("xin", shape=[B_CORE, T, D], dtype=F32)
    xout = p.dram_out("xout", [B_CORE, T, D], F32)
    xa = p.nc.dram_tensor("xa_i", [B_CORE, T, D], F32, kind="Internal").ap()
    xb = p.nc.dram_tensor("xb_i", [B_CORE, T, D], F32, kind="Internal").ap()
    Mint = p.nc.dram_tensor("M_i", [D, D], F32, kind="Internal").ap()
    We1b = p.nc.dram_tensor("We1b_i", [12 * 12 * 128, 4096], BF16,
                            kind="Internal").ap()
    We2b = p.nc.dram_tensor("We2b_i", [12 * 128 * 24, D], BF16,
                            kind="Internal").ap()
    Ws1b = p.nc.dram_tensor("Ws1b_i", [D, 4096], BF16, kind="Internal").ap()
    Ws2b = p.nc.dram_tensor("Ws2b_i", [2048, D], BF16, kind="Internal").ap()
    W = {}
    for name, shape in [
            ("w_qkv", [D, 3 * D]), ("w_proj", [D, D]),
            ("wfr", [D, D]), ("wfi", [D, D]), ("ffn_bias", [1, D]),
            ("WinT", [D, 2336]), ("WhT", [1056, 40]), ("WoT", [D, D]),
            ("convw", [1280, 4]), ("convb", [1, 1280]), ("dtb", [1, 1]),
            ("gam", [1, 16]), ("alog", [16, 1]), ("dd", [16, 1]), ("nw", [1, D]),
            ("Wq", [D, D]), ("Wk", [D, D]), ("Wv", [D, D]), ("Wo", [D, D]),
            ("Ws1", [D, 4096]), ("bs1", [1, 4096]), ("Ws2", [2048, D]),
            ("bs2", [1, D]), ("We1g", [12, 12, 128, 4096]), ("be1", [12, 6144]),
            ("We2g", [12 * 128 * 24, D]), ("be2", [12, D]), ("Wr", [D, 12]),
            ("br", [1, 12]),
            ("WqkvT", [D, 3 * D]), ("bqkv", [1, 3 * D]), ("WqaT", [D, D]),
            ("bqa", [1, D]), ("WkaT", [D, D]), ("bka", [1, D]),
            ("WdT", [D, D]), ("bd", [1, D])]:
        W[name] = p.dram_in(name, shape=shape, dtype=F32)
    W["Min"] = Mint
    W["We1b"], W["We2b"] = We1b, We2b
    W["Ws1b"], W["Ws2b"] = Ws1b, Ws2b
    seq = [
        ("mm_", lambda io: build_mmat(p=p, W=W, mout=Mint), None),
        ("s0_", lambda io: build_attn_causal(0, p=p, io=io, W=W, conv=True),
         (xin, xa)),
        ("s1_", lambda io: build_ffn0(p=p, io=io, W=W), (xa, xb)),
        ("s2_", lambda io: build_hades(p=p, io=io, W=W), (xb, xa)),
        ("s3_", lambda io: build_ffn0(p=p, io=io, W=W), (xa, xb)),
        ("s4_", lambda io: build_attn_causal(1, p=p, io=io, W=W), (xb, xa)),
        ("s5_", lambda io: build_moe_sparse(p=p, io=io, W=W), (xa, xb)),
        ("s6_", lambda io: build_attn_b1(p=p, io=io, W=W), (xb, xa)),
        ("s7_", lambda io: build_moe_sparse(p=p, io=io, W=W), (xa, xout)),
    ]
    for tag, fn, io in seq:
        p.tag = tag
        fn(io)
        p.tag = ""
    return p


def kernel(**d):
    x = np.ascontiguousarray(np.asarray(d["L0_attn_a_x"], dtype=np.float32))
    nb = x.shape[0]
    ncores = nb // B_CORE
    p = _prog("fused", build_fused)
    w = {
        "w_qkv": d["L0_attn_a_w_qkv"], "w_proj": d["L0_attn_a_w_proj"],
        "wfr": d["L0_ffn_w_freq_real"], "wfi": d["L0_ffn_w_freq_imag"],
        "ffn_bias": np.asarray(d["L0_ffn_bias"]).reshape(1, -1),
        "WinT": _tr(d["L0_attn_b_in_proj_w"]),
        "WhT": _tr(d["L0_attn_b_h_proj_w"]),
        "WoT": _tr(d["L0_attn_b_out_proj_w"]),
        "convw": d["L0_attn_b_conv_w"],
        "convb": np.asarray(d["L0_attn_b_conv_b"]).reshape(1, -1),
        "dtb": np.asarray(d["L0_attn_b_dt_bias"]).reshape(1, 1),
        "gam": np.asarray(d["L0_attn_b_gamma"]).reshape(1, -1),
        "alog": np.asarray(d["L0_attn_b_A_log"]).reshape(-1, 1),
        "dd": np.asarray(d["L0_attn_b_D"]).reshape(-1, 1),
        "nw": np.asarray(d["L0_attn_b_norm_w"]).reshape(1, -1),
        "Wq": d["L1_attn_a_Wq"], "Wk": d["L1_attn_a_Wk"],
        "Wv": d["L1_attn_a_Wv"], "Wo": d["L1_attn_a_Wo"],
        "Ws1": d["L1_ffn_Ws1"], "bs1": np.asarray(d["L1_ffn_bs1"]).reshape(1, -1),
        "Ws2": d["L1_ffn_Ws2"], "bs2": np.asarray(d["L1_ffn_bs2"]).reshape(1, -1),
        "We1g": np.ascontiguousarray(
            np.asarray(d["L1_ffn_We1"]).reshape(12, 8, 128, 2, 6, 512)
            .transpose(0, 3, 4, 2, 1, 5).reshape(12, 12, 128, 4096)),
        "be1": d["L1_ffn_be1"],
        "We2g": np.ascontiguousarray(
            np.asarray(d["L1_ffn_We2"]).reshape(12, 24, 128, 1024)
            .transpose(0, 2, 1, 3).reshape(12 * 128 * 24, 1024)),
        "be2": d["L1_ffn_be2"],
        "Wr": d["L1_ffn_Wr"], "br": np.asarray(d["L1_ffn_br"]).reshape(1, -1),
        "WqkvT": _tr(d["L1_attn_b_Wqkv"]),
        "bqkv": np.asarray(d["L1_attn_b_bqkv"]).reshape(1, -1),
        "WqaT": _tr(d["L1_attn_b_Wqa"]),
        "bqa": np.asarray(d["L1_attn_b_bqa"]).reshape(1, -1),
        "WkaT": _tr(d["L1_attn_b_Wka"]),
        "bka": np.asarray(d["L1_attn_b_bka"]).reshape(1, -1),
        "WdT": _tr(d["L1_attn_b_Wd"]),
        "bd": np.asarray(d["L1_attn_b_bd"]).reshape(1, -1),
    }
    per_core = [dict(w, xin=x[i * B_CORE:(i + 1) * B_CORE]) for i in range(ncores)]
    res = p.run(per_core, ["xout"])
    return np.concatenate([r["xout"] for r in res], axis=0).astype(np.float32)
